# version 07
# baseline (speedup 1.0000x reference)
import numpy as np
from contextlib import ExitStack
import concourse.bass as bass
import concourse.mybir as mybir
from concourse.bass_utils import run_bass_kernel_spmd

F32 = mybir.dt.float32
BF16 = mybir.dt.bfloat16
AF = mybir.ActivationFunctionType
ALU = mybir.AluOpType

B, T, D, NST, RK, INNER = 2, 8192, 1024, 16, 64, 4096
DSL = 256
Q = 512
NCH = T // Q
TQ = T // 4
FBL = 256
NFB = TQ // FBL
NTT = FBL // 128
MAGIC = float(np.float32(1.5 * 2 ** 23))
EPS = 1e-5
NW = float(INNER * D)
GRPS = [[0, 1, 2, 3], [4, 5, 6, 7]]


class Ledger:

    def __init__(self, sems, engines):
        self.sems = sems
        self.engines = engines
        self.cnt = {e: 0 for e in sems}
        self.mark = {e: {} for e in engines}
        self.buf = {}
        self.queue = {e: [] for e in engines}

    def emit(self, e, fn, reads=(), writes=(), dma=False, chan=None):
        sem_name = chan if (dma and chan) else e
        assert sem_name in self.sems, sem_name
        need = {}
        for k in reads:
            b = self.buf.get(k)
            if b and b["w"] is not None:
                se, sv = b["w"]
                need[se] = max(need.get(se, -1), sv)
        for k in writes:
            b = self.buf.get(k)
            if b:
                if b["w"] is not None:
                    se, sv = b["w"]
                    need[se] = max(need.get(se, -1), sv)
                for se, sv in b["r"].items():
                    need[se] = max(need.get(se, -1), sv)
        for se in sorted(need):
            sv = need[se]
            if se == e == "t":
                continue
            if se not in self.engines:
                sv = self.cnt.get(se, sv)
            if self.mark[e].get(se, -1) < sv:
                self.queue[e].append(("wait", se, sv))
                self.mark[e][se] = sv
        step = 16 if dma else 1
        self.cnt[sem_name] = self.cnt.get(sem_name, 0) + step
        v = self.cnt[sem_name]
        self.queue[e].append(("op", fn, step, sem_name))
        for k in reads:
            b = self.buf.setdefault(k, {"w": None, "r": {}})
            b["r"][sem_name] = max(b["r"].get(sem_name, -1), v)
        for k in writes:
            self.buf[k] = {"w": (sem_name, v), "r": {}}

    def barrier(self):
        tops = dict(self.cnt)
        for e in self.queue:
            for se, sv in tops.items():
                if se != e and sv > 0 and self.mark[e].get(se, -1) < sv:
                    self.queue[e].append(("wait", se, sv))
                    self.mark[e][se] = sv


def build_program():
    nc = bass.Bass("TRN2", target_bir_lowering=False)

    u_full = nc.dram_tensor("u_full", [T, D], F32, kind="ExternalInput").ap()
    u_tok = nc.dram_tensor("u_tok", [TQ, D], F32, kind="ExternalInput").ap()
    wxdtT_d = nc.dram_tensor("wxdtT", [D, RK], F32, kind="ExternalInput").ap()
    bxdt_d = nc.dram_tensor("bxdt", [RK, 1], F32, kind="ExternalInput").ap()
    wdtT_d = nc.dram_tensor("wdtT_sl", [RK, DSL], F32, kind="ExternalInput").ap()
    bdt_d = nc.dram_tensor("bdt_sl", [DSL, 1], F32, kind="ExternalInput").ap()
    wbT_d = nc.dram_tensor("wbT", [D, NST], F32, kind="ExternalInput").ap()
    bb_d = nc.dram_tensor("bb", [NST, 1], F32, kind="ExternalInput").ap()
    wcT_d = nc.dram_tensor("wcT", [D, NST], F32, kind="ExternalInput").ap()
    bc_d = nc.dram_tensor("bc", [NST, 1], F32, kind="ExternalInput").ap()
    alog_d = nc.dram_tensor("alog_sl", [DSL, NST], F32, kind="ExternalInput").ap()
    dvec_d = nc.dram_tensor("dvec_sl", [DSL, 1], F32, kind="ExternalInput").ap()
    eye_d = nc.dram_tensor("eye", [128, 128], F32, kind="ExternalInput").ap()
    wgT_d = nc.dram_tensor("wgT_q", [D, 1024], F32, kind="ExternalInput").ap()
    wuT_d = nc.dram_tensor("wuT_q", [D, 1024], F32, kind="ExternalInput").ap()
    wdT_d = nc.dram_tensor("wdT_q", [1024, D], F32, kind="ExternalInput").ap()
    out_d = nc.dram_tensor("out", [TQ, D], F32, kind="ExternalOutput").ap()
    bc_B = nc.dram_tensor("bc_B", [NST, T], F32).ap()
    bc_C = nc.dram_tensor("bc_C", [NST, T], F32).ap()
    scal_dr = nc.dram_tensor("scal_dr", [1, 8], F32).ap()
    ag_in = [nc.dram_tensor(f"agi{c}", [Q, DSL], F32) for c in range(NCH)]
    ag_out = [nc.dram_tensor(f"ago{c}", [4, Q, DSL], F32) for c in range(NCH)]
    cc_w_in = nc.dram_tensor("ccwi", [3, 1024, 1024], BF16)
    cc_w_out = nc.dram_tensor("ccwo", [4, 3, 1024, 1024], BF16)
    cc_s_in = nc.dram_tensor("ccsi", [1, 4], F32)
    cc_s_out = nc.dram_tensor("ccso", [1, 4], F32)

    _st = ExitStack()
    with _st:
        eye_sb = _st.enter_context(nc.sbuf_tensor("eye_sb", [128, 128], F32))
        eyeb_sb = _st.enter_context(nc.sbuf_tensor("eyeb_sb", [128, 128], BF16))
        wxdtT_sb = _st.enter_context(nc.sbuf_tensor("wxdtT_sb", [128, 8 * RK], BF16))
        wbT_sb = _st.enter_context(nc.sbuf_tensor("wbT_sb", [128, 8 * NST], BF16))
        wcT_sb = _st.enter_context(nc.sbuf_tensor("wcT_sb", [128, 8 * NST], BF16))
        wdtT_sb = _st.enter_context(nc.sbuf_tensor("wdtT_sb", [RK, DSL], BF16))
        bxdt_sb = _st.enter_context(nc.sbuf_tensor("bxdt_sb", [RK, 1], F32))
        bdt_sb = _st.enter_context(nc.sbuf_tensor("bdt_sb", [128, 2], F32))
        dcol_sb = _st.enter_context(nc.sbuf_tensor("dcol_sb", [128, 2], F32))
        bb_sb = _st.enter_context(nc.sbuf_tensor("bb_sb", [NST, 1], F32))
        bc_sb = _st.enter_context(nc.sbuf_tensor("bc_sb", [NST, 1], F32))
        acol_sb = _st.enter_context(nc.sbuf_tensor("acol_sb", [128, 2 * NST], F32))
        hstate = _st.enter_context(nc.sbuf_tensor("hstate", [128, 2 * NST], F32))
        wsum_sb = _st.enter_context(nc.sbuf_tensor("wsum_sb", [128, 4], F32))
        wrow_sb = _st.enter_context(nc.sbuf_tensor("wrow_sb", [1, 128], F32))
        wsc_sb = _st.enter_context(nc.sbuf_tensor("wsc_sb", [1, 8], F32))
        wscb_sb = _st.enter_context(nc.sbuf_tensor("wscb_sb", [128, 8], F32))
        u_ch = _st.enter_context(nc.sbuf_tensor("u_ch", [128, 4 * D], F32))
        ygq_sb = _st.enter_context(nc.sbuf_tensor("ygq_sb", [128, 2 * D], F32))
        ufq_sb = _st.enter_context(nc.sbuf_tensor("ufq_sb", [128, 2 * D], F32))
        fscr_sb = _st.enter_context(nc.sbuf_tensor("fscr_sb", [128, D], F32))
        sq_ch = _st.enter_context(nc.sbuf_tensor("sq_ch", [128, D], F32))
        st_ch = _st.enter_context(nc.sbuf_tensor("st_ch", [128, 12], F32))
        xnT_sb = _st.enter_context(nc.sbuf_tensor("xnT_sb", [128, 8 * Q], BF16))
        xnsl_sb = _st.enter_context(nc.sbuf_tensor("xnsl_sb", [128, 2 * Q], F32))
        xdt_sb = _st.enter_context(nc.sbuf_tensor("xdt_sb", [RK, Q], BF16))
        dt_sb = _st.enter_context(nc.sbuf_tensor("dt_sb", [128, 2 * Q], BF16))
        bcq_sb = _st.enter_context(nc.sbuf_tensor("bcq_sb", [NST, Q], F32))
        c_sb = _st.enter_context(nc.sbuf_tensor("c_sb", [128, 2 * Q], F32))
        bbc_sb = _st.enter_context(nc.sbuf_tensor("bbc_sb", [128, 2 * Q], F32))
        cbc_sb = _st.enter_context(nc.sbuf_tensor("cbc_sb", [128, 2 * Q], F32))
        abar_sb = _st.enter_context(nc.sbuf_tensor("abar_sb", [128, 2 * Q], F32))
        in1_sb = _st.enter_context(nc.sbuf_tensor("in1_sb", [128, 2 * Q], F32))
        h_sb = _st.enter_context(nc.sbuf_tensor("h_sb", [128, 2 * Q], F32))
        hc_sb = _st.enter_context(nc.sbuf_tensor("hc_sb", [128, 2 * Q], F32))
        ych_sb = _st.enter_context(nc.sbuf_tensor("ych_sb", [128, 4 * Q], F32))
        ytok_sb = _st.enter_context(nc.sbuf_tensor("ytok_sb", [128, 2 * DSL], F32))
        x2_sb = _st.enter_context(nc.sbuf_tensor("x2_sb", [128, NTT * D], F32))
        xn2_sb = _st.enter_context(nc.sbuf_tensor("xn2_sb", [128, 2048], F32))
        fst_sb = _st.enter_context(nc.sbuf_tensor("fst_sb", [128, 16], F32))
        xq_sb = _st.enter_context(nc.sbuf_tensor("xq_sb", [128, NTT * D], BF16))
        xqT_sb = _st.enter_context(nc.sbuf_tensor("xqT_sb", [128, 8 * FBL], BF16))
        wA_sb = _st.enter_context(nc.sbuf_tensor("wA_sb", [128, 2 * 8 * 512], BF16))
        wB_sb = _st.enter_context(nc.sbuf_tensor("wB_sb", [128, 2 * 8 * 512], BF16))
        gate_sb = _st.enter_context(nc.sbuf_tensor("gate_sb", [128, 512], F32))
        arena_a = _st.enter_context(nc.sbuf_tensor("arena_a", [128, 8192], F32))
        arena_b = _st.enter_context(nc.sbuf_tensor("arena_b", [128, 4096], F32))
        ot_sb = _st.enter_context(nc.sbuf_tensor("ot_sb", [128, 512], F32))
        g_sb = arena_a.ap()
        wq_ld = arena_a.ap()
        gq_sb = arena_a.ap().bitcast(BF16)[:, 0:NTT * INNER]
        gqT_sb = arena_b.ap().bitcast(BF16)[:, 0:NTT * INNER]
        wq_r = arena_a.ap()
        wq_b = xn2_sb.ap().bitcast(BF16)
        yg_sb = ygq_sb.ap()
        uf_sb = ufq_sb.ap()
        edt_sb = sq_ch.ap()[:, 0:Q]
        xnb_sb = ytok_sb.ap().bitcast(BF16)

        fsq_sb = fscr_sb.ap()
        fdu_sb = fscr_sb.ap()
        ps0 = _st.enter_context(nc.psum_tensor("ps0", [128, 512], F32))
        ps1 = _st.enter_context(nc.psum_tensor("ps1", [128, 512], F32))
        ps2 = _st.enter_context(nc.psum_tensor("ps2", [128, 512], F32))
        ps3 = _st.enter_context(nc.psum_tensor("ps3", [128, 512], F32))
        ps4 = _st.enter_context(nc.psum_tensor("ps4", [128, 512], F32))
        ps5 = _st.enter_context(nc.psum_tensor("ps5", [128, 512], F32))
        ps6 = _st.enter_context(nc.psum_tensor("ps6", [128, 512], F32))
        ps7 = _st.enter_context(nc.psum_tensor("ps7", [128, 512], F32))
        sem_v = _st.enter_context(nc.semaphore("sem_v"))
        sem_s = _st.enter_context(nc.semaphore("sem_s"))
        sem_t = _st.enter_context(nc.semaphore("sem_t"))
        sem_g = _st.enter_context(nc.semaphore("sem_g"))
        sem_q = _st.enter_context(nc.semaphore("sem_q"))
        block = _st.enter_context(nc.Block())
        sems = {"v": sem_v, "s": sem_s, "t": sem_t, "g": sem_g, "q": sem_q}
        CHANS = ["qm", "qwl", "qwr", "qss", "qsc", "qsw", "qu0", "qu1",
                 "qsB", "qsC", "qb0", "qb1", "qc0", "qc1",
                 "qsy0", "qsy1", "qsy2", "qsy3", "qsy4", "qsy5", "qsy6", "qsy7", "qsy8", "qsy9", "qsy10", "qsy11", "qsy12", "qsy13", "qsy14", "qsy15", "qdbg", "qd", "qu2", "qu3", "qtr",
                 "qy0", "qy1", "qf0", "qf1", "qw0", "qw1", "qw20", "qw21", "qso"]
        for cn in CHANS:
            sems[cn] = _st.enter_context(nc.semaphore("sem_" + cn))
        L = Ledger(sems, ["v", "s", "t", "g", "q"])
        E = L.emit
        PID = [None]

        E("q", lambda q: q.dma_start(out=eye_sb[:, :], in_=eye_d[:, :]),
          writes=["eye"], dma=True, chan="qm")
        E("v", lambda v: v.tensor_copy(eyeb_sb[:, :], eye_sb[:, :]),
          reads=["eye"], writes=["eyeb"])
        for nm, dst, src, nk, w in [
            ("wxdtT", wxdtT_sb, wxdtT_d, 8, RK),
            ("wbT", wbT_sb, wbT_d, 8, NST),
            ("wcT", wcT_sb, wcT_d, 8, NST),
        ]:
            E("q", lambda q, src=src, nk=nk, w=w: q.dma_start(
                out=wq_ld[:, 0:nk * w].rearrange("p (k r) -> p k r", k=nk),
                in_=src.rearrange("(k p) r -> p k r", p=128)),
              writes=["wq_ld"], dma=True, chan="qwl")
            E("v", lambda v, dst=dst, nk=nk, w=w: v.tensor_copy(
                dst[:, :], wq_ld[:, 0:nk * w]),
              reads=["wq_ld"], writes=[nm])
        E("q", lambda q: q.dma_start(out=wq_ld[0:RK, 0:DSL], in_=wdtT_d[:, :]),
          writes=["wq_ld"], dma=True, chan="qwl")
        E("v", lambda v: v.tensor_copy(wdtT_sb[:, :], wq_ld[0:RK, 0:DSL]),
          reads=["wq_ld"], writes=["wdtT"])
        E("q", lambda q: q.dma_start(out=bxdt_sb[:, :], in_=bxdt_d[:, :]),
          writes=["bxdt"], dma=True, chan="qm")
        E("q", lambda q: q.dma_start(
            out=bdt_sb[:, :].rearrange("p (d c) -> p d c", d=2),
            in_=bdt_d.rearrange("(d p) c -> p d c", p=128),
            allow_slow_non_contiguous=True),
          writes=["bdt"], dma=True, chan="qm")
        E("q", lambda q: q.dma_start(
            out=dcol_sb[:, :].rearrange("p (d c) -> p d c", d=2),
            in_=dvec_d.rearrange("(d p) c -> p d c", p=128),
            allow_slow_non_contiguous=True),
          writes=["dcol"], dma=True, chan="qm")
        E("q", lambda q: q.dma_start(out=bb_sb[:, :], in_=bb_d[:, :]),
          writes=["bbias"], dma=True, chan="qm")
        E("q", lambda q: q.dma_start(out=bc_sb[:, :], in_=bc_d[:, :]),
          writes=["cbias"], dma=True, chan="qm")
        E("q", lambda q: q.dma_start(
            out=acol_sb[:, :].rearrange("p (d n) -> p d n", d=2),
            in_=alog_d.rearrange("(d p) n -> p d n", p=128)),
          writes=["acol"], dma=True, chan="qm")
        E("s", lambda s: s.activation(acol_sb[:, :], acol_sb[:, :], AF.Exp),
          reads=["acol"], writes=["acol"])
        E("v", lambda v: v.tensor_scalar_mul(acol_sb[:, :], acol_sb[:, :], -1.0),
          reads=["acol"], writes=["acol"])
        E("v", lambda v: v.memset(hstate[:, :], 0.0),
          writes=[f"hst{i}" for i in range(2 * NST)])

        wsrc = [wgT_d, wuT_d, wdT_d]
        for m in range(3):
            E("q", lambda q, m=m: q.dma_start(
                out=wq_ld[:, :].rearrange("p (k c) -> p k c", k=8),
                in_=wsrc[m].rearrange("(k p) c -> p k c", p=128)),
              writes=["wq_ld"], dma=True, chan="qwl")
            E("s", lambda s, m=m: s.activation(
                wq_r[:, :], wq_ld[:, :], AF.Abs,
                accum_out=wsum_sb[:, m:m + 1]),
              reads=["wq_ld"], writes=["wq_r", f"wsum{m}"])
        for m in range(3):
            E("q", lambda q, m=m: q.dma_start(
                out=wrow_sb[:, :], in_=wsum_sb[:, m:m + 1],
                allow_slow_non_contiguous=True),
              reads=[f"wsum{m}"], writes=["wrow"], dma=True, chan="qwr")
            E("v", lambda v, m=m: v.reduce_sum(
                wsc_sb[0:1, m:m + 1], wrow_sb[:, :], axis=mybir.AxisListType.X),
              reads=["wrow"], writes=["wsc"])
        E("q", lambda q: q.dma_start(out=cc_s_in[0:1, :], in_=wsc_sb[0:1, 0:4]),
          reads=["wsc"], writes=["ccsi"], dma=True, chan="qss")
        E("g", lambda g: g.collective_compute(
            "AllReduce", ALU.add, replica_groups=GRPS,
            ins=[cc_s_in.ap().opt()], outs=[cc_s_out.ap().opt()]),
          reads=["ccsi"], writes=["ccso"])
        E("q", lambda q: q.dma_start(out=wsc_sb[0:1, 0:4], in_=cc_s_out[0:1, :]),
          reads=["ccso"], writes=["wsc"], dma=True, chan="qss")
        E("v", lambda v: v.tensor_scalar(
            wsc_sb[0:1, 4:8], wsc_sb[0:1, 0:4], 1.0 / NW, 1e-5,
            op0=ALU.mult, op1=ALU.max),
          reads=["wsc"], writes=["wsc"])
        E("v", lambda v: v.reciprocal(wsc_sb[0:1, 0:4], wsc_sb[0:1, 4:8]),
          reads=["wsc"], writes=["wsc"])
        E("q", lambda q: q.dma_start(out=scal_dr[0:1, :], in_=wsc_sb[0:1, :]),
          reads=["wsc"], writes=["scal"], dma=True, chan="qsc")
        E("q", lambda q: q.dma_start(out=wscb_sb[:, :],
                                     in_=scal_dr.broadcast_to([128, 8])),
          reads=["scal"], writes=["wscb"], dma=True, chan="qsc")
        for m in range(3):
            E("q", lambda q, m=m: q.dma_start(
                out=wq_ld[:, :].rearrange("p (k c) -> p k c", k=8),
                in_=wsrc[m].rearrange("(k p) c -> p k c", p=128)),
              writes=["wq_ld"], dma=True, chan="qwl")
            E("v", lambda v, m=m: v.tensor_scalar(
                wq_ld[:, :], wq_ld[:, :], wscb_sb[:, m:m + 1], MAGIC,
                op0=ALU.mult, op1=ALU.add),
              reads=["wq_ld", "wscb"], writes=["wq_ld"])
            E("v", lambda v: v.tensor_scalar(
                wq_ld[:, :], wq_ld[:, :], MAGIC, None, op0=ALU.subtract),
              reads=["wq_ld"], writes=["wq_ld"])
            for hf in range(2):
                E("v", lambda v, hf=hf: v.tensor_scalar(
                    wq_b[:, 0:4096], wq_ld[:, hf * 4096:(hf + 1) * 4096],
                    1.0, -1.0, op0=ALU.min, op1=ALU.max),
                  reads=["wq_ld"], writes=["wq_b"])
                E("q", lambda q, m=m, hf=hf: q.dma_start(
                    out=cc_w_in[m, :, :].rearrange("(k p) c -> p k c", p=128)
                    [:, hf * 4:(hf + 1) * 4, :],
                    in_=wq_b[:, 0:4096].rearrange("p (k c) -> p k c", k=4)),
                  reads=["wq_b"], writes=["ccwi"], dma=True, chan="qsw")
        E("g", lambda g: g.collective_compute(
            "AllGather", ALU.bypass, replica_groups=GRPS,
            ins=[cc_w_in.ap().opt()], outs=[cc_w_out.ap().opt()]),
          reads=["ccwi"], writes=["ccwo"])

        psT = [ps3, ps4]

        def ln_apply(src_key, src_tile, st, dst_key, dst_tile):
            E("v", lambda v: v.reduce_sum(
                st[:, 0:1], src_tile, axis=mybir.AxisListType.X),
              reads=[src_key], writes=["stm"])
            E("s", lambda s: s.activation(
                sq_ch[:, :], src_tile, AF.Square, accum_out=st[:, 2:3]),
              reads=[src_key], writes=["stsq", "sts2"])
            E("v", lambda v: v.tensor_scalar(
                st[:, 0:1], st[:, 0:1], 1.0 / D, None, op0=ALU.mult),
              reads=["stm"], writes=["stm"])
            E("v", lambda v: v.tensor_scalar(
                st[:, 2:3], st[:, 2:3], 1.0 / D, EPS,
                op0=ALU.mult, op1=ALU.add),
              reads=["sts2"], writes=["sts2"])
            E("v", lambda v: v.tensor_tensor(
                st[:, 3:4], st[:, 0:1], st[:, 0:1], op=ALU.mult),
              reads=["stm"], writes=["stm2"])
            E("v", lambda v: v.tensor_tensor(
                st[:, 1:2], st[:, 2:3], st[:, 3:4], op=ALU.subtract),
              reads=["sts2", "stm2"], writes=["str"])
            E("s", lambda s: s.activation(st[:, 1:2], st[:, 1:2], AF.Ln),
              reads=["str"], writes=["str"])
            E("s", lambda s: s.activation(st[:, 1:2], st[:, 1:2], AF.Exp,
                                          scale=-0.5),
              reads=["str"], writes=["str"])
            E("v", lambda v: v.tensor_scalar(
                dst_tile, src_tile, st[:, 0:1], st[:, 1:2],
                op0=ALU.subtract, op1=ALU.mult),
              reads=[src_key, "stm", "str"], writes=[dst_key])

        def ln2_apply(src_key, src_tile, st, dst_key, dst_tile, scr):
            E("v", lambda v: v.reduce_sum(
                st[:, 0:1], src_tile, axis=mybir.AxisListType.X),
              reads=[src_key], writes=["fm"])
            E("s", lambda s: s.activation(
                scr[:, :], src_tile, AF.Square, accum_out=st[:, 2:3]),
              reads=[src_key], writes=["fdu", "fs2"])
            E("v", lambda v: v.tensor_scalar(
                st[:, 0:1], st[:, 0:1], 1.0 / D, None, op0=ALU.mult),
              reads=["fm"], writes=["fm"])
            E("v", lambda v: v.tensor_scalar(
                st[:, 2:3], st[:, 2:3], 1.0 / D, EPS, op0=ALU.mult, op1=ALU.add),
              reads=["fs2"], writes=["fs2"])
            E("v", lambda v: v.tensor_tensor(
                st[:, 3:4], st[:, 0:1], st[:, 0:1], op=ALU.mult),
              reads=["fm"], writes=["fm2"])
            E("v", lambda v: v.tensor_tensor(
                st[:, 1:2], st[:, 2:3], st[:, 3:4], op=ALU.subtract),
              reads=["fs2", "fm2"], writes=["fr"])
            E("s", lambda s: s.activation(st[:, 1:2], st[:, 1:2], AF.Ln),
              reads=["fr"], writes=["fr"])
            E("s", lambda s: s.activation(st[:, 1:2], st[:, 1:2], AF.Exp,
                                          scale=-0.5),
              reads=["fr"], writes=["fr"])
            E("v", lambda v: v.tensor_scalar(
                dst_tile, src_tile, st[:, 0:1], st[:, 1:2],
                op0=ALU.subtract, op1=ALU.mult),
              reads=[src_key, "fm", "fr"], writes=[dst_key])

        psG = [ps1, ps5]
        psU = [ps2, ps6]
        psD_ = [ps7, ps5]

        def ffn_block_emit(bl):
                fb0 = bl * FBL
                for tt in range(NTT):
                    r0 = fb0 + tt * 128
                    ych_ = 2 * bl + tt
                    E("q", lambda q, tt=tt, ych_=ych_: q.dma_start(
                        out=yg_sb[:, tt * D:(tt + 1) * D]
                        .rearrange("p (j c) -> p j c", j=4),
                        in_=ag_out[ych_].ap()[:, bass.ts(PID[0], 128), :]
                        .transpose([1, 0, 2])),
                      reads=[f"ag{ych_}"], writes=[f"yg{tt}"], dma=True,
                      chan=f"qy{tt}")
                    yield 1
                    E("q", lambda q, r0=r0, tt=tt: q.dma_start(
                        out=uf_sb[:, tt * D:(tt + 1) * D],
                        in_=u_tok[r0:r0 + 128, :]),
                      writes=[f"uf{tt}"], dma=True, chan=f"qf{tt}")
                    yield 1
                    uft = uf_sb[:, tt * D:(tt + 1) * D]
                    E("v", lambda v, tt=tt, uft=uft: v.tensor_tensor(
                        x2_sb[:, tt * D:(tt + 1) * D], uft,
                        yg_sb[:, tt * D:(tt + 1) * D], op=ALU.add),
                      reads=[f"uf{tt}", f"yg{tt}"], writes=[f"x2{tt}"])
                    yield 1
                    ln2_apply(f"x2{tt}", x2_sb[:, tt * D:(tt + 1) * D], fst_sb,
                              "fsq", fsq_sb[:, :], fdu_sb)
                    E("v", lambda v: v.reduce_max(
                        fst_sb[:, 4:5], fsq_sb[:, :], axis=mybir.AxisListType.X,
                        apply_absolute_value=True),
                      reads=["fsq"], writes=["fmax"])
                    yield 1
                    E("v", lambda v: v.tensor_scalar_max(
                        fst_sb[:, 4:5], fst_sb[:, 4:5], 1e-5),
                      reads=["fmax"], writes=["fmax"])
                    yield 1
                    E("v", lambda v: v.reciprocal(fst_sb[:, 5:6], fst_sb[:, 4:5]),
                      reads=["fmax"], writes=["fsx"])
                    yield 1
                    E("v", lambda v: v.tensor_scalar(
                        fst_sb[:, 5:6], fst_sb[:, 5:6], 127.0, None, op0=ALU.mult),
                      reads=["fsx"], writes=["fsx"])
                    yield 1
                    E("v", lambda v, tt=tt: v.tensor_scalar(
                        fst_sb[:, 6 + tt:7 + tt], fst_sb[:, 4:5], 1.0 / 127.0, None,
                        op0=ALU.mult),
                      reads=["fmax"], writes=[f"fisx{tt}"])
                    yield 1
                    E("v", lambda v: v.tensor_scalar(
                        fdu_sb[:, :], fsq_sb[:, :], fst_sb[:, 5:6], MAGIC,
                        op0=ALU.mult, op1=ALU.add),
                      reads=["fsq", "fsx"], writes=["fdu"])
                    yield 1
                    E("v", lambda v, tt=tt: v.tensor_scalar(
                        xq_sb[:, tt * D:(tt + 1) * D], fdu_sb[:, :], MAGIC, None,
                        op0=ALU.subtract),
                      reads=["fdu"], writes=[f"xq{tt}"])
                    yield 1
                    for k in range(8):
                        pst = psT[k % 2]
                        E("t", lambda t, k=k, tt=tt, pst=pst: t.transpose(
                            pst.bitcast(BF16)[:, 0:128],
                            xq_sb[:, tt * D + k * 128: tt * D + (k + 1) * 128],
                            eyeb_sb[:, :]),
                          reads=[f"xq{tt}", "eyeb"], writes=[f"psT{k % 2}"])
                        yield 1
                        E("s", lambda s, k=k, tt=tt, pst=pst: s.activation(
                            xqT_sb[:, k * FBL + tt * 128: k * FBL + (tt + 1) * 128],
                            pst.bitcast(BF16)[:, 0:128], AF.Copy),
                          reads=[f"psT{k % 2}"], writes=[f"xqT{k}"])
                        yield 1
                    E("v", lambda v, tt=tt: v.tensor_tensor(
                        fst_sb[:, 8 + tt:9 + tt], fst_sb[:, 6 + tt:7 + tt],
                        wscb_sb[:, 4:5], op=ALU.mult),
                      reads=[f"fisx{tt}", "wscb"], writes=[f"sgv{tt}"])
                    yield 1
                    E("v", lambda v, tt=tt: v.tensor_tensor(
                        fst_sb[:, 10 + tt:11 + tt], fst_sb[:, 6 + tt:7 + tt],
                        wscb_sb[:, 5:6], op=ALU.mult),
                      reads=[f"fisx{tt}", "wscb"], writes=[f"suv{tt}"])
                    yield 1
                for ic in range(8):
                    sl_ = ic % 2
                    W0 = sl_ * 8 * 512
                    E("q", lambda q, ic=ic, W0=W0: q.dma_start(
                        out=wA_sb[:, W0:W0 + 8 * 512].rearrange("p (k c) -> p k c", k=8),
                        in_=cc_w_out[ic // 2, 0, :, (ic % 2) * 512:(ic % 2 + 1) * 512]
                        .rearrange("(k p) c -> p k c", p=128)),
                      reads=["ccwo"], writes=[f"wA{sl_}"], dma=True, chan=f"qw{sl_}")
                    yield 1
                    E("q", lambda q, ic=ic, W0=W0: q.dma_start(
                        out=wB_sb[:, W0:W0 + 8 * 512].rearrange("p (k c) -> p k c", k=8),
                        in_=cc_w_out[ic // 2, 1, :, (ic % 2) * 512:(ic % 2 + 1) * 512]
                        .rearrange("(k p) c -> p k c", p=128)),
                      reads=["ccwo"], writes=[f"wB{sl_}"], dma=True, chan=f"qw2{sl_}")
                    yield 1
                    for tt in range(NTT):
                        for k in range(8):
                            E("t", lambda t, k=k, tt=tt, W0=W0: t.matmul(
                                psG[tt][:, :],
                                xqT_sb[:, k * FBL + tt * 128:
                                       k * FBL + (tt + 1) * 128],
                                wA_sb[:, W0 + k * 512:W0 + (k + 1) * 512],
                                start=(k == 0), stop=(k == 7)),
                              reads=[f"xqT{k}", f"wA{sl_}"], writes=["ps1" if tt == 0 else "ps5"])
                            yield 1
                        for k in range(8):
                            E("t", lambda t, k=k, tt=tt, W0=W0: t.matmul(
                                psU[tt][:, :],
                                xqT_sb[:, k * FBL + tt * 128:
                                       k * FBL + (tt + 1) * 128],
                                wB_sb[:, W0 + k * 512:W0 + (k + 1) * 512],
                                start=(k == 0), stop=(k == 7)),
                              reads=[f"xqT{k}", f"wB{sl_}"], writes=["ps2" if tt == 0 else "ps6"])
                            yield 1
                        E("s", lambda s, tt=tt: s.activation(
                            gate_sb[:, :], psG[tt][:, :], AF.Silu,
                            scale=fst_sb[:, 8 + tt:9 + tt]),
                          reads=[("ps1" if tt == 0 else "ps5"), f"sgv{tt}"], writes=["gate"])
                        yield 1
                        E("v", lambda v, tt=tt, ic=ic: v.scalar_tensor_tensor(
                            g_sb[:, tt * INNER + ic * 512:
                                 tt * INNER + (ic + 1) * 512],
                            psU[tt][:, :], fst_sb[:, 10 + tt:11 + tt],
                            gate_sb[:, :], op0=ALU.mult, op1=ALU.mult),
                          reads=[("ps2" if tt == 0 else "ps6"), f"suv{tt}", "gate"],
                          writes=[f"g{tt}_{ic}"])
                        yield 1
                for tt in range(NTT):
                    E("v", lambda v, tt=tt: v.reduce_max(
                        fst_sb[:, 12 + tt:13 + tt],
                        g_sb[:, tt * INNER:(tt + 1) * INNER],
                        axis=mybir.AxisListType.X, apply_absolute_value=True),
                      reads=[f"g{tt}_{ic}" for ic in range(8)], writes=[f"gmax{tt}"])
                    yield 1
                    E("v", lambda v, tt=tt: v.tensor_scalar_max(
                        fst_sb[:, 12 + tt:13 + tt], fst_sb[:, 12 + tt:13 + tt],
                        1e-5),
                      reads=[f"gmax{tt}"], writes=[f"gmax{tt}"])
                    yield 1
                    E("v", lambda v, tt=tt: v.reciprocal(
                        fst_sb[:, 14 + tt:15 + tt], fst_sb[:, 12 + tt:13 + tt]),
                      reads=[f"gmax{tt}"], writes=[f"gsx{tt}"])
                    yield 1
                    E("v", lambda v, tt=tt: v.tensor_scalar(
                        fst_sb[:, 14 + tt:15 + tt], fst_sb[:, 14 + tt:15 + tt],
                        127.0, None, op0=ALU.mult),
                      reads=[f"gsx{tt}"], writes=[f"gsx{tt}"])
                    yield 1
                    E("v", lambda v, tt=tt: v.tensor_scalar(
                        fst_sb[:, 12 + tt:13 + tt], fst_sb[:, 12 + tt:13 + tt],
                        1.0 / 127.0, None, op0=ALU.mult),
                      reads=[f"gmax{tt}"], writes=[f"gisx{tt}"])
                    yield 1
                    E("v", lambda v, tt=tt: v.tensor_tensor(
                        fst_sb[:, 12 + tt:13 + tt], fst_sb[:, 12 + tt:13 + tt],
                        wscb_sb[:, 6:7], op=ALU.mult),
                      reads=[f"gisx{tt}", "wscb"], writes=[f"sdv{tt}"])
                    yield 1
                    for hf in range(2):
                        E("v", lambda v, tt=tt, hf=hf: v.tensor_scalar(
                            xn2_sb[:, :],
                            g_sb[:, tt * INNER + hf * 2048:
                                 tt * INNER + (hf + 1) * 2048],
                            fst_sb[:, 14 + tt:15 + tt], MAGIC,
                            op0=ALU.mult, op1=ALU.add),
                          reads=[f"g{tt}_{ic}" for ic in range(8)] + [f"gsx{tt}"],
                          writes=["gqr"])
                        yield 1
                        E("v", lambda v, tt=tt, hf=hf: v.tensor_scalar(
                            gq_sb[:, tt * INNER + hf * 2048:
                                  tt * INNER + (hf + 1) * 2048],
                            xn2_sb[:, :], MAGIC, None, op0=ALU.subtract),
                          reads=["gqr"] + (["wq_b"] if bl == 0 else []),
                      writes=[f"gq{tt}", f"gqg{tt}{hf}"])
                        yield 1
                    for k in range(32):
                        pst = psT[k % 2]
                        E("t", lambda t, k=k, tt=tt, pst=pst: t.transpose(
                            pst.bitcast(BF16)[:, 0:128],
                            gq_sb[:, tt * INNER + k * 128:
                                  tt * INNER + (k + 1) * 128],
                            eyeb_sb[:, :]),
                          reads=[f"gq{tt}", f"g{tt}_0", "eyeb"],
                          writes=[f"psT{k % 2}"])
                        yield 1
                        E("s", lambda s, k=k, tt=tt, pst=pst: s.activation(
                            gqT_sb[:, k * FBL + tt * 128: k * FBL + (tt + 1) * 128],
                            pst.bitcast(BF16)[:, 0:128], AF.Copy),
                          reads=[f"psT{k % 2}"], writes=[f"gqT{k}_{tt}"])
                        yield 1
                yield "gq"
                for dh in range(2):
                    for kg in range(4):
                        sl_ = kg % 2
                        W0 = sl_ * 8 * 512
                        E("q", lambda q, dh=dh, kg=kg, W0=W0: q.dma_start(
                            out=wA_sb[:, W0:W0 + 8 * 512].rearrange("p (k c) -> p k c", k=8),
                            in_=cc_w_out[kg, 2, :, dh * 512:(dh + 1) * 512]
                            .rearrange("(k p) c -> p k c", p=128)),
                          reads=["ccwo"], writes=[f"wA{sl_}"], dma=True, chan=f"qw{sl_}")
                        yield 1
                        for tt in range(NTT):
                            for k8 in range(8):
                                k = kg * 8 + k8
                                E("t", lambda t, k=k, k8=k8, tt=tt, dh=dh, W0=W0:
                                  t.matmul(
                                      psD_[tt][:, :],
                                      gqT_sb[:, k * FBL + tt * 128:
                                             k * FBL + (tt + 1) * 128],
                                      wA_sb[:, W0 + k8 * 512:W0 + (k8 + 1) * 512],
                                      start=(k == 0), stop=(k == 31)),
                                  reads=[f"gqT{k}_{tt}", f"wA{sl_}"],
                                  writes=["ps7" if tt == 0 else "ps5"])
                                yield 1
                    for tt in range(NTT):
                        E("v", lambda v, tt=tt, dh=dh: v.scalar_tensor_tensor(
                            ot_sb[:, :],
                            psD_[tt][:, :], fst_sb[:, 12 + tt:13 + tt],
                            x2_sb[:, tt * D + dh * 512: tt * D + (dh + 1) * 512],
                            op0=ALU.mult, op1=ALU.add),
                          reads=[("ps7" if tt == 0 else "ps5"), f"sdv{tt}", f"x2{tt}"],
                          writes=["otb"])
                        yield 1
                        E("q", lambda q, tt=tt, dh=dh, fb0=fb0: q.dma_start(
                            out=out_d[fb0 + tt * 128: fb0 + (tt + 1) * 128,
                                      dh * 512:(dh + 1) * 512],
                            in_=ot_sb[:, :]),
                          reads=["otb"], writes=["outdr"], dma=True,
                          chan="qso")
                        yield 1



                yield "end"

        _gens = [ffn_block_emit(bl) for bl in range(NFB)]
        _gpos = [0]

        def drive_ffn(ch, k_yield=10 ** 9, drain=False):
            while _gpos[0] < NFB:
                bl = _gpos[0]
                if not drain and ch < 2 * bl + 1:
                    return
                g = _gens[bl]
                steps = 0
                for _ in g:
                    steps += 1
                    if not drain and steps >= k_yield:
                        return
                _gpos[0] += 1

        for ch in range(NCH):
            tok0 = ch * Q
            kq = ch // (NCH // 4)
            for tt in range(4):
                ut = u_ch[:, tt * D:(tt + 1) * D]
                E("q", lambda q, tok0=tok0, tt=tt, ut=ut: q.dma_start(
                    out=ut, in_=u_full[tok0 + tt * 128: tok0 + (tt + 1) * 128, :]),
                  writes=[f"u{tt}", f"xn{tt}"], dma=True, chan=f"qu{tt}")
                E("v", lambda v, tt=tt, ut=ut: v.reduce_sum(
                    st_ch[:, tt:tt + 1], ut, axis=mybir.AxisListType.X),
                  reads=[f"u{tt}"], writes=["stm"])
                E("s", lambda s, tt=tt, ut=ut: s.activation(
                    sq_ch[:, :], ut, AF.Square, accum_out=st_ch[:, 4 + tt:5 + tt]),
                  reads=[f"u{tt}"], writes=["stsq", "sts2"])
            E("v", lambda v: v.tensor_scalar(
                st_ch[:, 0:4], st_ch[:, 0:4], 1.0 / D, None, op0=ALU.mult),
              reads=["stm"], writes=["stm"])
            E("v", lambda v: v.tensor_scalar(
                st_ch[:, 4:8], st_ch[:, 4:8], 1.0 / D, EPS,
                op0=ALU.mult, op1=ALU.add),
              reads=["sts2"], writes=["sts2"])
            E("v", lambda v: v.tensor_tensor(
                st_ch[:, 8:12], st_ch[:, 0:4], st_ch[:, 0:4], op=ALU.mult),
              reads=["stm"], writes=["stm2"])
            E("v", lambda v: v.tensor_tensor(
                st_ch[:, 4:8], st_ch[:, 4:8], st_ch[:, 8:12], op=ALU.subtract),
              reads=["sts2", "stm2"], writes=["str"])
            E("s", lambda s: s.activation(st_ch[:, 4:8], st_ch[:, 4:8], AF.Ln),
              reads=["str"], writes=["str"])
            E("s", lambda s: s.activation(st_ch[:, 4:8], st_ch[:, 4:8], AF.Exp,
                                          scale=-0.5),
              reads=["str"], writes=["str"])
            for tt in range(4):
                ut = u_ch[:, tt * D:(tt + 1) * D]
                E("v", lambda v, tt=tt, ut=ut: v.tensor_scalar(
                    ut, ut, st_ch[:, tt:tt + 1], st_ch[:, 4 + tt:5 + tt],
                    op0=ALU.subtract, op1=ALU.mult),
                  reads=[f"u{tt}", "stm", "str"], writes=[f"u{tt}", f"xn{tt}"])
                E("v", lambda v, tt=tt, ut=ut: v.tensor_copy(
                    xnb_sb[:, :], ut),
                  reads=[f"xn{tt}"], writes=["xnb", "ytok0", "ytok1"])
                for k in range(8):
                    E("s", lambda t, k=k, tt=tt: t.dma_start_transpose(
                        xnT_sb[:, k * Q + tt * 128: k * Q + (tt + 1) * 128],
                        xnb_sb[:, k * 128:(k + 1) * 128]),
                      reads=["xnb"], writes=[f"xnT{k}", "xnTall"], dma=True,
                      chan="qtr")
                for k in range(2):
                    pst = psT[k]
                    E("t", lambda t, ut=ut, k=k, pst=pst: t.transpose(
                        pst[:, 0:128], ut[:, k * 128:(k + 1) * 128],
                        eye_sb[:, :]),
                      reads=[f"xn{tt}", "eye"], writes=[f"psT{k}"])
                    E("s", lambda s, k=k, tt=tt, pst=pst: s.activation(
                        xnsl_sb[:, k * Q + tt * 128: k * Q + (tt + 1) * 128],
                        pst[:, 0:128], AF.Copy),
                      reads=[f"psT{k}"], writes=[f"xnsl{k}"])
            for k in range(8):
                E("t", lambda t, k=k: t.matmul(
                    ps0[0:RK, 0:Q], wxdtT_sb[:, k * RK:(k + 1) * RK],
                    xnT_sb[:, k * Q:(k + 1) * Q],
                    start=(k == 0), stop=(k == 7)),
                  reads=[f"xnT{k}", "xnTall", "wxdtT"], writes=["ps0"])
            E("s", lambda s: s.activation(
                xdt_sb[:, :], ps0[0:RK, 0:Q], AF.Identity, bias=bxdt_sb[:, :]),
              reads=["ps0", "bxdt"], writes=["xdt"])
            for k in range(8):
                E("t", lambda t, k=k: t.matmul(
                    ps0[0:NST, 0:Q], wbT_sb[:, k * NST:(k + 1) * NST],
                    xnT_sb[:, k * Q:(k + 1) * Q],
                    start=(k == 0), stop=(k == 7)),
                  reads=[f"xnT{k}", "xnTall", "wbT"], writes=["ps0"])
            E("s", lambda s: s.activation(
                bcq_sb[:, :], ps0[0:NST, 0:Q], AF.Identity, bias=bb_sb[:, :]),
              reads=["ps0", "bbias"], writes=["bcq"])
            E("q", lambda q, tok0=tok0: q.dma_start(
                out=bc_B[:, tok0:tok0 + Q], in_=bcq_sb[:, :]),
              reads=["bcq"], writes=["bcBdr"], dma=True, chan="qsB")
            for k in range(8):
                E("t", lambda t, k=k: t.matmul(
                    ps0[0:NST, 0:Q], wcT_sb[:, k * NST:(k + 1) * NST],
                    xnT_sb[:, k * Q:(k + 1) * Q],
                    start=(k == 0), stop=(k == 7)),
                  reads=[f"xnT{k}", "xnTall", "wcT"], writes=["ps0"])
            E("s", lambda s: s.activation(
                bcq_sb[:, :], ps0[0:NST, 0:Q], AF.Identity, bias=bc_sb[:, :]),
              reads=["ps0", "cbias"], writes=["ccq", "bcq"])
            E("q", lambda q, tok0=tok0: q.dma_start(
                out=bc_C[:, tok0:tok0 + Q], in_=bcq_sb[:, :]),
              reads=["ccq"], writes=["bcCdr"], dma=True, chan="qsC")
            for dti in range(2):
                psd = ps0
                E("t", lambda t, dti=dti, psd=psd: t.matmul(
                    psd[:, 0:Q], wdtT_sb[:, dti * 128:(dti + 1) * 128],
                    xdt_sb[:, :], start=True, stop=True),
                  reads=["xdt", "wdtT"], writes=["ps0"])
                E("s", lambda s, dti=dti, psd=psd: s.activation(
                    edt_sb[:, :], psd[:, 0:Q], AF.Exp,
                    bias=bdt_sb[:, dti:dti + 1]),
                  reads=["ps0", "bdt"], writes=["stsq"])
                E("s", lambda s, dti=dti: s.activation(
                    dt_sb[:, dti * Q:(dti + 1) * Q], edt_sb[:, :], AF.Ln,
                    bias=1.0),
                  reads=["stsq"], writes=[f"dt{dti}"])
                E("g", lambda g, dti=dti: g.tensor_tensor(
                    c_sb[:, dti * Q:(dti + 1) * Q],
                    dt_sb[:, dti * Q:(dti + 1) * Q],
                    xnsl_sb[:, dti * Q:(dti + 1) * Q], op=ALU.mult),
                  reads=[f"dt{dti}", f"xnsl{dti}"], writes=[f"c{dti}"])
            for n in range(NST):
                bsl = n % 2
                E("q", lambda q, n=n, tok0=tok0, bsl=bsl: q.dma_start(
                    out=bbc_sb[:, bsl * Q:(bsl + 1) * Q],
                    in_=bc_B[n:n + 1, tok0:tok0 + Q].broadcast_to([128, Q])),
                  reads=["bcBdr"], writes=[f"bbc{bsl}"], dma=True, chan=f"qb{bsl}")
                E("q", lambda q, n=n, tok0=tok0, bsl=bsl: q.dma_start(
                    out=cbc_sb[:, bsl * Q:(bsl + 1) * Q],
                    in_=bc_C[n:n + 1, tok0:tok0 + Q].broadcast_to([128, Q])),
                  reads=["bcCdr"], writes=[f"cbc{bsl}"], dma=True, chan=f"qc{bsl}")
                for dti in range(2):
                    asl = (2 * n + dti) % 2
                    col = dti * NST + n
                    E("s", lambda s, n=n, dti=dti, asl=asl: s.activation(
                        abar_sb[:, asl * Q:(asl + 1) * Q],
                        dt_sb[:, dti * Q:(dti + 1) * Q], AF.Exp,
                        scale=acol_sb[:, dti * NST + n:dti * NST + n + 1]),
                      reads=[f"dt{dti}", "acol"], writes=[f"abar{asl}"])
                    E("g", lambda g, dti=dti, asl=asl, bsl=bsl: g.tensor_tensor(
                        in1_sb[:, asl * Q:(asl + 1) * Q],
                        c_sb[:, dti * Q:(dti + 1) * Q],
                        bbc_sb[:, bsl * Q:(bsl + 1) * Q], op=ALU.mult),
                      reads=[f"c{dti}", f"bbc{bsl}"], writes=[f"in1{asl}"])
                    hsl = dti
                    E("v", lambda v, asl=asl, hsl=hsl, col=col:
                      v.tensor_tensor_scan(
                          h_sb[:, hsl * Q:(hsl + 1) * Q],
                          abar_sb[:, asl * Q:(asl + 1) * Q],
                          in1_sb[:, asl * Q:(asl + 1) * Q],
                          hstate[:, col:col + 1], op0=ALU.mult, op1=ALU.add),
                      reads=[f"abar{asl}", f"in1{asl}", f"hst{col}"],
                      writes=[f"h{hsl}"])
                    E("s", lambda s, hsl=hsl, col=col: s.activation(
                        hstate[:, col:col + 1],
                        h_sb[:, hsl * Q + Q - 1: hsl * Q + Q], AF.Copy),
                      reads=[f"h{hsl}"], writes=[f"hst{col}"])
                    ei = n % 2
                    heng = "v" if ei == 0 else "g"
                    ysl = ei * 2 + dti
                    if n < 2:
                        E(heng, lambda e, hsl=hsl, bsl=bsl, ysl=ysl:
                          e.tensor_tensor(
                              ych_sb[:, ysl * Q:(ysl + 1) * Q],
                              h_sb[:, hsl * Q:(hsl + 1) * Q],
                              cbc_sb[:, bsl * Q:(bsl + 1) * Q], op=ALU.mult),
                          reads=[f"h{hsl}", f"cbc{bsl}"], writes=[f"ych{ysl}"])
                    else:
                        E(heng, lambda e, hsl=hsl, bsl=bsl, hcs=ei:
                          e.tensor_tensor(
                              hc_sb[:, hcs * Q:(hcs + 1) * Q],
                              h_sb[:, hsl * Q:(hsl + 1) * Q],
                              cbc_sb[:, bsl * Q:(bsl + 1) * Q], op=ALU.mult),
                          reads=[f"h{hsl}", f"cbc{bsl}"], writes=[f"hc{ei}"])
                        E(heng, lambda e, ysl=ysl, hsl=ei: e.tensor_tensor(
                            ych_sb[:, ysl * Q:(ysl + 1) * Q],
                            ych_sb[:, ysl * Q:(ysl + 1) * Q],
                            hc_sb[:, hsl * Q:(hsl + 1) * Q],
                            op=ALU.add),
                          reads=[f"hc{ei}", f"ych{ysl}"], writes=[f"ych{ysl}"])
            for dti in range(2):
                E("v", lambda v, dti=dti: v.tensor_tensor(
                    ych_sb[:, dti * Q:(dti + 1) * Q],
                    ych_sb[:, dti * Q:(dti + 1) * Q],
                    ych_sb[:, (2 + dti) * Q:(3 + dti) * Q], op=ALU.add),
                  reads=[f"ych{dti}", f"ych{2 + dti}"], writes=[f"ych{dti}"])
                E("v", lambda v, dti=dti: v.scalar_tensor_tensor(
                    ych_sb[:, dti * Q:(dti + 1) * Q],
                    xnsl_sb[:, dti * Q:(dti + 1) * Q],
                    dcol_sb[:, dti:dti + 1],
                    ych_sb[:, dti * Q:(dti + 1) * Q],
                    op0=ALU.mult, op1=ALU.add),
                  reads=[f"xnsl{dti}", "dcol", f"ych{dti}"],
                  writes=[f"ych{dti}"])
            for tt in range(4):
                for dti in range(2):
                    pst = psT[dti]
                    E("t", lambda t, dti=dti, tt=tt, pst=pst: t.transpose(
                        pst[:, 0:128],
                        ych_sb[:, dti * Q + tt * 128: dti * Q + (tt + 1) * 128],
                        eye_sb[:, :]),
                      reads=[f"ych{dti}", "eye"], writes=[f"psT{dti}"])
                    E("s", lambda s, dti=dti, tt=tt, pst=pst: s.activation(
                        ytok_sb[:, (tt % 2) * DSL + dti * 128:
                                (tt % 2) * DSL + (dti + 1) * 128],
                        pst[:, 0:128], AF.Copy),
                      reads=[f"psT{dti}"], writes=[f"ytok{tt % 2}", "xnb"])
                E("q", lambda q, tt=tt, ch=ch: q.dma_start(
                    out=ag_in[ch][tt * 128:(tt + 1) * 128, :],
                    in_=ytok_sb[:, (tt % 2) * DSL:(tt % 2 + 1) * DSL]),
                  reads=[f"ytok{tt % 2}"], writes=[f"ccy{ch}"], dma=True,
                  chan=f"qsy{ch}")
            E("g", lambda g, ch=ch: g.collective_compute(
                "AllGather", ALU.bypass, replica_groups=GRPS,
                ins=[ag_in[ch].ap().opt()],
                outs=[ag_out[ch].ap().opt()]),
              reads=[f"ccy{ch}"], writes=[f"ag{ch}"])
            drive_ffn(ch)

        drive_ffn(NCH, drain=True)

        L.barrier()

        def play(engine, name):
            if name == "q":
                pid = engine.partition_id()
                PID[0] = pid % 4
            for item in L.queue[name]:
                if item[0] == "wait":
                    engine.wait_ge(sems[item[1]], item[2])
                else:
                    _, fn, step, sem_name = item
                    fn(engine).then_inc(sems[sem_name], step)

        @block.vector
        def _(v):
            play(v, "v")

        @block.scalar
        def _(s):
            play(s, "s")

        @block.tensor
        def _(t):
            play(t, "t")

        @block.gpsimd
        def _(g):
            play(g, "g")

        @block.sync
        def _(q):
            play(q, "q")

    return nc


_CACHE = {}


def kernel(L0_attn_u, L0_attn_A_log, L0_attn_W_xdt, L0_attn_b_xdt,
           L0_attn_W_dt, L0_attn_b_dt, L0_attn_W_B, L0_attn_b_B,
           L0_attn_W_C, L0_attn_b_C, L0_attn_D, L0_ffn_w_gate,
           L0_ffn_w_up, L0_ffn_w_down):
    u = np.ascontiguousarray(np.asarray(L0_attn_u, dtype=np.float32))
    eye = np.eye(128, dtype=np.float32)
    in_maps = []
    for c in range(8):
        b, q = c // 4, c % 4
        perm = np.r_[np.arange(q * DSL, (q + 1) * DSL),
                     np.arange(0, q * DSL), np.arange((q + 1) * DSL, D)]
        im = {
            "u_full": np.ascontiguousarray(u[b][:, perm]),
            "u_tok": np.ascontiguousarray(
                u[b].reshape(NCH, 4, 128, D)[:, q].reshape(TQ, D)),
            "wxdtT": np.ascontiguousarray(
                np.asarray(L0_attn_W_xdt, np.float32).T[perm, :]),
            "bxdt": np.asarray(L0_attn_b_xdt, np.float32).reshape(RK, 1),
            "wdtT_sl": np.ascontiguousarray(
                np.asarray(L0_attn_W_dt, np.float32)[q * DSL:(q + 1) * DSL, :].T),
            "bdt_sl": np.ascontiguousarray(
                np.asarray(L0_attn_b_dt, np.float32)[q * DSL:(q + 1) * DSL]
                .reshape(DSL, 1)),
            "wbT": np.ascontiguousarray(
                np.asarray(L0_attn_W_B, np.float32).T[perm, :]),
            "bb": np.asarray(L0_attn_b_B, np.float32).reshape(NST, 1),
            "wcT": np.ascontiguousarray(
                np.asarray(L0_attn_W_C, np.float32).T[perm, :]),
            "bc": np.asarray(L0_attn_b_C, np.float32).reshape(NST, 1),
            "alog_sl": np.ascontiguousarray(
                np.asarray(L0_attn_A_log, np.float32)[q * DSL:(q + 1) * DSL, :]),
            "dvec_sl": np.ascontiguousarray(
                np.asarray(L0_attn_D, np.float32)[q * DSL:(q + 1) * DSL]
                .reshape(DSL, 1)),
            "eye": eye,
            "wgT_q": np.ascontiguousarray(
                np.asarray(L0_ffn_w_gate, np.float32).T[:, q * 1024:(q + 1) * 1024]),
            "wuT_q": np.ascontiguousarray(
                np.asarray(L0_ffn_w_up, np.float32).T[:, q * 1024:(q + 1) * 1024]),
            "wdT_q": np.ascontiguousarray(
                np.asarray(L0_ffn_w_down, np.float32).T[q * 1024:(q + 1) * 1024, :]),
        }
        in_maps.append(im)
    if "nc" not in _CACHE:
        _CACHE["nc"] = build_program()
    nc = _CACHE["nc"]
    res = run_bass_kernel_spmd(nc, in_maps, core_ids=list(range(8)))
    _CACHE["res"] = res
    out = np.empty((B, T, D), dtype=np.float32)
    ov = out.reshape(B, NCH, 4, 128, D)
    for c in range(8):
        b, q = c // 4, c % 4
        ov[b, :, q] = res.results[c]["out"].reshape(NCH, 128, D)
    return out


# version 08
# speedup vs baseline: 1.0837x; 1.0837x over previous
import numpy as np
from contextlib import ExitStack
import concourse.bass as bass
import concourse.mybir as mybir
from concourse.bass_utils import run_bass_kernel_spmd

F32 = mybir.dt.float32
BF16 = mybir.dt.bfloat16
AF = mybir.ActivationFunctionType
ALU = mybir.AluOpType

B, T, D, NST, RK, INNER = 2, 8192, 1024, 16, 64, 4096
DSL = 256
Q = 512
NCH = T // Q
TQ = T // 4
FBL = 256
NFB = TQ // FBL
NTT = FBL // 128
MAGIC = float(np.float32(1.5 * 2 ** 23))
EPS = 1e-5
NW = float(INNER * D)
GRPS = [[0, 1, 2, 3], [4, 5, 6, 7]]


class Ledger:

    def __init__(self, sems, engines):
        self.sems = sems
        self.engines = engines
        self.cnt = {e: 0 for e in sems}
        self.mark = {e: {} for e in engines}
        self.buf = {}
        self.queue = {e: [] for e in engines}

    def emit(self, e, fn, reads=(), writes=(), dma=False, chan=None):
        sem_name = chan if (dma and chan) else e
        assert sem_name in self.sems, sem_name
        need = {}
        for k in reads:
            b = self.buf.get(k)
            if b and b["w"] is not None:
                se, sv = b["w"]
                need[se] = max(need.get(se, -1), sv)
        for k in writes:
            b = self.buf.get(k)
            if b:
                if b["w"] is not None:
                    se, sv = b["w"]
                    need[se] = max(need.get(se, -1), sv)
                for se, sv in b["r"].items():
                    need[se] = max(need.get(se, -1), sv)
        for se in sorted(need):
            sv = need[se]
            if se == e == "t":
                continue
            if se not in self.engines:
                sv = self.cnt.get(se, sv)
            if self.mark[e].get(se, -1) < sv:
                self.queue[e].append(("wait", se, sv))
                self.mark[e][se] = sv
        step = 16 if dma else 1
        self.cnt[sem_name] = self.cnt.get(sem_name, 0) + step
        v = self.cnt[sem_name]
        self.queue[e].append(("op", fn, step, sem_name))
        for k in reads:
            b = self.buf.setdefault(k, {"w": None, "r": {}})
            b["r"][sem_name] = max(b["r"].get(sem_name, -1), v)
        for k in writes:
            self.buf[k] = {"w": (sem_name, v), "r": {}}

    def barrier(self):
        tops = dict(self.cnt)
        for e in self.queue:
            for se, sv in tops.items():
                if se != e and sv > 0 and self.mark[e].get(se, -1) < sv:
                    self.queue[e].append(("wait", se, sv))
                    self.mark[e][se] = sv


def build_program():
    nc = bass.Bass("TRN2", target_bir_lowering=False)

    u_full = nc.dram_tensor("u_full", [T, D], F32, kind="ExternalInput").ap()
    u_tok = nc.dram_tensor("u_tok", [TQ, D], F32, kind="ExternalInput").ap()
    wxdtT_d = nc.dram_tensor("wxdtT", [D, RK], F32, kind="ExternalInput").ap()
    bxdt_d = nc.dram_tensor("bxdt", [RK, 1], F32, kind="ExternalInput").ap()
    wdtT_d = nc.dram_tensor("wdtT_sl", [RK, DSL], F32, kind="ExternalInput").ap()
    bdt_d = nc.dram_tensor("bdt_sl", [DSL, 1], F32, kind="ExternalInput").ap()
    wbT_d = nc.dram_tensor("wbT", [D, NST], F32, kind="ExternalInput").ap()
    bb_d = nc.dram_tensor("bb", [NST, 1], F32, kind="ExternalInput").ap()
    wcT_d = nc.dram_tensor("wcT", [D, NST], F32, kind="ExternalInput").ap()
    bc_d = nc.dram_tensor("bc", [NST, 1], F32, kind="ExternalInput").ap()
    alog_d = nc.dram_tensor("alog_sl", [DSL, NST], F32, kind="ExternalInput").ap()
    dvec_d = nc.dram_tensor("dvec_sl", [DSL, 1], F32, kind="ExternalInput").ap()
    eye_d = nc.dram_tensor("eye", [128, 128], F32, kind="ExternalInput").ap()
    wgT_d = nc.dram_tensor("wgT_q", [D, 1024], F32, kind="ExternalInput").ap()
    wuT_d = nc.dram_tensor("wuT_q", [D, 1024], F32, kind="ExternalInput").ap()
    wdT_d = nc.dram_tensor("wdT_q", [1024, D], F32, kind="ExternalInput").ap()
    out_d = nc.dram_tensor("out", [TQ, D], F32, kind="ExternalOutput").ap()
    bc_B = nc.dram_tensor("bc_B", [NST, T], F32).ap()
    bc_C = nc.dram_tensor("bc_C", [NST, T], F32).ap()
    scal_dr = nc.dram_tensor("scal_dr", [1, 8], F32).ap()
    ag_in = [nc.dram_tensor(f"agi{c}", [Q, DSL], F32) for c in range(NCH)]
    ag_out = [nc.dram_tensor(f"ago{c}", [4, Q, DSL], F32) for c in range(NCH)]
    cc_w_in = nc.dram_tensor("ccwi", [3, 1024, 1024], BF16)
    cc_w_out = nc.dram_tensor("ccwo", [4, 3, 1024, 1024], BF16)
    cc_s_in = nc.dram_tensor("ccsi", [1, 4], F32)
    cc_s_out = nc.dram_tensor("ccso", [1, 4], F32)

    _st = ExitStack()
    with _st:
        eye_sb = _st.enter_context(nc.sbuf_tensor("eye_sb", [128, 128], F32))
        eyeb_sb = _st.enter_context(nc.sbuf_tensor("eyeb_sb", [128, 128], BF16))
        wxdtT_sb = _st.enter_context(nc.sbuf_tensor("wxdtT_sb", [128, 8 * RK], BF16))
        wbT_sb = _st.enter_context(nc.sbuf_tensor("wbT_sb", [128, 8 * NST], BF16))
        wcT_sb = _st.enter_context(nc.sbuf_tensor("wcT_sb", [128, 8 * NST], BF16))
        wdtT_sb = _st.enter_context(nc.sbuf_tensor("wdtT_sb", [RK, DSL], BF16))
        bxdt_sb = _st.enter_context(nc.sbuf_tensor("bxdt_sb", [RK, 1], F32))
        bdt_sb = _st.enter_context(nc.sbuf_tensor("bdt_sb", [128, 2], F32))
        dcol_sb = _st.enter_context(nc.sbuf_tensor("dcol_sb", [128, 2], F32))
        bb_sb = _st.enter_context(nc.sbuf_tensor("bb_sb", [NST, 1], F32))
        bc_sb = _st.enter_context(nc.sbuf_tensor("bc_sb", [NST, 1], F32))
        acol_sb = _st.enter_context(nc.sbuf_tensor("acol_sb", [128, 2 * NST], F32))
        hstate = _st.enter_context(nc.sbuf_tensor("hstate", [128, 2 * NST], F32))
        wsum_sb = _st.enter_context(nc.sbuf_tensor("wsum_sb", [128, 4], F32))
        wsc_sb = _st.enter_context(nc.sbuf_tensor("wsc_sb", [1, 8], F32))
        wscb_sb = _st.enter_context(nc.sbuf_tensor("wscb_sb", [128, 8], F32))
        u_ch = _st.enter_context(nc.sbuf_tensor("u_ch", [128, 4 * D], F32))
        ygq_sb = _st.enter_context(nc.sbuf_tensor("ygq_sb", [128, 2 * D], F32))
        ufq_sb = _st.enter_context(nc.sbuf_tensor("ufq_sb", [128, 2 * D], F32))
        fscr_sb = _st.enter_context(nc.sbuf_tensor("fscr_sb", [128, D], F32))
        sq_ch = _st.enter_context(nc.sbuf_tensor("sq_ch", [128, D], F32))
        st_ch = _st.enter_context(nc.sbuf_tensor("st_ch", [128, 12], F32))
        xnT_sb = _st.enter_context(nc.sbuf_tensor("xnT_sb", [128, 8 * Q], BF16))
        xnsl_sb = _st.enter_context(nc.sbuf_tensor("xnsl_sb", [128, 2 * Q], F32))
        dt_sb = _st.enter_context(nc.sbuf_tensor("dt_sb", [128, 2 * Q], BF16))
        bcq_sb = _st.enter_context(nc.sbuf_tensor("bcq_sb", [NST, Q], F32))
        c_sb = _st.enter_context(nc.sbuf_tensor("c_sb", [128, 2 * Q], F32))
        bbc_sb = _st.enter_context(nc.sbuf_tensor("bbc_sb", [128, 2 * Q], F32))
        cbc_sb = _st.enter_context(nc.sbuf_tensor("cbc_sb", [128, 2 * Q], F32))
        abar_sb = _st.enter_context(nc.sbuf_tensor("abar_sb", [128, 4 * Q], F32))
        in1_sb = _st.enter_context(nc.sbuf_tensor("in1_sb", [128, 4 * Q], F32))
        h_sb = _st.enter_context(nc.sbuf_tensor("h_sb", [128, 4 * Q], F32))
        hc_sb = _st.enter_context(nc.sbuf_tensor("hc_sb", [128, 2 * Q], F32))
        ych_sb = _st.enter_context(nc.sbuf_tensor("ych_sb", [128, 4 * Q], F32))
        ytok_sb = _st.enter_context(nc.sbuf_tensor("ytok_sb", [128, 2 * DSL], F32))
        x2_sb = _st.enter_context(nc.sbuf_tensor("x2_sb", [128, NTT * D], F32))
        fst_sb = _st.enter_context(nc.sbuf_tensor("fst_sb", [128, 16], F32))
        xq_sb = _st.enter_context(nc.sbuf_tensor("xq_sb", [128, NTT * D], BF16))
        xqT_sb = _st.enter_context(nc.sbuf_tensor("xqT_sb", [128, 8 * FBL], BF16))
        wA_sb = _st.enter_context(nc.sbuf_tensor("wA_sb", [128, 2 * 8 * 512], BF16))
        wB_sb = _st.enter_context(nc.sbuf_tensor("wB_sb", [128, 2 * 8 * 512], BF16))
        gate_sb = _st.enter_context(nc.sbuf_tensor("gate_sb", [128, 512], F32))
        arena_a = _st.enter_context(nc.sbuf_tensor("arena_a", [128, 8192], F32))
        arena_b = _st.enter_context(nc.sbuf_tensor("arena_b", [128, 4096], F32))
        g_sb = arena_a.ap()
        wq_ld = arena_a.ap()
        gq_sb = arena_a.ap().bitcast(BF16)[:, 0:NTT * INNER]
        gqT_sb = arena_b.ap().bitcast(BF16)[:, 0:NTT * INNER]
        wq_r = arena_a.ap()
        wq_b = wA_sb.ap()[:, 0:4096]
        yg_sb = ygq_sb.ap()
        uf_sb = ufq_sb.ap()
        edt_sb = sq_ch.ap()[:, 0:Q]
        xnb_sb = ytok_sb.ap().bitcast(BF16)

        fsq_sb = fscr_sb.ap()
        fdu_sb = fscr_sb.ap()
        xdt_sb = dt_sb.ap()[0:RK, Q:2 * Q]
        wrow_sb = u_ch.ap()[0:1, 0:128]
        ps0 = _st.enter_context(nc.psum_tensor("ps0", [128, 512], F32))
        ps1 = _st.enter_context(nc.psum_tensor("ps1", [128, 512], F32))
        ps2 = _st.enter_context(nc.psum_tensor("ps2", [128, 512], F32))
        ps3 = _st.enter_context(nc.psum_tensor("ps3", [128, 512], F32))
        ps4 = _st.enter_context(nc.psum_tensor("ps4", [128, 512], F32))
        ps5 = _st.enter_context(nc.psum_tensor("ps5", [128, 512], F32))
        ps6 = _st.enter_context(nc.psum_tensor("ps6", [128, 512], F32))
        ps7 = _st.enter_context(nc.psum_tensor("ps7", [128, 512], F32))
        sem_v = _st.enter_context(nc.semaphore("sem_v"))
        sem_s = _st.enter_context(nc.semaphore("sem_s"))
        sem_t = _st.enter_context(nc.semaphore("sem_t"))
        sem_g = _st.enter_context(nc.semaphore("sem_g"))
        sem_q = _st.enter_context(nc.semaphore("sem_q"))
        block = _st.enter_context(nc.Block())
        sems = {"v": sem_v, "s": sem_s, "t": sem_t, "g": sem_g, "q": sem_q}
        CHANS = ["qm", "qwl", "qwr", "qss", "qsc", "qsw", "qu0", "qu1",
                 "qsB", "qsC", "qb0", "qb1", "qc0", "qc1",
                 "qsy0", "qsy1", "qsy2", "qsy3", "qsy4", "qsy5", "qsy6", "qsy7", "qsy8", "qsy9", "qsy10", "qsy11", "qsy12", "qsy13", "qsy14", "qsy15", "qdbg", "qd", "qu2", "qu3", "qtr",
                 "qy0", "qy1", "qf0", "qf1", "qw0", "qw1", "qw20", "qw21", "qso"]
        for cn in CHANS:
            sems[cn] = _st.enter_context(nc.semaphore("sem_" + cn))
        L = Ledger(sems, ["v", "s", "t", "g", "q"])
        E = L.emit
        PID = [None]

        E("q", lambda q: q.dma_start(out=eye_sb[:, :], in_=eye_d[:, :]),
          writes=["eye"], dma=True, chan="qm")
        E("v", lambda v: v.tensor_copy(eyeb_sb[:, :], eye_sb[:, :]),
          reads=["eye"], writes=["eyeb"])
        for nm, dst, src, nk, w in [
            ("wxdtT", wxdtT_sb, wxdtT_d, 8, RK),
            ("wbT", wbT_sb, wbT_d, 8, NST),
            ("wcT", wcT_sb, wcT_d, 8, NST),
        ]:
            E("q", lambda q, src=src, nk=nk, w=w: q.dma_start(
                out=wq_ld[:, 0:nk * w].rearrange("p (k r) -> p k r", k=nk),
                in_=src.rearrange("(k p) r -> p k r", p=128)),
              writes=["wq_ld"], dma=True, chan="qwl")
            E("v", lambda v, dst=dst, nk=nk, w=w: v.tensor_copy(
                dst[:, :], wq_ld[:, 0:nk * w]),
              reads=["wq_ld"], writes=[nm])
        E("q", lambda q: q.dma_start(out=wq_ld[0:RK, 0:DSL], in_=wdtT_d[:, :]),
          writes=["wq_ld"], dma=True, chan="qwl")
        E("v", lambda v: v.tensor_copy(wdtT_sb[:, :], wq_ld[0:RK, 0:DSL]),
          reads=["wq_ld"], writes=["wdtT"])
        E("q", lambda q: q.dma_start(out=bxdt_sb[:, :], in_=bxdt_d[:, :]),
          writes=["bxdt"], dma=True, chan="qm")
        E("q", lambda q: q.dma_start(
            out=bdt_sb[:, :].rearrange("p (d c) -> p d c", d=2),
            in_=bdt_d.rearrange("(d p) c -> p d c", p=128),
            allow_slow_non_contiguous=True),
          writes=["bdt"], dma=True, chan="qm")
        E("q", lambda q: q.dma_start(
            out=dcol_sb[:, :].rearrange("p (d c) -> p d c", d=2),
            in_=dvec_d.rearrange("(d p) c -> p d c", p=128),
            allow_slow_non_contiguous=True),
          writes=["dcol"], dma=True, chan="qm")
        E("q", lambda q: q.dma_start(out=bb_sb[:, :], in_=bb_d[:, :]),
          writes=["bbias"], dma=True, chan="qm")
        E("q", lambda q: q.dma_start(out=bc_sb[:, :], in_=bc_d[:, :]),
          writes=["cbias"], dma=True, chan="qm")
        E("q", lambda q: q.dma_start(
            out=acol_sb[:, :].rearrange("p (d n) -> p d n", d=2),
            in_=alog_d.rearrange("(d p) n -> p d n", p=128)),
          writes=["acol"], dma=True, chan="qm")
        E("s", lambda s: s.activation(acol_sb[:, :], acol_sb[:, :], AF.Exp),
          reads=["acol"], writes=["acol"])
        E("v", lambda v: v.tensor_scalar_mul(acol_sb[:, :], acol_sb[:, :], -1.0),
          reads=["acol"], writes=["acol"])
        E("v", lambda v: v.memset(hstate[:, :], 0.0),
          writes=[f"hst{i}" for i in range(2 * NST)])

        wsrc = [wgT_d, wuT_d, wdT_d]
        for m in range(3):
            E("q", lambda q, m=m: q.dma_start(
                out=wq_ld[:, :].rearrange("p (k c) -> p k c", k=8),
                in_=wsrc[m].rearrange("(k p) c -> p k c", p=128)),
              writes=["wq_ld"], dma=True, chan="qwl")
            E("s", lambda s, m=m: s.activation(
                wq_r[:, :], wq_ld[:, :], AF.Abs,
                accum_out=wsum_sb[:, m:m + 1]),
              reads=["wq_ld"], writes=["wq_r", f"wsum{m}"])
        for m in range(3):
            E("q", lambda q, m=m: q.dma_start(
                out=wrow_sb[:, :], in_=wsum_sb[:, m:m + 1],
                allow_slow_non_contiguous=True),
              reads=[f"wsum{m}"], writes=["wrow"], dma=True, chan="qwr")
            E("v", lambda v, m=m: v.reduce_sum(
                wsc_sb[0:1, m:m + 1], wrow_sb[:, :], axis=mybir.AxisListType.X),
              reads=["wrow"], writes=["wsc"])
        E("q", lambda q: q.dma_start(out=cc_s_in[0:1, :], in_=wsc_sb[0:1, 0:4]),
          reads=["wsc"], writes=["ccsi"], dma=True, chan="qss")
        E("g", lambda g: g.collective_compute(
            "AllReduce", ALU.add, replica_groups=GRPS,
            ins=[cc_s_in.ap().opt()], outs=[cc_s_out.ap().opt()]),
          reads=["ccsi"], writes=["ccso"])
        E("q", lambda q: q.dma_start(out=wsc_sb[0:1, 0:4], in_=cc_s_out[0:1, :]),
          reads=["ccso"], writes=["wsc"], dma=True, chan="qss")
        E("v", lambda v: v.tensor_scalar(
            wsc_sb[0:1, 4:8], wsc_sb[0:1, 0:4], 1.0 / NW, 1e-5,
            op0=ALU.mult, op1=ALU.max),
          reads=["wsc"], writes=["wsc"])
        E("v", lambda v: v.reciprocal(wsc_sb[0:1, 0:4], wsc_sb[0:1, 4:8]),
          reads=["wsc"], writes=["wsc"])
        E("q", lambda q: q.dma_start(out=scal_dr[0:1, :], in_=wsc_sb[0:1, :]),
          reads=["wsc"], writes=["scal"], dma=True, chan="qsc")
        E("q", lambda q: q.dma_start(out=wscb_sb[:, :],
                                     in_=scal_dr.broadcast_to([128, 8])),
          reads=["scal"], writes=["wscb"], dma=True, chan="qsc")
        for m in range(3):
            E("q", lambda q, m=m: q.dma_start(
                out=wq_ld[:, :].rearrange("p (k c) -> p k c", k=8),
                in_=wsrc[m].rearrange("(k p) c -> p k c", p=128)),
              writes=["wq_ld"], dma=True, chan="qwl")
            E("v", lambda v, m=m: v.tensor_scalar(
                wq_ld[:, :], wq_ld[:, :], wscb_sb[:, m:m + 1], MAGIC,
                op0=ALU.mult, op1=ALU.add),
              reads=["wq_ld", "wscb"], writes=["wq_ld"])
            E("v", lambda v: v.tensor_scalar(
                wq_ld[:, :], wq_ld[:, :], MAGIC, None, op0=ALU.subtract),
              reads=["wq_ld"], writes=["wq_ld"])
            for hf in range(2):
                E("v", lambda v, hf=hf: v.tensor_scalar(
                    wq_b[:, 0:4096], wq_ld[:, hf * 4096:(hf + 1) * 4096],
                    1.0, -1.0, op0=ALU.min, op1=ALU.max),
                  reads=["wq_ld"], writes=["wq_b"])
                E("q", lambda q, m=m, hf=hf: q.dma_start(
                    out=cc_w_in[m, :, :].rearrange("(k p) c -> p k c", p=128)
                    [:, hf * 4:(hf + 1) * 4, :],
                    in_=wq_b[:, 0:4096].rearrange("p (k c) -> p k c", k=4)),
                  reads=["wq_b"], writes=["ccwi"], dma=True, chan="qsw")
        E("g", lambda g: g.collective_compute(
            "AllGather", ALU.bypass, replica_groups=GRPS,
            ins=[cc_w_in.ap().opt()], outs=[cc_w_out.ap().opt()]),
          reads=["ccwi"], writes=["ccwo"])

        psT = [ps3, ps4]

        def ln_apply(src_key, src_tile, st, dst_key, dst_tile):
            E("v", lambda v: v.reduce_sum(
                st[:, 0:1], src_tile, axis=mybir.AxisListType.X),
              reads=[src_key], writes=["stm"])
            E("s", lambda s: s.activation(
                sq_ch[:, :], src_tile, AF.Square, accum_out=st[:, 2:3]),
              reads=[src_key], writes=["stsq", "sts2"])
            E("v", lambda v: v.tensor_scalar(
                st[:, 0:1], st[:, 0:1], 1.0 / D, None, op0=ALU.mult),
              reads=["stm"], writes=["stm"])
            E("v", lambda v: v.tensor_scalar(
                st[:, 2:3], st[:, 2:3], 1.0 / D, EPS,
                op0=ALU.mult, op1=ALU.add),
              reads=["sts2"], writes=["sts2"])
            E("v", lambda v: v.tensor_tensor(
                st[:, 3:4], st[:, 0:1], st[:, 0:1], op=ALU.mult),
              reads=["stm"], writes=["stm2"])
            E("v", lambda v: v.tensor_tensor(
                st[:, 1:2], st[:, 2:3], st[:, 3:4], op=ALU.subtract),
              reads=["sts2", "stm2"], writes=["str"])
            E("s", lambda s: s.activation(st[:, 1:2], st[:, 1:2], AF.Ln),
              reads=["str"], writes=["str"])
            E("s", lambda s: s.activation(st[:, 1:2], st[:, 1:2], AF.Exp,
                                          scale=-0.5),
              reads=["str"], writes=["str"])
            E("v", lambda v: v.tensor_scalar(
                dst_tile, src_tile, st[:, 0:1], st[:, 1:2],
                op0=ALU.subtract, op1=ALU.mult),
              reads=[src_key, "stm", "str"], writes=[dst_key])

        def ln2_apply(src_key, src_tile, st, dst_key, dst_tile, scr):
            E("v", lambda v: v.reduce_sum(
                st[:, 0:1], src_tile, axis=mybir.AxisListType.X),
              reads=[src_key], writes=["fm"])
            E("s", lambda s: s.activation(
                scr[:, :], src_tile, AF.Square, accum_out=st[:, 2:3]),
              reads=[src_key], writes=["fdu", "fs2"])
            E("v", lambda v: v.tensor_scalar(
                st[:, 0:1], st[:, 0:1], 1.0 / D, None, op0=ALU.mult),
              reads=["fm"], writes=["fm"])
            E("v", lambda v: v.tensor_scalar(
                st[:, 2:3], st[:, 2:3], 1.0 / D, EPS, op0=ALU.mult, op1=ALU.add),
              reads=["fs2"], writes=["fs2"])
            E("v", lambda v: v.tensor_tensor(
                st[:, 3:4], st[:, 0:1], st[:, 0:1], op=ALU.mult),
              reads=["fm"], writes=["fm2"])
            E("v", lambda v: v.tensor_tensor(
                st[:, 1:2], st[:, 2:3], st[:, 3:4], op=ALU.subtract),
              reads=["fs2", "fm2"], writes=["fr"])
            E("s", lambda s: s.activation(st[:, 1:2], st[:, 1:2], AF.Ln),
              reads=["fr"], writes=["fr"])
            E("s", lambda s: s.activation(st[:, 1:2], st[:, 1:2], AF.Exp,
                                          scale=-0.5),
              reads=["fr"], writes=["fr"])
            E("v", lambda v: v.tensor_scalar(
                dst_tile, src_tile, st[:, 0:1], st[:, 1:2],
                op0=ALU.subtract, op1=ALU.mult),
              reads=[src_key, "fm", "fr"], writes=[dst_key])

        psG = [ps1, ps5]
        psU = [ps2, ps6]
        psD_ = [ps7, ps5]

        def ffn_block_emit(bl):
                fb0 = bl * FBL
                for tt in range(NTT):
                    r0 = fb0 + tt * 128
                    ych_ = 2 * bl + tt
                    E("q", lambda q, tt=tt, ych_=ych_: q.dma_start(
                        out=yg_sb[:, tt * D:(tt + 1) * D]
                        .rearrange("p (j c) -> p j c", j=4),
                        in_=ag_out[ych_].ap()[:, bass.ts(PID[0], 128), :]
                        .transpose([1, 0, 2])),
                      reads=[f"ag{ych_}"], writes=[f"yg{tt}"], dma=True,
                      chan=f"qy{tt}")
                    yield 1
                    E("q", lambda q, r0=r0, tt=tt: q.dma_start(
                        out=uf_sb[:, tt * D:(tt + 1) * D],
                        in_=u_tok[r0:r0 + 128, :]),
                      writes=[f"uf{tt}"], dma=True, chan=f"qf{tt}")
                    yield 1
                    uft = uf_sb[:, tt * D:(tt + 1) * D]
                    E("v", lambda v, tt=tt, uft=uft: v.tensor_tensor(
                        x2_sb[:, tt * D:(tt + 1) * D], uft,
                        yg_sb[:, tt * D:(tt + 1) * D], op=ALU.add),
                      reads=[f"uf{tt}", f"yg{tt}"], writes=[f"x2{tt}"])
                    yield 1
                    ln2_apply(f"x2{tt}", x2_sb[:, tt * D:(tt + 1) * D], fst_sb,
                              "fsq", fsq_sb[:, :], fdu_sb)
                    E("v", lambda v: v.reduce_max(
                        fst_sb[:, 4:5], fsq_sb[:, :], axis=mybir.AxisListType.X,
                        apply_absolute_value=True),
                      reads=["fsq"], writes=["fmax"])
                    yield 1
                    E("v", lambda v: v.tensor_scalar_max(
                        fst_sb[:, 4:5], fst_sb[:, 4:5], 1e-5),
                      reads=["fmax"], writes=["fmax"])
                    yield 1
                    E("v", lambda v: v.reciprocal(fst_sb[:, 5:6], fst_sb[:, 4:5]),
                      reads=["fmax"], writes=["fsx"])
                    yield 1
                    E("v", lambda v: v.tensor_scalar(
                        fst_sb[:, 5:6], fst_sb[:, 5:6], 127.0, None, op0=ALU.mult),
                      reads=["fsx"], writes=["fsx"])
                    yield 1
                    E("v", lambda v, tt=tt: v.tensor_scalar(
                        fst_sb[:, 6 + tt:7 + tt], fst_sb[:, 4:5], 1.0 / 127.0, None,
                        op0=ALU.mult),
                      reads=["fmax"], writes=[f"fisx{tt}"])
                    yield 1
                    E("v", lambda v: v.tensor_scalar(
                        fdu_sb[:, :], fsq_sb[:, :], fst_sb[:, 5:6], MAGIC,
                        op0=ALU.mult, op1=ALU.add),
                      reads=["fsq", "fsx"], writes=["fdu"])
                    yield 1
                    E("v", lambda v, tt=tt: v.tensor_scalar(
                        xq_sb[:, tt * D:(tt + 1) * D], fdu_sb[:, :], MAGIC, None,
                        op0=ALU.subtract),
                      reads=["fdu"], writes=[f"xq{tt}"])
                    yield 1
                    for k in range(8):
                        pst = psT[k % 2]
                        E("t", lambda t, k=k, tt=tt, pst=pst: t.transpose(
                            pst.bitcast(BF16)[:, 0:128],
                            xq_sb[:, tt * D + k * 128: tt * D + (k + 1) * 128],
                            eyeb_sb[:, :]),
                          reads=[f"xq{tt}", "eyeb"], writes=[f"psT{k % 2}"])
                        yield 1
                        E("s", lambda s, k=k, tt=tt, pst=pst: s.activation(
                            xqT_sb[:, k * FBL + tt * 128: k * FBL + (tt + 1) * 128],
                            pst.bitcast(BF16)[:, 0:128], AF.Copy),
                          reads=[f"psT{k % 2}"], writes=[f"xqT{k}"])
                        yield 1
                    E("v", lambda v, tt=tt: v.tensor_tensor(
                        fst_sb[:, 8 + tt:9 + tt], fst_sb[:, 6 + tt:7 + tt],
                        wscb_sb[:, 4:5], op=ALU.mult),
                      reads=[f"fisx{tt}", "wscb"], writes=[f"sgv{tt}"])
                    yield 1
                    E("v", lambda v, tt=tt: v.tensor_tensor(
                        fst_sb[:, 10 + tt:11 + tt], fst_sb[:, 6 + tt:7 + tt],
                        wscb_sb[:, 5:6], op=ALU.mult),
                      reads=[f"fisx{tt}", "wscb"], writes=[f"suv{tt}"])
                    yield 1
                for ic in range(8):
                    sl_ = ic % 2
                    W0 = sl_ * 8 * 512
                    E("q", lambda q, ic=ic, W0=W0: q.dma_start(
                        out=wA_sb[:, W0:W0 + 8 * 512].rearrange("p (k c) -> p k c", k=8),
                        in_=cc_w_out[ic // 2, 0, :, (ic % 2) * 512:(ic % 2 + 1) * 512]
                        .rearrange("(k p) c -> p k c", p=128)),
                      reads=["ccwo"], writes=[f"wA{sl_}"], dma=True, chan=f"qw{sl_}")
                    yield 1
                    E("q", lambda q, ic=ic, W0=W0: q.dma_start(
                        out=wB_sb[:, W0:W0 + 8 * 512].rearrange("p (k c) -> p k c", k=8),
                        in_=cc_w_out[ic // 2, 1, :, (ic % 2) * 512:(ic % 2 + 1) * 512]
                        .rearrange("(k p) c -> p k c", p=128)),
                      reads=["ccwo"], writes=[f"wB{sl_}"], dma=True, chan=f"qw2{sl_}")
                    yield 1
                    for tt in range(NTT):
                        for k in range(8):
                            E("t", lambda t, k=k, tt=tt, W0=W0: t.matmul(
                                psG[tt][:, :],
                                xqT_sb[:, k * FBL + tt * 128:
                                       k * FBL + (tt + 1) * 128],
                                wA_sb[:, W0 + k * 512:W0 + (k + 1) * 512],
                                start=(k == 0), stop=(k == 7)),
                              reads=[f"xqT{k}", f"wA{sl_}"], writes=["ps1" if tt == 0 else "ps5"])
                            yield 1
                        for k in range(8):
                            E("t", lambda t, k=k, tt=tt, W0=W0: t.matmul(
                                psU[tt][:, :],
                                xqT_sb[:, k * FBL + tt * 128:
                                       k * FBL + (tt + 1) * 128],
                                wB_sb[:, W0 + k * 512:W0 + (k + 1) * 512],
                                start=(k == 0), stop=(k == 7)),
                              reads=[f"xqT{k}", f"wB{sl_}"], writes=["ps2" if tt == 0 else "ps6"])
                            yield 1
                        E("s", lambda s, tt=tt: s.activation(
                            gate_sb[:, :], psG[tt][:, :], AF.Silu,
                            scale=fst_sb[:, 8 + tt:9 + tt]),
                          reads=[("ps1" if tt == 0 else "ps5"), f"sgv{tt}"], writes=["gate"])
                        yield 1
                        E("v", lambda v, tt=tt, ic=ic: v.scalar_tensor_tensor(
                            g_sb[:, tt * INNER + ic * 512:
                                 tt * INNER + (ic + 1) * 512],
                            psU[tt][:, :], fst_sb[:, 10 + tt:11 + tt],
                            gate_sb[:, :], op0=ALU.mult, op1=ALU.mult),
                          reads=[("ps2" if tt == 0 else "ps6"), f"suv{tt}", "gate"],
                          writes=[f"g{tt}_{ic}"])
                        yield 1
                for tt in range(NTT):
                    E("v", lambda v, tt=tt: v.reduce_max(
                        fst_sb[:, 12 + tt:13 + tt],
                        g_sb[:, tt * INNER:(tt + 1) * INNER],
                        axis=mybir.AxisListType.X, apply_absolute_value=True),
                      reads=[f"g{tt}_{ic}" for ic in range(8)], writes=[f"gmax{tt}"])
                    yield 1
                    E("v", lambda v, tt=tt: v.tensor_scalar_max(
                        fst_sb[:, 12 + tt:13 + tt], fst_sb[:, 12 + tt:13 + tt],
                        1e-5),
                      reads=[f"gmax{tt}"], writes=[f"gmax{tt}"])
                    yield 1
                    E("v", lambda v, tt=tt: v.reciprocal(
                        fst_sb[:, 14 + tt:15 + tt], fst_sb[:, 12 + tt:13 + tt]),
                      reads=[f"gmax{tt}"], writes=[f"gsx{tt}"])
                    yield 1
                    E("v", lambda v, tt=tt: v.tensor_scalar(
                        fst_sb[:, 14 + tt:15 + tt], fst_sb[:, 14 + tt:15 + tt],
                        127.0, None, op0=ALU.mult),
                      reads=[f"gsx{tt}"], writes=[f"gsx{tt}"])
                    yield 1
                    E("v", lambda v, tt=tt: v.tensor_scalar(
                        fst_sb[:, 12 + tt:13 + tt], fst_sb[:, 12 + tt:13 + tt],
                        1.0 / 127.0, None, op0=ALU.mult),
                      reads=[f"gmax{tt}"], writes=[f"gisx{tt}"])
                    yield 1
                    E("v", lambda v, tt=tt: v.tensor_tensor(
                        fst_sb[:, 12 + tt:13 + tt], fst_sb[:, 12 + tt:13 + tt],
                        wscb_sb[:, 6:7], op=ALU.mult),
                      reads=[f"gisx{tt}", "wscb"], writes=[f"sdv{tt}"])
                    yield 1
                    for hf in range(4):
                        E("v", lambda v, tt=tt, hf=hf: v.tensor_scalar(
                            fscr_sb[:, :],
                            g_sb[:, tt * INNER + hf * 1024:
                                 tt * INNER + (hf + 1) * 1024],
                            fst_sb[:, 14 + tt:15 + tt], MAGIC,
                            op0=ALU.mult, op1=ALU.add),
                          reads=[f"g{tt}_{ic}" for ic in range(8)] + [f"gsx{tt}"],
                          writes=["fdu"])
                        yield 1
                        E("v", lambda v, tt=tt, hf=hf: v.tensor_scalar(
                            gq_sb[:, tt * INNER + hf * 1024:
                                  tt * INNER + (hf + 1) * 1024],
                            fscr_sb[:, :], MAGIC, None, op0=ALU.subtract),
                          reads=["fdu"],
                      writes=[f"gq{tt}", f"gqg{tt}{hf % 2}"])
                        yield 1
                    for k in range(32):
                        pst = psT[k % 2]
                        E("t", lambda t, k=k, tt=tt, pst=pst: t.transpose(
                            pst.bitcast(BF16)[:, 0:128],
                            gq_sb[:, tt * INNER + k * 128:
                                  tt * INNER + (k + 1) * 128],
                            eyeb_sb[:, :]),
                          reads=[f"gq{tt}", f"g{tt}_0", "eyeb"],
                          writes=[f"psT{k % 2}"])
                        yield 1
                        E("s", lambda s, k=k, tt=tt, pst=pst: s.activation(
                            gqT_sb[:, k * FBL + tt * 128: k * FBL + (tt + 1) * 128],
                            pst.bitcast(BF16)[:, 0:128], AF.Copy),
                          reads=[f"psT{k % 2}"], writes=[f"gqT{k}_{tt}"])
                        yield 1
                yield "gq"
                for dh in range(2):
                    for kg in range(4):
                        sl_ = kg % 2
                        W0 = sl_ * 8 * 512
                        E("q", lambda q, dh=dh, kg=kg, W0=W0: q.dma_start(
                            out=wA_sb[:, W0:W0 + 8 * 512].rearrange("p (k c) -> p k c", k=8),
                            in_=cc_w_out[kg, 2, :, dh * 512:(dh + 1) * 512]
                            .rearrange("(k p) c -> p k c", p=128)),
                          reads=["ccwo"], writes=[f"wA{sl_}"], dma=True, chan=f"qw{sl_}")
                        yield 1
                        for tt in range(NTT):
                            for k8 in range(8):
                                k = kg * 8 + k8
                                E("t", lambda t, k=k, k8=k8, tt=tt, dh=dh, W0=W0:
                                  t.matmul(
                                      psD_[tt][:, :],
                                      gqT_sb[:, k * FBL + tt * 128:
                                             k * FBL + (tt + 1) * 128],
                                      wA_sb[:, W0 + k8 * 512:W0 + (k8 + 1) * 512],
                                      start=(k == 0), stop=(k == 31)),
                                  reads=[f"gqT{k}_{tt}", f"wA{sl_}"],
                                  writes=["ps7" if tt == 0 else "ps5"])
                                yield 1
                    for tt in range(NTT):
                        E("v", lambda v, tt=tt, dh=dh: v.scalar_tensor_tensor(
                            x2_sb[:, tt * D + dh * 512: tt * D + (dh + 1) * 512],
                            psD_[tt][:, :], fst_sb[:, 12 + tt:13 + tt],
                            x2_sb[:, tt * D + dh * 512: tt * D + (dh + 1) * 512],
                            op0=ALU.mult, op1=ALU.add),
                          reads=[("ps7" if tt == 0 else "ps5"), f"sdv{tt}",
                                 f"x2{tt}"],
                          writes=[f"x2{tt}"])
                        yield 1
                        E("q", lambda q, tt=tt, dh=dh, fb0=fb0: q.dma_start(
                            out=out_d[fb0 + tt * 128: fb0 + (tt + 1) * 128,
                                      dh * 512:(dh + 1) * 512],
                            in_=x2_sb[:, tt * D + dh * 512:
                                      tt * D + (dh + 1) * 512]),
                          reads=[f"x2{tt}"], writes=["outdr"], dma=True,
                          chan="qso")
                        yield 1



                yield "end"

        _gens = [ffn_block_emit(bl) for bl in range(NFB)]
        _gpos = [0]

        def drive_ffn(ch, k_yield=12, drain=False):
            while _gpos[0] < NFB:
                bl = _gpos[0]
                if not drain and ch < 2 * bl + 1:
                    return
                g = _gens[bl]
                steps = 0
                for _ in g:
                    steps += 1
                    if not drain and steps >= k_yield:
                        return
                _gpos[0] += 1

        for ch in range(NCH):
            tok0 = ch * Q
            kq = ch // (NCH // 4)
            for tt in range(4):
                ut = u_ch[:, tt * D:(tt + 1) * D]
                E("q", lambda q, tok0=tok0, tt=tt, ut=ut: q.dma_start(
                    out=ut, in_=u_full[tok0 + tt * 128: tok0 + (tt + 1) * 128, :]),
                  writes=[f"u{tt}", f"xn{tt}"]
                  + (["wrow"] if ch == 0 and tt == 0 else []),
                  dma=True, chan=f"qu{tt}")
                E("v", lambda v, tt=tt, ut=ut: v.reduce_sum(
                    st_ch[:, tt:tt + 1], ut, axis=mybir.AxisListType.X),
                  reads=[f"u{tt}"], writes=["stm"])
                E("s", lambda s, tt=tt, ut=ut: s.activation(
                    sq_ch[:, :], ut, AF.Square, accum_out=st_ch[:, 4 + tt:5 + tt]),
                  reads=[f"u{tt}"], writes=["stsq", "sts2"])
            E("v", lambda v: v.tensor_scalar(
                st_ch[:, 0:4], st_ch[:, 0:4], 1.0 / D, None, op0=ALU.mult),
              reads=["stm"], writes=["stm"])
            E("v", lambda v: v.tensor_scalar(
                st_ch[:, 4:8], st_ch[:, 4:8], 1.0 / D, EPS,
                op0=ALU.mult, op1=ALU.add),
              reads=["sts2"], writes=["sts2"])
            E("v", lambda v: v.tensor_tensor(
                st_ch[:, 8:12], st_ch[:, 0:4], st_ch[:, 0:4], op=ALU.mult),
              reads=["stm"], writes=["stm2"])
            E("v", lambda v: v.tensor_tensor(
                st_ch[:, 4:8], st_ch[:, 4:8], st_ch[:, 8:12], op=ALU.subtract),
              reads=["sts2", "stm2"], writes=["str"])
            E("s", lambda s: s.activation(st_ch[:, 4:8], st_ch[:, 4:8], AF.Ln),
              reads=["str"], writes=["str"])
            E("s", lambda s: s.activation(st_ch[:, 4:8], st_ch[:, 4:8], AF.Exp,
                                          scale=-0.5),
              reads=["str"], writes=["str"])
            for tt in range(4):
                ut = u_ch[:, tt * D:(tt + 1) * D]
                E("v", lambda v, tt=tt, ut=ut: v.tensor_scalar(
                    ut, ut, st_ch[:, tt:tt + 1], st_ch[:, 4 + tt:5 + tt],
                    op0=ALU.subtract, op1=ALU.mult),
                  reads=[f"u{tt}", "stm", "str"], writes=[f"u{tt}", f"xn{tt}"])
                E("v", lambda v, tt=tt, ut=ut: v.tensor_copy(
                    xnb_sb[:, :], ut),
                  reads=[f"xn{tt}"], writes=["xnb", "ytok0", "ytok1"])
                for k in range(8):
                    E("s", lambda t, k=k, tt=tt: t.dma_start_transpose(
                        xnT_sb[:, k * Q + tt * 128: k * Q + (tt + 1) * 128],
                        xnb_sb[:, k * 128:(k + 1) * 128]),
                      reads=["xnb"], writes=[f"xnT{k}", "xnTall"], dma=True,
                      chan="qtr")
                for k in range(2):
                    pst = psT[k]
                    E("t", lambda t, ut=ut, k=k, pst=pst: t.transpose(
                        pst[:, 0:128], ut[:, k * 128:(k + 1) * 128],
                        eye_sb[:, :]),
                      reads=[f"xn{tt}", "eye"], writes=[f"psT{k}"])
                    E("s", lambda s, k=k, tt=tt, pst=pst: s.activation(
                        xnsl_sb[:, k * Q + tt * 128: k * Q + (tt + 1) * 128],
                        pst[:, 0:128], AF.Copy),
                      reads=[f"psT{k}"], writes=[f"xnsl{k}"])
            for k in range(8):
                E("t", lambda t, k=k: t.matmul(
                    ps0[0:RK, 0:Q], wxdtT_sb[:, k * RK:(k + 1) * RK],
                    xnT_sb[:, k * Q:(k + 1) * Q],
                    start=(k == 0), stop=(k == 7)),
                  reads=[f"xnT{k}", "xnTall", "wxdtT"], writes=["ps0"])
            E("s", lambda s: s.activation(
                xdt_sb[:, :], ps0[0:RK, 0:Q], AF.Identity, bias=bxdt_sb[:, :]),
              reads=["ps0", "bxdt"], writes=["xdt", "dt1"])
            for k in range(8):
                E("t", lambda t, k=k: t.matmul(
                    ps0[0:NST, 0:Q], wbT_sb[:, k * NST:(k + 1) * NST],
                    xnT_sb[:, k * Q:(k + 1) * Q],
                    start=(k == 0), stop=(k == 7)),
                  reads=[f"xnT{k}", "xnTall", "wbT"], writes=["ps0"])
            E("s", lambda s: s.activation(
                bcq_sb[:, :], ps0[0:NST, 0:Q], AF.Identity, bias=bb_sb[:, :]),
              reads=["ps0", "bbias"], writes=["bcq"])
            E("q", lambda q, tok0=tok0: q.dma_start(
                out=bc_B[:, tok0:tok0 + Q], in_=bcq_sb[:, :]),
              reads=["bcq"], writes=["bcBdr"], dma=True, chan="qsB")
            for k in range(8):
                E("t", lambda t, k=k: t.matmul(
                    ps0[0:NST, 0:Q], wcT_sb[:, k * NST:(k + 1) * NST],
                    xnT_sb[:, k * Q:(k + 1) * Q],
                    start=(k == 0), stop=(k == 7)),
                  reads=[f"xnT{k}", "xnTall", "wcT"], writes=["ps0"])
            E("s", lambda s: s.activation(
                bcq_sb[:, :], ps0[0:NST, 0:Q], AF.Identity, bias=bc_sb[:, :]),
              reads=["ps0", "cbias"], writes=["ccq", "bcq"])
            E("q", lambda q, tok0=tok0: q.dma_start(
                out=bc_C[:, tok0:tok0 + Q], in_=bcq_sb[:, :]),
              reads=["ccq"], writes=["bcCdr"], dma=True, chan="qsC")
            for dti in range(2):
                psd = ps0
                E("t", lambda t, dti=dti, psd=psd: t.matmul(
                    psd[:, 0:Q], wdtT_sb[:, dti * 128:(dti + 1) * 128],
                    xdt_sb[:, :], start=True, stop=True),
                  reads=["xdt", "wdtT"], writes=["ps0"])
                E("s", lambda s, dti=dti, psd=psd: s.activation(
                    edt_sb[:, :], psd[:, 0:Q], AF.Exp,
                    bias=bdt_sb[:, dti:dti + 1]),
                  reads=["ps0", "bdt"], writes=["stsq"])
                E("s", lambda s, dti=dti: s.activation(
                    dt_sb[:, dti * Q:(dti + 1) * Q], edt_sb[:, :], AF.Ln,
                    bias=1.0),
                  reads=["stsq"],
                  writes=[f"dt{dti}"] + (["xdt"] if dti == 1 else []))
                E("g", lambda g, dti=dti: g.tensor_tensor(
                    c_sb[:, dti * Q:(dti + 1) * Q],
                    dt_sb[:, dti * Q:(dti + 1) * Q],
                    xnsl_sb[:, dti * Q:(dti + 1) * Q], op=ALU.mult),
                  reads=[f"dt{dti}", f"xnsl{dti}"], writes=[f"c{dti}"])
            for n in range(NST + 1):
                if n < NST:
                    bsl = n % 2
                    E("q", lambda q, n=n, tok0=tok0, bsl=bsl: q.dma_start(
                        out=bbc_sb[:, bsl * Q:(bsl + 1) * Q],
                        in_=bc_B[n:n + 1, tok0:tok0 + Q].broadcast_to([128, Q])),
                      reads=["bcBdr"], writes=[f"bbc{bsl}"], dma=True,
                      chan=f"qb{bsl}")
                    E("q", lambda q, n=n, tok0=tok0, bsl=bsl: q.dma_start(
                        out=cbc_sb[:, bsl * Q:(bsl + 1) * Q],
                        in_=bc_C[n:n + 1, tok0:tok0 + Q].broadcast_to([128, Q])),
                      reads=["bcCdr"], writes=[f"cbc{bsl}"], dma=True,
                      chan=f"qc{bsl}")
                    for dti in range(2):
                        asl = (n % 2) * 2 + dti
                        E("s", lambda s, n=n, dti=dti, asl=asl: s.activation(
                            abar_sb[:, asl * Q:(asl + 1) * Q],
                            dt_sb[:, dti * Q:(dti + 1) * Q], AF.Exp,
                            scale=acol_sb[:, dti * NST + n:dti * NST + n + 1]),
                          reads=[f"dt{dti}", "acol"], writes=[f"abar{asl}"])
                    for dti in range(2):
                        asl = (n % 2) * 2 + dti
                        E("g", lambda g, dti=dti, asl=asl, bsl=bsl:
                          g.tensor_tensor(
                              in1_sb[:, asl * Q:(asl + 1) * Q],
                              c_sb[:, dti * Q:(dti + 1) * Q],
                              bbc_sb[:, bsl * Q:(bsl + 1) * Q], op=ALU.mult),
                          reads=[f"c{dti}", f"bbc{bsl}"], writes=[f"in1{asl}"])
                if n == 0:
                    continue
                m = n - 1
                msl0 = (m % 2) * 2
                for dti in range(2):
                    hsl = msl0 + dti
                    col = dti * NST + m
                    E("v", lambda v, hsl=hsl, col=col:
                      v.tensor_tensor_scan(
                          h_sb[:, hsl * Q:(hsl + 1) * Q],
                          abar_sb[:, hsl * Q:(hsl + 1) * Q],
                          in1_sb[:, hsl * Q:(hsl + 1) * Q],
                          hstate[:, col:col + 1], op0=ALU.mult, op1=ALU.add),
                      reads=[f"abar{hsl}", f"in1{hsl}", f"hst{col}"],
                      writes=[f"h{hsl}"])
                for dti in range(2):
                    hsl = msl0 + dti
                    col = dti * NST + m
                    E("v", lambda v, hsl=hsl, col=col: v.tensor_copy(
                        hstate[:, col:col + 1],
                        h_sb[:, hsl * Q + Q - 1: hsl * Q + Q]),
                      reads=[f"h{hsl}"], writes=[f"hst{col}"])
                ei = m % 2
                heng = "v" if ei == 0 else "g"
                bslm = m % 2
                for dti in range(2):
                    hsl = msl0 + dti
                    ysl = ei * 2 + dti
                    if m < 2:
                        E(heng, lambda e, hsl=hsl, bslm=bslm, ysl=ysl:
                          e.tensor_tensor(
                              ych_sb[:, ysl * Q:(ysl + 1) * Q],
                              h_sb[:, hsl * Q:(hsl + 1) * Q],
                              cbc_sb[:, bslm * Q:(bslm + 1) * Q], op=ALU.mult),
                          reads=[f"h{hsl}", f"cbc{bslm}"], writes=[f"ych{ysl}"])
                    else:
                        E(heng, lambda e, hsl=hsl, bslm=bslm, hcs=ei:
                          e.tensor_tensor(
                              hc_sb[:, hcs * Q:(hcs + 1) * Q],
                              h_sb[:, hsl * Q:(hsl + 1) * Q],
                              cbc_sb[:, bslm * Q:(bslm + 1) * Q], op=ALU.mult),
                          reads=[f"h{hsl}", f"cbc{bslm}"], writes=[f"hc{ei}"])
                        E(heng, lambda e, ysl=ysl, hsl=ei: e.tensor_tensor(
                            ych_sb[:, ysl * Q:(ysl + 1) * Q],
                            ych_sb[:, ysl * Q:(ysl + 1) * Q],
                            hc_sb[:, hsl * Q:(hsl + 1) * Q],
                            op=ALU.add),
                          reads=[f"hc{ei}", f"ych{ysl}"], writes=[f"ych{ysl}"])
            for dti in range(2):
                E("v", lambda v, dti=dti: v.tensor_tensor(
                    ych_sb[:, dti * Q:(dti + 1) * Q],
                    ych_sb[:, dti * Q:(dti + 1) * Q],
                    ych_sb[:, (2 + dti) * Q:(3 + dti) * Q], op=ALU.add),
                  reads=[f"ych{dti}", f"ych{2 + dti}"], writes=[f"ych{dti}"])
                E("v", lambda v, dti=dti: v.scalar_tensor_tensor(
                    ych_sb[:, dti * Q:(dti + 1) * Q],
                    xnsl_sb[:, dti * Q:(dti + 1) * Q],
                    dcol_sb[:, dti:dti + 1],
                    ych_sb[:, dti * Q:(dti + 1) * Q],
                    op0=ALU.mult, op1=ALU.add),
                  reads=[f"xnsl{dti}", "dcol", f"ych{dti}"],
                  writes=[f"ych{dti}"])
            for tt in range(4):
                for dti in range(2):
                    pst = psT[dti]
                    E("t", lambda t, dti=dti, tt=tt, pst=pst: t.transpose(
                        pst[:, 0:128],
                        ych_sb[:, dti * Q + tt * 128: dti * Q + (tt + 1) * 128],
                        eye_sb[:, :]),
                      reads=[f"ych{dti}", "eye"], writes=[f"psT{dti}"])
                    E("s", lambda s, dti=dti, tt=tt, pst=pst: s.activation(
                        ytok_sb[:, (tt % 2) * DSL + dti * 128:
                                (tt % 2) * DSL + (dti + 1) * 128],
                        pst[:, 0:128], AF.Copy),
                      reads=[f"psT{dti}"], writes=[f"ytok{tt % 2}", "xnb"])
                E("q", lambda q, tt=tt, ch=ch: q.dma_start(
                    out=ag_in[ch][tt * 128:(tt + 1) * 128, :],
                    in_=ytok_sb[:, (tt % 2) * DSL:(tt % 2 + 1) * DSL]),
                  reads=[f"ytok{tt % 2}"], writes=[f"ccy{ch}"], dma=True,
                  chan=f"qsy{ch}")
            E("g", lambda g, ch=ch: g.collective_compute(
                "AllGather", ALU.bypass, replica_groups=GRPS,
                ins=[ag_in[ch].ap().opt()],
                outs=[ag_out[ch].ap().opt()]),
              reads=[f"ccy{ch}"], writes=[f"ag{ch}"])
            drive_ffn(ch)

        drive_ffn(NCH, drain=True)

        L.barrier()

        def play(engine, name):
            if name == "q":
                pid = engine.partition_id()
                PID[0] = pid % 4
            for item in L.queue[name]:
                if item[0] == "wait":
                    engine.wait_ge(sems[item[1]], item[2])
                else:
                    _, fn, step, sem_name = item
                    fn(engine).then_inc(sems[sem_name], step)

        @block.vector
        def _(v):
            play(v, "v")

        @block.scalar
        def _(s):
            play(s, "s")

        @block.tensor
        def _(t):
            play(t, "t")

        @block.gpsimd
        def _(g):
            play(g, "g")

        @block.sync
        def _(q):
            play(q, "q")

    return nc


_CACHE = {}


def kernel(L0_attn_u, L0_attn_A_log, L0_attn_W_xdt, L0_attn_b_xdt,
           L0_attn_W_dt, L0_attn_b_dt, L0_attn_W_B, L0_attn_b_B,
           L0_attn_W_C, L0_attn_b_C, L0_attn_D, L0_ffn_w_gate,
           L0_ffn_w_up, L0_ffn_w_down):
    u = np.ascontiguousarray(np.asarray(L0_attn_u, dtype=np.float32))
    eye = np.eye(128, dtype=np.float32)
    in_maps = []
    for c in range(8):
        b, q = c // 4, c % 4
        perm = np.r_[np.arange(q * DSL, (q + 1) * DSL),
                     np.arange(0, q * DSL), np.arange((q + 1) * DSL, D)]
        im = {
            "u_full": np.ascontiguousarray(u[b][:, perm]),
            "u_tok": np.ascontiguousarray(
                u[b].reshape(NCH, 4, 128, D)[:, q].reshape(TQ, D)),
            "wxdtT": np.ascontiguousarray(
                np.asarray(L0_attn_W_xdt, np.float32).T[perm, :]),
            "bxdt": np.asarray(L0_attn_b_xdt, np.float32).reshape(RK, 1),
            "wdtT_sl": np.ascontiguousarray(
                np.asarray(L0_attn_W_dt, np.float32)[q * DSL:(q + 1) * DSL, :].T),
            "bdt_sl": np.ascontiguousarray(
                np.asarray(L0_attn_b_dt, np.float32)[q * DSL:(q + 1) * DSL]
                .reshape(DSL, 1)),
            "wbT": np.ascontiguousarray(
                np.asarray(L0_attn_W_B, np.float32).T[perm, :]),
            "bb": np.asarray(L0_attn_b_B, np.float32).reshape(NST, 1),
            "wcT": np.ascontiguousarray(
                np.asarray(L0_attn_W_C, np.float32).T[perm, :]),
            "bc": np.asarray(L0_attn_b_C, np.float32).reshape(NST, 1),
            "alog_sl": np.ascontiguousarray(
                np.asarray(L0_attn_A_log, np.float32)[q * DSL:(q + 1) * DSL, :]),
            "dvec_sl": np.ascontiguousarray(
                np.asarray(L0_attn_D, np.float32)[q * DSL:(q + 1) * DSL]
                .reshape(DSL, 1)),
            "eye": eye,
            "wgT_q": np.ascontiguousarray(
                np.asarray(L0_ffn_w_gate, np.float32).T[:, q * 1024:(q + 1) * 1024]),
            "wuT_q": np.ascontiguousarray(
                np.asarray(L0_ffn_w_up, np.float32).T[:, q * 1024:(q + 1) * 1024]),
            "wdT_q": np.ascontiguousarray(
                np.asarray(L0_ffn_w_down, np.float32).T[q * 1024:(q + 1) * 1024, :]),
        }
        in_maps.append(im)
    if "nc" not in _CACHE:
        _CACHE["nc"] = build_program()
    nc = _CACHE["nc"]
    res = run_bass_kernel_spmd(nc, in_maps, core_ids=list(range(8)))
    _CACHE["res"] = res
    out = np.empty((B, T, D), dtype=np.float32)
    ov = out.reshape(B, NCH, 4, 128, D)
    for c in range(8):
        b, q = c // 4, c % 4
        ov[b, :, q] = res.results[c]["out"].reshape(NCH, 128, D)
    return out


# version 09
# speedup vs baseline: 1.2179x; 1.1239x over previous
import numpy as np
from contextlib import ExitStack
import concourse.bass as bass
import concourse.mybir as mybir
from concourse.bass_utils import run_bass_kernel_spmd

F32 = mybir.dt.float32
BF16 = mybir.dt.bfloat16
AF = mybir.ActivationFunctionType
ALU = mybir.AluOpType

B, T, D, NST, RK, INNER = 2, 8192, 1024, 16, 64, 4096
DSL = 256
Q = 512
NCH = T // Q
TQ = T // 4
FBL = 256
NFB = TQ // FBL
NTT = FBL // 128
MAGIC = float(np.float32(1.5 * 2 ** 23))
EPS = 1e-5
NW = float(INNER * D)
GRPS = [[0, 1, 2, 3], [4, 5, 6, 7]]


class Ledger:

    def __init__(self, sems, engines):
        self.sems = sems
        self.engines = engines
        self.cnt = {e: 0 for e in sems}
        self.mark = {e: {} for e in engines}
        self.buf = {}
        self.queue = {e: [] for e in engines}

    def emit(self, e, fn, reads=(), writes=(), dma=False, chan=None):
        sem_name = chan if (dma and chan) else e
        assert sem_name in self.sems, sem_name
        need = {}
        for k in reads:
            b = self.buf.get(k)
            if b and b["w"] is not None:
                se, sv = b["w"]
                need[se] = max(need.get(se, -1), sv)
        for k in writes:
            b = self.buf.get(k)
            if b:
                if b["w"] is not None:
                    se, sv = b["w"]
                    need[se] = max(need.get(se, -1), sv)
                for se, sv in b["r"].items():
                    need[se] = max(need.get(se, -1), sv)
        for se in sorted(need):
            sv = need[se]
            if se == e == "t":
                continue
            if se not in self.engines:
                sv = self.cnt.get(se, sv)
            if self.mark[e].get(se, -1) < sv:
                self.queue[e].append(("wait", se, sv))
                self.mark[e][se] = sv
        step = 16 if dma else 1
        self.cnt[sem_name] = self.cnt.get(sem_name, 0) + step
        v = self.cnt[sem_name]
        self.queue[e].append(("op", fn, step, sem_name))
        for k in reads:
            b = self.buf.setdefault(k, {"w": None, "r": {}})
            b["r"][sem_name] = max(b["r"].get(sem_name, -1), v)
        for k in writes:
            self.buf[k] = {"w": (sem_name, v), "r": {}}

    def barrier(self):
        tops = dict(self.cnt)
        for e in self.queue:
            for se, sv in tops.items():
                if se != e and sv > 0 and self.mark[e].get(se, -1) < sv:
                    self.queue[e].append(("wait", se, sv))
                    self.mark[e][se] = sv


def build_program():
    nc = bass.Bass("TRN2", target_bir_lowering=False)

    u_full = nc.dram_tensor("u_full", [T, D], F32, kind="ExternalInput").ap()
    u_tok = nc.dram_tensor("u_tok", [TQ, D], F32, kind="ExternalInput").ap()
    wxdtT_d = nc.dram_tensor("wxdtT", [D, RK], F32, kind="ExternalInput").ap()
    bxdt_d = nc.dram_tensor("bxdt", [RK, 1], F32, kind="ExternalInput").ap()
    wdtT_d = nc.dram_tensor("wdtT_sl", [RK, DSL], F32, kind="ExternalInput").ap()
    bdt_d = nc.dram_tensor("bdt_sl", [DSL, 1], F32, kind="ExternalInput").ap()
    wbT_d = nc.dram_tensor("wbT", [D, NST], F32, kind="ExternalInput").ap()
    bb_d = nc.dram_tensor("bb", [NST, 1], F32, kind="ExternalInput").ap()
    wcT_d = nc.dram_tensor("wcT", [D, NST], F32, kind="ExternalInput").ap()
    bc_d = nc.dram_tensor("bc", [NST, 1], F32, kind="ExternalInput").ap()
    alog_d = nc.dram_tensor("alog_sl", [DSL, NST], F32, kind="ExternalInput").ap()
    dvec_d = nc.dram_tensor("dvec_sl", [DSL, 1], F32, kind="ExternalInput").ap()
    eye_d = nc.dram_tensor("eye", [128, 128], F32, kind="ExternalInput").ap()
    wgT_d = nc.dram_tensor("wgT_q", [D, 1024], F32, kind="ExternalInput").ap()
    wuT_d = nc.dram_tensor("wuT_q", [D, 1024], F32, kind="ExternalInput").ap()
    wdT_d = nc.dram_tensor("wdT_q", [1024, D], F32, kind="ExternalInput").ap()
    out_d = nc.dram_tensor("out", [TQ, D], F32, kind="ExternalOutput").ap()
    bc_B = nc.dram_tensor("bc_B", [NST, T], F32).ap()
    bc_C = nc.dram_tensor("bc_C", [NST, T], F32).ap()
    scal_dr = nc.dram_tensor("scal_dr", [1, 8], F32).ap()
    ag_in = [nc.dram_tensor(f"agi{c}", [Q, DSL], F32) for c in range(NCH)]
    ag_out = [nc.dram_tensor(f"ago{c}", [4, Q, DSL], F32) for c in range(NCH)]
    cc_w_in = nc.dram_tensor("ccwi", [3, 1024, 1024], BF16)
    cc_w_out = nc.dram_tensor("ccwo", [4, 3, 1024, 1024], BF16)
    cc_s_in = nc.dram_tensor("ccsi", [1, 4], F32)
    cc_s_out = nc.dram_tensor("ccso", [1, 4], F32)

    _st = ExitStack()
    with _st:
        eye_sb = _st.enter_context(nc.sbuf_tensor("eye_sb", [128, 128], F32))
        eyeb_sb = _st.enter_context(nc.sbuf_tensor("eyeb_sb", [128, 128], BF16))
        wxdtT_sb = _st.enter_context(nc.sbuf_tensor("wxdtT_sb", [128, 8 * RK], BF16))
        wbT_sb = _st.enter_context(nc.sbuf_tensor("wbT_sb", [128, 8 * NST], BF16))
        wcT_sb = _st.enter_context(nc.sbuf_tensor("wcT_sb", [128, 8 * NST], BF16))
        wdtT_sb = _st.enter_context(nc.sbuf_tensor("wdtT_sb", [RK, DSL], BF16))
        bxdt_sb = _st.enter_context(nc.sbuf_tensor("bxdt_sb", [RK, 1], F32))
        bdt_sb = _st.enter_context(nc.sbuf_tensor("bdt_sb", [128, 2], F32))
        dcol_sb = _st.enter_context(nc.sbuf_tensor("dcol_sb", [128, 2], F32))
        bb_sb = _st.enter_context(nc.sbuf_tensor("bb_sb", [NST, 1], F32))
        bc_sb = _st.enter_context(nc.sbuf_tensor("bc_sb", [NST, 1], F32))
        acol_sb = _st.enter_context(nc.sbuf_tensor("acol_sb", [128, 2 * NST], F32))
        hstate = _st.enter_context(nc.sbuf_tensor("hstate", [128, 2 * NST], F32))
        wsum_sb = _st.enter_context(nc.sbuf_tensor("wsum_sb", [128, 4], F32))
        wsc_sb = _st.enter_context(nc.sbuf_tensor("wsc_sb", [1, 8], F32))
        wscb_sb = _st.enter_context(nc.sbuf_tensor("wscb_sb", [128, 8], F32))
        u_ch = _st.enter_context(nc.sbuf_tensor("u_ch", [128, 4 * D], F32))
        ygq_sb = _st.enter_context(nc.sbuf_tensor("ygq_sb", [128, 2 * D], F32))
        ufq_sb = _st.enter_context(nc.sbuf_tensor("ufq_sb", [128, 2 * D], F32))
        fscr_sb = _st.enter_context(nc.sbuf_tensor("fscr_sb", [128, D], F32))
        st_ch = _st.enter_context(nc.sbuf_tensor("st_ch", [128, 12], F32))
        xnT_sb = _st.enter_context(nc.sbuf_tensor("xnT_sb", [128, 8 * Q], BF16))
        xnsl_sb = _st.enter_context(nc.sbuf_tensor("xnsl_sb", [128, 4 * Q], BF16))
        dt_sb = _st.enter_context(nc.sbuf_tensor("dt_sb", [128, 4 * Q], BF16))
        xdt_sb = _st.enter_context(nc.sbuf_tensor("xdt_sb", [RK, Q], BF16))
        bcq_sb = _st.enter_context(nc.sbuf_tensor("bcq_sb", [NST, Q], F32))
        c_sb = _st.enter_context(nc.sbuf_tensor("c_sb", [128, 4 * Q], BF16))
        bbc_sb = _st.enter_context(nc.sbuf_tensor("bbc_sb", [128, 2 * Q], F32))
        cbc_sb = _st.enter_context(nc.sbuf_tensor("cbc_sb", [128, 2 * Q], F32))
        abar_sb = _st.enter_context(nc.sbuf_tensor("abar_sb", [128, 4 * Q], F32))
        in1_sb = _st.enter_context(nc.sbuf_tensor("in1_sb", [128, 4 * Q], F32))
        h_sb = _st.enter_context(nc.sbuf_tensor("h_sb", [128, 4 * Q], F32))
        hc_sb = _st.enter_context(nc.sbuf_tensor("hc_sb", [128, 2 * Q], F32))
        ych_sb = _st.enter_context(nc.sbuf_tensor("ych_sb", [128, 4 * Q], F32))
        ytok_sb = _st.enter_context(nc.sbuf_tensor("ytok_sb", [128, 2 * DSL], F32))
        x2_sb = _st.enter_context(nc.sbuf_tensor("x2_sb", [128, NTT * D], F32))
        fst_sb = _st.enter_context(nc.sbuf_tensor("fst_sb", [128, 16], F32))
        xq_sb = _st.enter_context(nc.sbuf_tensor("xq_sb", [128, NTT * D], BF16))
        xqT_sb = _st.enter_context(nc.sbuf_tensor("xqT_sb", [128, 8 * FBL], BF16))
        wA_sb = _st.enter_context(nc.sbuf_tensor("wA_sb", [128, 2 * 8 * 512], BF16))
        wB_sb = _st.enter_context(nc.sbuf_tensor("wB_sb", [128, 2 * 8 * 512], BF16))
        gate_sb = _st.enter_context(nc.sbuf_tensor("gate_sb", [128, 512], F32))
        arena_a = _st.enter_context(nc.sbuf_tensor("arena_a", [128, 8192], F32))
        arena_b = _st.enter_context(nc.sbuf_tensor("arena_b", [128, 4096], F32))
        g_sb = arena_a.ap()
        wq_ld = arena_a.ap()
        gq_sb = arena_a.ap().bitcast(BF16)[:, 0:NTT * INNER]
        gqT_sb = arena_b.ap().bitcast(BF16)[:, 0:NTT * INNER]
        wq_r = arena_a.ap()
        wq_b = wA_sb.ap()[:, 0:4096]
        yg_sb = ygq_sb.ap()
        uf_sb = ufq_sb.ap()
        sq_ch = xnT_sb.ap().bitcast(F32)[:, 0:D]
        edt_sb = xnT_sb.ap().bitcast(F32)[:, 0:Q]
        xnb_sb = ytok_sb.ap().bitcast(BF16)

        fsq_sb = fscr_sb.ap()
        fdu_sb = fscr_sb.ap()
        wrow_sb = u_ch.ap()[0:1, 0:128]
        ps0 = _st.enter_context(nc.psum_tensor("ps0", [128, 512], F32))
        ps1 = _st.enter_context(nc.psum_tensor("ps1", [128, 512], F32))
        ps2 = _st.enter_context(nc.psum_tensor("ps2", [128, 512], F32))
        ps3 = _st.enter_context(nc.psum_tensor("ps3", [128, 512], F32))
        ps4 = _st.enter_context(nc.psum_tensor("ps4", [128, 512], F32))
        ps5 = _st.enter_context(nc.psum_tensor("ps5", [128, 512], F32))
        ps6 = _st.enter_context(nc.psum_tensor("ps6", [128, 512], F32))
        ps7 = _st.enter_context(nc.psum_tensor("ps7", [128, 512], F32))
        sem_v = _st.enter_context(nc.semaphore("sem_v"))
        sem_s = _st.enter_context(nc.semaphore("sem_s"))
        sem_t = _st.enter_context(nc.semaphore("sem_t"))
        sem_g = _st.enter_context(nc.semaphore("sem_g"))
        sem_q = _st.enter_context(nc.semaphore("sem_q"))
        block = _st.enter_context(nc.Block())
        sems = {"v": sem_v, "s": sem_s, "t": sem_t, "g": sem_g, "q": sem_q}
        CHANS = ["qm", "qwl", "qwr", "qss", "qsc", "qsw", "qu0", "qu1",
                 "qsB", "qsC", "qb0", "qb1", "qc0", "qc1",
                 "qsy0", "qsy1", "qsy2", "qsy3", "qsy4", "qsy5", "qsy6", "qsy7", "qsy8", "qsy9", "qsy10", "qsy11", "qsy12", "qsy13", "qsy14", "qsy15", "qdbg", "qd", "qu2", "qu3", "qtr",
                 "qy0", "qy1", "qf0", "qf1", "qw0", "qw1", "qw20", "qw21", "qso"]
        for cn in CHANS:
            sems[cn] = _st.enter_context(nc.semaphore("sem_" + cn))
        L = Ledger(sems, ["v", "s", "t", "g", "q"])
        E = L.emit
        PID = [None]

        E("q", lambda q: q.dma_start(out=eye_sb[:, :], in_=eye_d[:, :]),
          writes=["eye"], dma=True, chan="qm")
        E("v", lambda v: v.tensor_copy(eyeb_sb[:, :], eye_sb[:, :]),
          reads=["eye"], writes=["eyeb"])
        for nm, dst, src, nk, w in [
            ("wxdtT", wxdtT_sb, wxdtT_d, 8, RK),
            ("wbT", wbT_sb, wbT_d, 8, NST),
            ("wcT", wcT_sb, wcT_d, 8, NST),
        ]:
            E("q", lambda q, src=src, nk=nk, w=w: q.dma_start(
                out=wq_ld[:, 0:nk * w].rearrange("p (k r) -> p k r", k=nk),
                in_=src.rearrange("(k p) r -> p k r", p=128)),
              writes=["wq_ld"], dma=True, chan="qwl")
            E("v", lambda v, dst=dst, nk=nk, w=w: v.tensor_copy(
                dst[:, :], wq_ld[:, 0:nk * w]),
              reads=["wq_ld"], writes=[nm])
        E("q", lambda q: q.dma_start(out=wq_ld[0:RK, 0:DSL], in_=wdtT_d[:, :]),
          writes=["wq_ld"], dma=True, chan="qwl")
        E("v", lambda v: v.tensor_copy(wdtT_sb[:, :], wq_ld[0:RK, 0:DSL]),
          reads=["wq_ld"], writes=["wdtT"])
        E("q", lambda q: q.dma_start(out=bxdt_sb[:, :], in_=bxdt_d[:, :]),
          writes=["bxdt"], dma=True, chan="qm")
        E("q", lambda q: q.dma_start(
            out=bdt_sb[:, :].rearrange("p (d c) -> p d c", d=2),
            in_=bdt_d.rearrange("(d p) c -> p d c", p=128),
            allow_slow_non_contiguous=True),
          writes=["bdt"], dma=True, chan="qm")
        E("q", lambda q: q.dma_start(
            out=dcol_sb[:, :].rearrange("p (d c) -> p d c", d=2),
            in_=dvec_d.rearrange("(d p) c -> p d c", p=128),
            allow_slow_non_contiguous=True),
          writes=["dcol"], dma=True, chan="qm")
        E("q", lambda q: q.dma_start(out=bb_sb[:, :], in_=bb_d[:, :]),
          writes=["bbias"], dma=True, chan="qm")
        E("q", lambda q: q.dma_start(out=bc_sb[:, :], in_=bc_d[:, :]),
          writes=["cbias"], dma=True, chan="qm")
        E("q", lambda q: q.dma_start(
            out=acol_sb[:, :].rearrange("p (d n) -> p d n", d=2),
            in_=alog_d.rearrange("(d p) n -> p d n", p=128)),
          writes=["acol"], dma=True, chan="qm")
        E("s", lambda s: s.activation(acol_sb[:, :], acol_sb[:, :], AF.Exp),
          reads=["acol"], writes=["acol"])
        E("v", lambda v: v.tensor_scalar_mul(acol_sb[:, :], acol_sb[:, :], -1.0),
          reads=["acol"], writes=["acol"])
        E("v", lambda v: v.memset(hstate[:, :], 0.0),
          writes=[f"hst{i}" for i in range(2 * NST)])

        wsrc = [wgT_d, wuT_d, wdT_d]
        for m in range(3):
            E("q", lambda q, m=m: q.dma_start(
                out=wq_ld[:, :].rearrange("p (k c) -> p k c", k=8),
                in_=wsrc[m].rearrange("(k p) c -> p k c", p=128)),
              writes=["wq_ld"], dma=True, chan="qwl")
            E("s", lambda s, m=m: s.activation(
                wq_r[:, :], wq_ld[:, :], AF.Abs,
                accum_out=wsum_sb[:, m:m + 1]),
              reads=["wq_ld"], writes=["wq_r", f"wsum{m}"])
        for m in range(3):
            E("q", lambda q, m=m: q.dma_start(
                out=wrow_sb[:, :], in_=wsum_sb[:, m:m + 1],
                allow_slow_non_contiguous=True),
              reads=[f"wsum{m}"], writes=["wrow"], dma=True, chan="qwr")
            E("v", lambda v, m=m: v.reduce_sum(
                wsc_sb[0:1, m:m + 1], wrow_sb[:, :], axis=mybir.AxisListType.X),
              reads=["wrow"], writes=["wsc"])
        E("q", lambda q: q.dma_start(out=cc_s_in[0:1, :], in_=wsc_sb[0:1, 0:4]),
          reads=["wsc"], writes=["ccsi"], dma=True, chan="qss")
        E("g", lambda g: g.collective_compute(
            "AllReduce", ALU.add, replica_groups=GRPS,
            ins=[cc_s_in.ap().opt()], outs=[cc_s_out.ap().opt()]),
          reads=["ccsi"], writes=["ccso"])
        E("q", lambda q: q.dma_start(out=wsc_sb[0:1, 0:4], in_=cc_s_out[0:1, :]),
          reads=["ccso"], writes=["wsc"], dma=True, chan="qss")
        E("v", lambda v: v.tensor_scalar(
            wsc_sb[0:1, 4:8], wsc_sb[0:1, 0:4], 1.0 / NW, 1e-5,
            op0=ALU.mult, op1=ALU.max),
          reads=["wsc"], writes=["wsc"])
        E("v", lambda v: v.reciprocal(wsc_sb[0:1, 0:4], wsc_sb[0:1, 4:8]),
          reads=["wsc"], writes=["wsc"])
        E("q", lambda q: q.dma_start(out=scal_dr[0:1, :], in_=wsc_sb[0:1, :]),
          reads=["wsc"], writes=["scal"], dma=True, chan="qsc")
        E("q", lambda q: q.dma_start(out=wscb_sb[:, :],
                                     in_=scal_dr.broadcast_to([128, 8])),
          reads=["scal"], writes=["wscb"], dma=True, chan="qsc")
        for m in range(3):
            E("q", lambda q, m=m: q.dma_start(
                out=wq_ld[:, :].rearrange("p (k c) -> p k c", k=8),
                in_=wsrc[m].rearrange("(k p) c -> p k c", p=128)),
              writes=["wq_ld"], dma=True, chan="qwl")
            E("v", lambda v, m=m: v.tensor_scalar(
                wq_ld[:, :], wq_ld[:, :], wscb_sb[:, m:m + 1], MAGIC,
                op0=ALU.mult, op1=ALU.add),
              reads=["wq_ld", "wscb"], writes=["wq_ld"])
            E("v", lambda v: v.tensor_scalar(
                wq_ld[:, :], wq_ld[:, :], MAGIC, None, op0=ALU.subtract),
              reads=["wq_ld"], writes=["wq_ld"])
            for hf in range(2):
                E("v", lambda v, hf=hf: v.tensor_scalar(
                    wq_b[:, 0:4096], wq_ld[:, hf * 4096:(hf + 1) * 4096],
                    1.0, -1.0, op0=ALU.min, op1=ALU.max),
                  reads=["wq_ld"], writes=["wq_b"])
                E("q", lambda q, m=m, hf=hf: q.dma_start(
                    out=cc_w_in[m, :, :].rearrange("(k p) c -> p k c", p=128)
                    [:, hf * 4:(hf + 1) * 4, :],
                    in_=wq_b[:, 0:4096].rearrange("p (k c) -> p k c", k=4)),
                  reads=["wq_b"], writes=["ccwi"], dma=True, chan="qsw")
        E("g", lambda g: g.collective_compute(
            "AllGather", ALU.bypass, replica_groups=GRPS,
            ins=[cc_w_in.ap().opt()], outs=[cc_w_out.ap().opt()]),
          reads=["ccwi"], writes=["ccwo"])

        psT = [ps3, ps4]

        def ln2_apply(src_key, src_tile, st, dst_key, dst_tile, scr):
            E("v", lambda v: v.reduce_sum(
                st[:, 0:1], src_tile, axis=mybir.AxisListType.X),
              reads=[src_key], writes=["fm"])
            E("s", lambda s: s.activation(
                scr[:, :], src_tile, AF.Square, accum_out=st[:, 2:3]),
              reads=[src_key], writes=["fdu", "fs2"])
            E("v", lambda v: v.tensor_scalar(
                st[:, 0:1], st[:, 0:1], 1.0 / D, None, op0=ALU.mult),
              reads=["fm"], writes=["fm"])
            E("v", lambda v: v.tensor_scalar(
                st[:, 2:3], st[:, 2:3], 1.0 / D, EPS, op0=ALU.mult, op1=ALU.add),
              reads=["fs2"], writes=["fs2"])
            E("v", lambda v: v.tensor_tensor(
                st[:, 3:4], st[:, 0:1], st[:, 0:1], op=ALU.mult),
              reads=["fm"], writes=["fm2"])
            E("v", lambda v: v.tensor_tensor(
                st[:, 1:2], st[:, 2:3], st[:, 3:4], op=ALU.subtract),
              reads=["fs2", "fm2"], writes=["fr"])
            E("s", lambda s: s.activation(st[:, 1:2], st[:, 1:2], AF.Ln),
              reads=["fr"], writes=["fr"])
            E("s", lambda s: s.activation(st[:, 1:2], st[:, 1:2], AF.Exp,
                                          scale=-0.5),
              reads=["fr"], writes=["fr"])
            E("v", lambda v: v.tensor_scalar(
                dst_tile, src_tile, st[:, 0:1], st[:, 1:2],
                op0=ALU.subtract, op1=ALU.mult),
              reads=[src_key, "fm", "fr"], writes=[dst_key])

        psG = [ps1, ps5]
        psU = [ps2, ps6]
        psD_ = [ps7, ps5]

        def ffn_block_emit(bl):
                fb0 = bl * FBL
                for tt in range(NTT):
                    r0 = fb0 + tt * 128
                    ych_ = 2 * bl + tt
                    E("q", lambda q, tt=tt, ych_=ych_: q.dma_start(
                        out=yg_sb[:, tt * D:(tt + 1) * D]
                        .rearrange("p (j c) -> p j c", j=4),
                        in_=ag_out[ych_].ap()[:, bass.ts(PID[0], 128), :]
                        .transpose([1, 0, 2])),
                      reads=[f"ag{ych_}"], writes=[f"yg{tt}"], dma=True,
                      chan=f"qy{tt}")
                    yield 1
                    E("q", lambda q, r0=r0, tt=tt: q.dma_start(
                        out=uf_sb[:, tt * D:(tt + 1) * D],
                        in_=u_tok[r0:r0 + 128, :]),
                      writes=[f"uf{tt}"], dma=True, chan=f"qf{tt}")
                    yield 1
                    uft = uf_sb[:, tt * D:(tt + 1) * D]
                    E("v", lambda v, tt=tt, uft=uft: v.tensor_tensor(
                        x2_sb[:, tt * D:(tt + 1) * D], uft,
                        yg_sb[:, tt * D:(tt + 1) * D], op=ALU.add),
                      reads=[f"uf{tt}", f"yg{tt}"], writes=[f"x2{tt}"])
                    yield 1
                    ln2_apply(f"x2{tt}", x2_sb[:, tt * D:(tt + 1) * D], fst_sb,
                              "fsq", fsq_sb[:, :], fdu_sb)
                    E("v", lambda v: v.reduce_max(
                        fst_sb[:, 4:5], fsq_sb[:, :], axis=mybir.AxisListType.X,
                        apply_absolute_value=True),
                      reads=["fsq"], writes=["fmax"])
                    yield 1
                    E("v", lambda v: v.tensor_scalar_max(
                        fst_sb[:, 4:5], fst_sb[:, 4:5], 1e-5),
                      reads=["fmax"], writes=["fmax"])
                    yield 1
                    E("v", lambda v: v.reciprocal(fst_sb[:, 5:6], fst_sb[:, 4:5]),
                      reads=["fmax"], writes=["fsx"])
                    yield 1
                    E("v", lambda v: v.tensor_scalar(
                        fst_sb[:, 5:6], fst_sb[:, 5:6], 127.0, None, op0=ALU.mult),
                      reads=["fsx"], writes=["fsx"])
                    yield 1
                    E("v", lambda v, tt=tt: v.tensor_scalar(
                        fst_sb[:, 6 + tt:7 + tt], fst_sb[:, 4:5], 1.0 / 127.0, None,
                        op0=ALU.mult),
                      reads=["fmax"], writes=[f"fisx{tt}"])
                    yield 1
                    E("v", lambda v: v.tensor_scalar(
                        fdu_sb[:, :], fsq_sb[:, :], fst_sb[:, 5:6], MAGIC,
                        op0=ALU.mult, op1=ALU.add),
                      reads=["fsq", "fsx"], writes=["fdu"])
                    yield 1
                    E("v", lambda v, tt=tt: v.tensor_scalar(
                        xq_sb[:, tt * D:(tt + 1) * D], fdu_sb[:, :], MAGIC, None,
                        op0=ALU.subtract),
                      reads=["fdu"], writes=[f"xq{tt}"])
                    yield 1
                    for k in range(8):
                        pst = psT[k % 2]
                        E("t", lambda t, k=k, tt=tt, pst=pst: t.transpose(
                            pst.bitcast(BF16)[:, 0:128],
                            xq_sb[:, tt * D + k * 128: tt * D + (k + 1) * 128],
                            eyeb_sb[:, :]),
                          reads=[f"xq{tt}", "eyeb"], writes=[f"psT{k % 2}"])
                        yield 1
                        E("s", lambda s, k=k, tt=tt, pst=pst: s.activation(
                            xqT_sb[:, k * FBL + tt * 128: k * FBL + (tt + 1) * 128],
                            pst.bitcast(BF16)[:, 0:128], AF.Copy),
                          reads=[f"psT{k % 2}"], writes=[f"xqT{k}"])
                        yield 1
                    E("v", lambda v, tt=tt: v.tensor_tensor(
                        fst_sb[:, 8 + tt:9 + tt], fst_sb[:, 6 + tt:7 + tt],
                        wscb_sb[:, 4:5], op=ALU.mult),
                      reads=[f"fisx{tt}", "wscb"], writes=[f"sgv{tt}"])
                    yield 1
                    E("v", lambda v, tt=tt: v.tensor_tensor(
                        fst_sb[:, 10 + tt:11 + tt], fst_sb[:, 6 + tt:7 + tt],
                        wscb_sb[:, 5:6], op=ALU.mult),
                      reads=[f"fisx{tt}", "wscb"], writes=[f"suv{tt}"])
                    yield 1
                for ic in range(8):
                    sl_ = ic % 2
                    W0 = sl_ * 8 * 512
                    E("q", lambda q, ic=ic, W0=W0: q.dma_start(
                        out=wA_sb[:, W0:W0 + 8 * 512].rearrange("p (k c) -> p k c", k=8),
                        in_=cc_w_out[ic // 2, 0, :, (ic % 2) * 512:(ic % 2 + 1) * 512]
                        .rearrange("(k p) c -> p k c", p=128)),
                      reads=["ccwo"], writes=[f"wA{sl_}"], dma=True, chan=f"qw{sl_}")
                    yield 1
                    E("q", lambda q, ic=ic, W0=W0: q.dma_start(
                        out=wB_sb[:, W0:W0 + 8 * 512].rearrange("p (k c) -> p k c", k=8),
                        in_=cc_w_out[ic // 2, 1, :, (ic % 2) * 512:(ic % 2 + 1) * 512]
                        .rearrange("(k p) c -> p k c", p=128)),
                      reads=["ccwo"], writes=[f"wB{sl_}"], dma=True, chan=f"qw2{sl_}")
                    yield 1
                    for tt in range(NTT):
                        for k in range(8):
                            E("t", lambda t, k=k, tt=tt, W0=W0: t.matmul(
                                psG[tt][:, :],
                                xqT_sb[:, k * FBL + tt * 128:
                                       k * FBL + (tt + 1) * 128],
                                wA_sb[:, W0 + k * 512:W0 + (k + 1) * 512],
                                start=(k == 0), stop=(k == 7)),
                              reads=[f"xqT{k}", f"wA{sl_}"], writes=["ps1" if tt == 0 else "ps5"])
                            yield 1
                        for k in range(8):
                            E("t", lambda t, k=k, tt=tt, W0=W0: t.matmul(
                                psU[tt][:, :],
                                xqT_sb[:, k * FBL + tt * 128:
                                       k * FBL + (tt + 1) * 128],
                                wB_sb[:, W0 + k * 512:W0 + (k + 1) * 512],
                                start=(k == 0), stop=(k == 7)),
                              reads=[f"xqT{k}", f"wB{sl_}"], writes=["ps2" if tt == 0 else "ps6"])
                            yield 1
                        E("s", lambda s, tt=tt: s.activation(
                            gate_sb[:, :], psG[tt][:, :], AF.Silu,
                            scale=fst_sb[:, 8 + tt:9 + tt]),
                          reads=[("ps1" if tt == 0 else "ps5"), f"sgv{tt}"], writes=["gate"])
                        yield 1
                        E("v", lambda v, tt=tt, ic=ic: v.scalar_tensor_tensor(
                            g_sb[:, tt * INNER + ic * 512:
                                 tt * INNER + (ic + 1) * 512],
                            psU[tt][:, :], fst_sb[:, 10 + tt:11 + tt],
                            gate_sb[:, :], op0=ALU.mult, op1=ALU.mult),
                          reads=[("ps2" if tt == 0 else "ps6"), f"suv{tt}", "gate"],
                          writes=[f"g{tt}_{ic}"])
                        yield 1
                for tt in range(NTT):
                    E("v", lambda v, tt=tt: v.reduce_max(
                        fst_sb[:, 12 + tt:13 + tt],
                        g_sb[:, tt * INNER:(tt + 1) * INNER],
                        axis=mybir.AxisListType.X, apply_absolute_value=True),
                      reads=[f"g{tt}_{ic}" for ic in range(8)], writes=[f"gmax{tt}"])
                    yield 1
                    E("v", lambda v, tt=tt: v.tensor_scalar_max(
                        fst_sb[:, 12 + tt:13 + tt], fst_sb[:, 12 + tt:13 + tt],
                        1e-5),
                      reads=[f"gmax{tt}"], writes=[f"gmax{tt}"])
                    yield 1
                    E("v", lambda v, tt=tt: v.reciprocal(
                        fst_sb[:, 14 + tt:15 + tt], fst_sb[:, 12 + tt:13 + tt]),
                      reads=[f"gmax{tt}"], writes=[f"gsx{tt}"])
                    yield 1
                    E("v", lambda v, tt=tt: v.tensor_scalar(
                        fst_sb[:, 14 + tt:15 + tt], fst_sb[:, 14 + tt:15 + tt],
                        127.0, None, op0=ALU.mult),
                      reads=[f"gsx{tt}"], writes=[f"gsx{tt}"])
                    yield 1
                    E("v", lambda v, tt=tt: v.tensor_scalar(
                        fst_sb[:, 12 + tt:13 + tt], fst_sb[:, 12 + tt:13 + tt],
                        1.0 / 127.0, None, op0=ALU.mult),
                      reads=[f"gmax{tt}"], writes=[f"gisx{tt}"])
                    yield 1
                    E("v", lambda v, tt=tt: v.tensor_tensor(
                        fst_sb[:, 12 + tt:13 + tt], fst_sb[:, 12 + tt:13 + tt],
                        wscb_sb[:, 6:7], op=ALU.mult),
                      reads=[f"gisx{tt}", "wscb"], writes=[f"sdv{tt}"])
                    yield 1
                    for hf in range(4):
                        E("v", lambda v, tt=tt, hf=hf: v.tensor_scalar(
                            fscr_sb[:, :],
                            g_sb[:, tt * INNER + hf * 1024:
                                 tt * INNER + (hf + 1) * 1024],
                            fst_sb[:, 14 + tt:15 + tt], MAGIC,
                            op0=ALU.mult, op1=ALU.add),
                          reads=[f"g{tt}_{ic}" for ic in range(8)] + [f"gsx{tt}"],
                          writes=["fdu"])
                        yield 1
                        E("v", lambda v, tt=tt, hf=hf: v.tensor_scalar(
                            gq_sb[:, tt * INNER + hf * 1024:
                                  tt * INNER + (hf + 1) * 1024],
                            fscr_sb[:, :], MAGIC, None, op0=ALU.subtract),
                          reads=["fdu"],
                      writes=[f"gq{tt}", f"gqg{tt}{hf % 2}"])
                        yield 1
                    for k in range(32):
                        pst = psT[k % 2]
                        E("t", lambda t, k=k, tt=tt, pst=pst: t.transpose(
                            pst.bitcast(BF16)[:, 0:128],
                            gq_sb[:, tt * INNER + k * 128:
                                  tt * INNER + (k + 1) * 128],
                            eyeb_sb[:, :]),
                          reads=[f"gq{tt}", f"g{tt}_0", "eyeb"],
                          writes=[f"psT{k % 2}"])
                        yield 1
                        E("s", lambda s, k=k, tt=tt, pst=pst: s.activation(
                            gqT_sb[:, k * FBL + tt * 128: k * FBL + (tt + 1) * 128],
                            pst.bitcast(BF16)[:, 0:128], AF.Copy),
                          reads=[f"psT{k % 2}"], writes=[f"gqT{k}_{tt}"])
                        yield 1
                yield "gq"
                for dh in range(2):
                    for kg in range(4):
                        sl_ = kg % 2
                        W0 = sl_ * 8 * 512
                        E("q", lambda q, dh=dh, kg=kg, W0=W0: q.dma_start(
                            out=wA_sb[:, W0:W0 + 8 * 512].rearrange("p (k c) -> p k c", k=8),
                            in_=cc_w_out[kg, 2, :, dh * 512:(dh + 1) * 512]
                            .rearrange("(k p) c -> p k c", p=128)),
                          reads=["ccwo"], writes=[f"wA{sl_}"], dma=True, chan=f"qw{sl_}")
                        yield 1
                        for tt in range(NTT):
                            for k8 in range(8):
                                k = kg * 8 + k8
                                E("t", lambda t, k=k, k8=k8, tt=tt, dh=dh, W0=W0:
                                  t.matmul(
                                      psD_[tt][:, :],
                                      gqT_sb[:, k * FBL + tt * 128:
                                             k * FBL + (tt + 1) * 128],
                                      wA_sb[:, W0 + k8 * 512:W0 + (k8 + 1) * 512],
                                      start=(k == 0), stop=(k == 31)),
                                  reads=[f"gqT{k}_{tt}", f"wA{sl_}"],
                                  writes=["ps7" if tt == 0 else "ps5"])
                                yield 1
                    for tt in range(NTT):
                        E("v", lambda v, tt=tt, dh=dh: v.scalar_tensor_tensor(
                            x2_sb[:, tt * D + dh * 512: tt * D + (dh + 1) * 512],
                            psD_[tt][:, :], fst_sb[:, 12 + tt:13 + tt],
                            x2_sb[:, tt * D + dh * 512: tt * D + (dh + 1) * 512],
                            op0=ALU.mult, op1=ALU.add),
                          reads=[("ps7" if tt == 0 else "ps5"), f"sdv{tt}",
                                 f"x2{tt}"],
                          writes=[f"x2{tt}"])
                        yield 1
                        E("q", lambda q, tt=tt, dh=dh, fb0=fb0: q.dma_start(
                            out=out_d[fb0 + tt * 128: fb0 + (tt + 1) * 128,
                                      dh * 512:(dh + 1) * 512],
                            in_=x2_sb[:, tt * D + dh * 512:
                                      tt * D + (dh + 1) * 512]),
                          reads=[f"x2{tt}"], writes=["outdr"], dma=True,
                          chan="qso")
                        yield 1



                yield "end"

        _gens = [ffn_block_emit(bl) for bl in range(NFB)]
        _gpos = [0]

        def drive_ffn(ch, k_yield=12, drain=False):
            while _gpos[0] < NFB:
                bl = _gpos[0]
                if not drain and ch < 2 * bl + 1:
                    return
                g = _gens[bl]
                steps = 0
                for _ in g:
                    steps += 1
                    if not drain and steps >= k_yield:
                        return
                _gpos[0] += 1

        def pre_emit(ch):
            cp = ch % 2
            tok0 = ch * Q
            for tt in range(4):
                ut = u_ch[:, tt * D:(tt + 1) * D]
                E("q", lambda q, tok0=tok0, tt=tt, ut=ut: q.dma_start(
                    out=ut, in_=u_full[tok0 + tt * 128: tok0 + (tt + 1) * 128, :]),
                  writes=[f"u{tt}", f"xn{tt}"]
                  + (["wrow"] if ch == 0 and tt == 0 else []),
                  dma=True, chan=f"qu{tt}")
                yield 1
                E("v", lambda v, tt=tt, ut=ut: v.reduce_sum(
                    st_ch[:, tt:tt + 1], ut, axis=mybir.AxisListType.X),
                  reads=[f"u{tt}"], writes=["stm"])
                yield 1
                E("s", lambda s, tt=tt, ut=ut: s.activation(
                    sq_ch[:, :], ut, AF.Square, accum_out=st_ch[:, 4 + tt:5 + tt]),
                  reads=[f"u{tt}"],
                  writes=["stsq", "sts2", "xnT0", "xnT1", "xnT2", "xnT3"])
                yield 1
            E("v", lambda v: v.tensor_scalar(
                st_ch[:, 0:4], st_ch[:, 0:4], 1.0 / D, None, op0=ALU.mult),
              reads=["stm"], writes=["stm"])
            E("v", lambda v: v.tensor_scalar(
                st_ch[:, 4:8], st_ch[:, 4:8], 1.0 / D, EPS,
                op0=ALU.mult, op1=ALU.add),
              reads=["sts2"], writes=["sts2"])
            E("v", lambda v: v.tensor_tensor(
                st_ch[:, 8:12], st_ch[:, 0:4], st_ch[:, 0:4], op=ALU.mult),
              reads=["stm"], writes=["stm2"])
            E("v", lambda v: v.tensor_tensor(
                st_ch[:, 4:8], st_ch[:, 4:8], st_ch[:, 8:12], op=ALU.subtract),
              reads=["sts2", "stm2"], writes=["str"])
            yield 1
            E("s", lambda s: s.activation(st_ch[:, 4:8], st_ch[:, 4:8], AF.Ln),
              reads=["str"], writes=["str"])
            E("s", lambda s: s.activation(st_ch[:, 4:8], st_ch[:, 4:8], AF.Exp,
                                          scale=-0.5),
              reads=["str"], writes=["str"])
            yield 1
            for tt in range(4):
                ut = u_ch[:, tt * D:(tt + 1) * D]
                E("v", lambda v, tt=tt, ut=ut: v.tensor_scalar(
                    ut, ut, st_ch[:, tt:tt + 1], st_ch[:, 4 + tt:5 + tt],
                    op0=ALU.subtract, op1=ALU.mult),
                  reads=[f"u{tt}", "stm", "str"], writes=[f"u{tt}", f"xn{tt}"])
                yield 1
                E("v", lambda v, tt=tt, ut=ut: v.tensor_copy(
                    xnb_sb[:, :], ut),
                  reads=[f"xn{tt}"], writes=["xnb", "ytok0", "ytok1"])
                yield 1
                for k in range(8):
                    E("s", lambda t, k=k, tt=tt: t.dma_start_transpose(
                        xnT_sb[:, k * Q + tt * 128: k * Q + (tt + 1) * 128],
                        xnb_sb[:, k * 128:(k + 1) * 128]),
                      reads=["xnb"], writes=[f"xnT{k}", "xnTall"], dma=True,
                      chan="qtr")
                    yield 1
                for k in range(2):
                    pst = psT[k]
                    E("t", lambda t, ut=ut, k=k, pst=pst: t.transpose(
                        pst[:, 0:128], ut[:, k * 128:(k + 1) * 128],
                        eye_sb[:, :]),
                      reads=[f"xn{tt}", "eye"], writes=[f"psT{k}"])
                    E("s", lambda s, k=k, tt=tt, cp=cp, pst=pst: s.activation(
                        xnsl_sb[:, cp * 2 * Q + k * Q + tt * 128:
                                cp * 2 * Q + k * Q + (tt + 1) * 128],
                        pst[:, 0:128], AF.Copy),
                      reads=[f"psT{k}"], writes=[f"xnsl{cp}{k}"])
                    yield 1
            for k in range(8):
                E("t", lambda t, k=k: t.matmul(
                    ps0[0:RK, 0:Q], wxdtT_sb[:, k * RK:(k + 1) * RK],
                    xnT_sb[:, k * Q:(k + 1) * Q],
                    start=(k == 0), stop=(k == 7)),
                  reads=[f"xnT{k}", "xnTall", "wxdtT"], writes=["ps0"])
                yield 1
            E("s", lambda s: s.activation(
                xdt_sb[:, :], ps0[0:RK, 0:Q], AF.Identity, bias=bxdt_sb[:, :]),
              reads=["ps0", "bxdt"], writes=["xdt"])
            yield 1
            for k in range(8):
                E("t", lambda t, k=k: t.matmul(
                    ps0[0:NST, 0:Q], wbT_sb[:, k * NST:(k + 1) * NST],
                    xnT_sb[:, k * Q:(k + 1) * Q],
                    start=(k == 0), stop=(k == 7)),
                  reads=[f"xnT{k}", "xnTall", "wbT"], writes=["ps0"])
                yield 1
            E("s", lambda s: s.activation(
                bcq_sb[:, :], ps0[0:NST, 0:Q], AF.Identity, bias=bb_sb[:, :]),
              reads=["ps0", "bbias"], writes=["bcq"])
            E("q", lambda q, tok0=tok0: q.dma_start(
                out=bc_B[:, tok0:tok0 + Q], in_=bcq_sb[:, :]),
              reads=["bcq"], writes=["bcBdr"], dma=True, chan="qsB")
            yield 1
            for k in range(8):
                E("t", lambda t, k=k: t.matmul(
                    ps0[0:NST, 0:Q], wcT_sb[:, k * NST:(k + 1) * NST],
                    xnT_sb[:, k * Q:(k + 1) * Q],
                    start=(k == 0), stop=(k == 7)),
                  reads=[f"xnT{k}", "xnTall", "wcT"], writes=["ps0"])
                yield 1
            E("s", lambda s: s.activation(
                bcq_sb[:, :], ps0[0:NST, 0:Q], AF.Identity, bias=bc_sb[:, :]),
              reads=["ps0", "cbias"], writes=["ccq", "bcq"])
            E("q", lambda q, tok0=tok0: q.dma_start(
                out=bc_C[:, tok0:tok0 + Q], in_=bcq_sb[:, :]),
              reads=["ccq"], writes=["bcCdr"], dma=True, chan="qsC")
            yield 1
            for dti in range(2):
                psd = ps0
                E("t", lambda t, dti=dti, psd=psd: t.matmul(
                    psd[:, 0:Q], wdtT_sb[:, dti * 128:(dti + 1) * 128],
                    xdt_sb[:, :], start=True, stop=True),
                  reads=["xdt", "wdtT"], writes=["ps0"])
                yield 1
                E("s", lambda s, dti=dti, psd=psd: s.activation(
                    edt_sb[:, :], psd[:, 0:Q], AF.Exp,
                    bias=bdt_sb[:, dti:dti + 1]),
                  reads=["ps0", "bdt"], writes=["stsq", "xnT0", "xnT1"])
                yield 1
                E("s", lambda s, dti=dti, cp=cp: s.activation(
                    dt_sb[:, cp * 2 * Q + dti * Q:
                          cp * 2 * Q + (dti + 1) * Q], edt_sb[:, :], AF.Ln,
                    bias=1.0),
                  reads=["stsq"], writes=[f"dt{cp}{dti}"])
                yield 1
                E("g", lambda g, dti=dti, cp=cp: g.tensor_tensor(
                    c_sb[:, cp * 2 * Q + dti * Q:cp * 2 * Q + (dti + 1) * Q],
                    dt_sb[:, cp * 2 * Q + dti * Q:cp * 2 * Q + (dti + 1) * Q],
                    xnsl_sb[:, cp * 2 * Q + dti * Q:
                            cp * 2 * Q + (dti + 1) * Q], op=ALU.mult),
                  reads=[f"dt{cp}{dti}", f"xnsl{cp}{dti}"],
                  writes=[f"c{cp}{dti}"])
                yield 1

        PG = {"g": iter(())}

        def drive_pre(k):
            for _ in range(k):
                if next(PG["g"], None) is None:
                    return

        PG["g"] = pre_emit(0)
        drive_pre(10 ** 9)

        for ch in range(NCH):
            tok0 = ch * Q
            cp = ch % 2
            PG["g"] = pre_emit(ch + 1) if ch + 1 < NCH else iter(())
            for n in range(NST + 1):
                if n < NST:
                    bsl = n % 2
                    E("q", lambda q, n=n, tok0=tok0, bsl=bsl: q.dma_start(
                        out=bbc_sb[:, bsl * Q:(bsl + 1) * Q],
                        in_=bc_B[n:n + 1, tok0:tok0 + Q].broadcast_to([128, Q])),
                      reads=["bcBdr"], writes=[f"bbc{bsl}"], dma=True,
                      chan=f"qb{bsl}")
                    E("q", lambda q, n=n, tok0=tok0, bsl=bsl: q.dma_start(
                        out=cbc_sb[:, bsl * Q:(bsl + 1) * Q],
                        in_=bc_C[n:n + 1, tok0:tok0 + Q].broadcast_to([128, Q])),
                      reads=["bcCdr"], writes=[f"cbc{bsl}"], dma=True,
                      chan=f"qc{bsl}")
                    for dti in range(2):
                        asl = (n % 2) * 2 + dti
                        E("s", lambda s, n=n, dti=dti, asl=asl, cp=cp:
                          s.activation(
                              abar_sb[:, asl * Q:(asl + 1) * Q],
                              dt_sb[:, cp * 2 * Q + dti * Q:
                                    cp * 2 * Q + (dti + 1) * Q], AF.Exp,
                              scale=acol_sb[:, dti * NST + n:
                                            dti * NST + n + 1]),
                          reads=[f"dt{cp}{dti}", "acol"], writes=[f"abar{asl}"])
                    for dti in range(2):
                        asl = (n % 2) * 2 + dti
                        E("g", lambda g, dti=dti, asl=asl, bsl=bsl, cp=cp:
                          g.tensor_tensor(
                              in1_sb[:, asl * Q:(asl + 1) * Q],
                              c_sb[:, cp * 2 * Q + dti * Q:
                                   cp * 2 * Q + (dti + 1) * Q],
                              bbc_sb[:, bsl * Q:(bsl + 1) * Q], op=ALU.mult),
                          reads=[f"c{cp}{dti}", f"bbc{bsl}"],
                          writes=[f"in1{asl}"])
                drive_pre(6)
                if n == 0:
                    continue
                m = n - 1
                msl0 = (m % 2) * 2
                for dti in range(2):
                    hsl = msl0 + dti
                    col = dti * NST + m
                    E("v", lambda v, hsl=hsl, col=col:
                      v.tensor_tensor_scan(
                          h_sb[:, hsl * Q:(hsl + 1) * Q],
                          abar_sb[:, hsl * Q:(hsl + 1) * Q],
                          in1_sb[:, hsl * Q:(hsl + 1) * Q],
                          hstate[:, col:col + 1], op0=ALU.mult, op1=ALU.add),
                      reads=[f"abar{hsl}", f"in1{hsl}", f"hst{col}"],
                      writes=[f"h{hsl}"])
                for dti in range(2):
                    hsl = msl0 + dti
                    col = dti * NST + m
                    E("v", lambda v, hsl=hsl, col=col: v.tensor_copy(
                        hstate[:, col:col + 1],
                        h_sb[:, hsl * Q + Q - 1: hsl * Q + Q]),
                      reads=[f"h{hsl}"], writes=[f"hst{col}"])
                ei = m % 2
                heng = "v" if ei == 0 else "g"
                bslm = m % 2
                for dti in range(2):
                    hsl = msl0 + dti
                    ysl = ei * 2 + dti
                    if m < 2:
                        E(heng, lambda e, hsl=hsl, bslm=bslm, ysl=ysl:
                          e.tensor_tensor(
                              ych_sb[:, ysl * Q:(ysl + 1) * Q],
                              h_sb[:, hsl * Q:(hsl + 1) * Q],
                              cbc_sb[:, bslm * Q:(bslm + 1) * Q], op=ALU.mult),
                          reads=[f"h{hsl}", f"cbc{bslm}"], writes=[f"ych{ysl}"])
                    else:
                        E(heng, lambda e, hsl=hsl, bslm=bslm, hcs=ei:
                          e.tensor_tensor(
                              hc_sb[:, hcs * Q:(hcs + 1) * Q],
                              h_sb[:, hsl * Q:(hsl + 1) * Q],
                              cbc_sb[:, bslm * Q:(bslm + 1) * Q], op=ALU.mult),
                          reads=[f"h{hsl}", f"cbc{bslm}"], writes=[f"hc{ei}"])
                        E(heng, lambda e, ysl=ysl, hsl=ei: e.tensor_tensor(
                            ych_sb[:, ysl * Q:(ysl + 1) * Q],
                            ych_sb[:, ysl * Q:(ysl + 1) * Q],
                            hc_sb[:, hsl * Q:(hsl + 1) * Q],
                            op=ALU.add),
                          reads=[f"hc{ei}", f"ych{ysl}"], writes=[f"ych{ysl}"])
            drive_pre(10 ** 9)
            for dti in range(2):
                E("v", lambda v, dti=dti: v.tensor_tensor(
                    ych_sb[:, dti * Q:(dti + 1) * Q],
                    ych_sb[:, dti * Q:(dti + 1) * Q],
                    ych_sb[:, (2 + dti) * Q:(3 + dti) * Q], op=ALU.add),
                  reads=[f"ych{dti}", f"ych{2 + dti}"], writes=[f"ych{dti}"])
                E("v", lambda v, dti=dti, cp=cp: v.scalar_tensor_tensor(
                    ych_sb[:, dti * Q:(dti + 1) * Q],
                    xnsl_sb[:, cp * 2 * Q + dti * Q:
                            cp * 2 * Q + (dti + 1) * Q],
                    dcol_sb[:, dti:dti + 1],
                    ych_sb[:, dti * Q:(dti + 1) * Q],
                    op0=ALU.mult, op1=ALU.add),
                  reads=[f"xnsl{cp}{dti}", "dcol", f"ych{dti}"],
                  writes=[f"ych{dti}"])
            for tt in range(4):
                for dti in range(2):
                    pst = psT[dti]
                    E("t", lambda t, dti=dti, tt=tt, pst=pst: t.transpose(
                        pst[:, 0:128],
                        ych_sb[:, dti * Q + tt * 128: dti * Q + (tt + 1) * 128],
                        eye_sb[:, :]),
                      reads=[f"ych{dti}", "eye"], writes=[f"psT{dti}"])
                    E("s", lambda s, dti=dti, tt=tt, pst=pst: s.activation(
                        ytok_sb[:, (tt % 2) * DSL + dti * 128:
                                (tt % 2) * DSL + (dti + 1) * 128],
                        pst[:, 0:128], AF.Copy),
                      reads=[f"psT{dti}"], writes=[f"ytok{tt % 2}", "xnb"])
                E("q", lambda q, tt=tt, ch=ch: q.dma_start(
                    out=ag_in[ch][tt * 128:(tt + 1) * 128, :],
                    in_=ytok_sb[:, (tt % 2) * DSL:(tt % 2 + 1) * DSL]),
                  reads=[f"ytok{tt % 2}"], writes=[f"ccy{ch}"], dma=True,
                  chan=f"qsy{ch}")
            E("g", lambda g, ch=ch: g.collective_compute(
                "AllGather", ALU.bypass, replica_groups=GRPS,
                ins=[ag_in[ch].ap().opt()],
                outs=[ag_out[ch].ap().opt()]),
              reads=[f"ccy{ch}"], writes=[f"ag{ch}"])
            drive_ffn(ch)

        drive_ffn(NCH, drain=True)

        L.barrier()

        def play(engine, name):
            if name == "q":
                pid = engine.partition_id()
                PID[0] = pid % 4
            for item in L.queue[name]:
                if item[0] == "wait":
                    engine.wait_ge(sems[item[1]], item[2])
                else:
                    _, fn, step, sem_name = item
                    fn(engine).then_inc(sems[sem_name], step)

        @block.vector
        def _(v):
            play(v, "v")

        @block.scalar
        def _(s):
            play(s, "s")

        @block.tensor
        def _(t):
            play(t, "t")

        @block.gpsimd
        def _(g):
            play(g, "g")

        @block.sync
        def _(q):
            play(q, "q")

    return nc


_CACHE = {}


def kernel(L0_attn_u, L0_attn_A_log, L0_attn_W_xdt, L0_attn_b_xdt,
           L0_attn_W_dt, L0_attn_b_dt, L0_attn_W_B, L0_attn_b_B,
           L0_attn_W_C, L0_attn_b_C, L0_attn_D, L0_ffn_w_gate,
           L0_ffn_w_up, L0_ffn_w_down):
    u = np.ascontiguousarray(np.asarray(L0_attn_u, dtype=np.float32))
    eye = np.eye(128, dtype=np.float32)
    in_maps = []
    for c in range(8):
        b, q = c // 4, c % 4
        perm = np.r_[np.arange(q * DSL, (q + 1) * DSL),
                     np.arange(0, q * DSL), np.arange((q + 1) * DSL, D)]
        im = {
            "u_full": np.ascontiguousarray(u[b][:, perm]),
            "u_tok": np.ascontiguousarray(
                u[b].reshape(NCH, 4, 128, D)[:, q].reshape(TQ, D)),
            "wxdtT": np.ascontiguousarray(
                np.asarray(L0_attn_W_xdt, np.float32).T[perm, :]),
            "bxdt": np.asarray(L0_attn_b_xdt, np.float32).reshape(RK, 1),
            "wdtT_sl": np.ascontiguousarray(
                np.asarray(L0_attn_W_dt, np.float32)[q * DSL:(q + 1) * DSL, :].T),
            "bdt_sl": np.ascontiguousarray(
                np.asarray(L0_attn_b_dt, np.float32)[q * DSL:(q + 1) * DSL]
                .reshape(DSL, 1)),
            "wbT": np.ascontiguousarray(
                np.asarray(L0_attn_W_B, np.float32).T[perm, :]),
            "bb": np.asarray(L0_attn_b_B, np.float32).reshape(NST, 1),
            "wcT": np.ascontiguousarray(
                np.asarray(L0_attn_W_C, np.float32).T[perm, :]),
            "bc": np.asarray(L0_attn_b_C, np.float32).reshape(NST, 1),
            "alog_sl": np.ascontiguousarray(
                np.asarray(L0_attn_A_log, np.float32)[q * DSL:(q + 1) * DSL, :]),
            "dvec_sl": np.ascontiguousarray(
                np.asarray(L0_attn_D, np.float32)[q * DSL:(q + 1) * DSL]
                .reshape(DSL, 1)),
            "eye": eye,
            "wgT_q": np.ascontiguousarray(
                np.asarray(L0_ffn_w_gate, np.float32).T[:, q * 1024:(q + 1) * 1024]),
            "wuT_q": np.ascontiguousarray(
                np.asarray(L0_ffn_w_up, np.float32).T[:, q * 1024:(q + 1) * 1024]),
            "wdT_q": np.ascontiguousarray(
                np.asarray(L0_ffn_w_down, np.float32).T[q * 1024:(q + 1) * 1024, :]),
        }
        in_maps.append(im)
    if "nc" not in _CACHE:
        _CACHE["nc"] = build_program()
    nc = _CACHE["nc"]
    res = run_bass_kernel_spmd(nc, in_maps, core_ids=list(range(8)))
    _CACHE["res"] = res
    out = np.empty((B, T, D), dtype=np.float32)
    ov = out.reshape(B, NCH, 4, 128, D)
    for c in range(8):
        b, q = c // 4, c % 4
        ov[b, :, q] = res.results[c]["out"].reshape(NCH, 128, D)
    return out


# version 10
# speedup vs baseline: 1.4173x; 1.1637x over previous
import numpy as np
from contextlib import ExitStack
import concourse.bass as bass
import concourse.mybir as mybir
from concourse.bass_utils import run_bass_kernel_spmd

F32 = mybir.dt.float32
BF16 = mybir.dt.bfloat16
AF = mybir.ActivationFunctionType
ALU = mybir.AluOpType

B, T, D, NST, RK, INNER = 2, 8192, 1024, 16, 64, 4096
DSL = 256
Q = 512
NCH = T // Q
TQ = T // 4
FBL = 256
NFB = TQ // FBL
NTT = FBL // 128
MAGIC = float(np.float32(1.5 * 2 ** 23))
EPS = 1e-5
NW = float(INNER * D)
GRPS = [[0, 1, 2, 3], [4, 5, 6, 7]]


class Ledger:

    def __init__(self, sems, engines):
        self.sems = sems
        self.engines = engines
        self.cnt = {e: 0 for e in sems}
        self.mark = {e: {} for e in engines}
        self.buf = {}
        self.queue = {e: [] for e in engines}

    def emit(self, e, fn, reads=(), writes=(), dma=False, chan=None):
        sem_name = chan if (dma and chan) else e
        assert sem_name in self.sems, sem_name
        need = {}
        for k in reads:
            b = self.buf.get(k)
            if b and b["w"] is not None:
                se, sv = b["w"]
                need[se] = max(need.get(se, -1), sv)
        for k in writes:
            b = self.buf.get(k)
            if b:
                if b["w"] is not None:
                    se, sv = b["w"]
                    need[se] = max(need.get(se, -1), sv)
                for se, sv in b["r"].items():
                    need[se] = max(need.get(se, -1), sv)
        for se in sorted(need):
            sv = need[se]
            if se == e == "t":
                continue
            if se not in self.engines:
                sv = self.cnt.get(se, sv)
            if self.mark[e].get(se, -1) < sv:
                self.queue[e].append(("wait", se, sv))
                self.mark[e][se] = sv
        step = 16 if dma else 1
        self.cnt[sem_name] = self.cnt.get(sem_name, 0) + step
        v = self.cnt[sem_name]
        self.queue[e].append(("op", fn, step, sem_name))
        for k in reads:
            b = self.buf.setdefault(k, {"w": None, "r": {}})
            b["r"][sem_name] = max(b["r"].get(sem_name, -1), v)
        for k in writes:
            self.buf[k] = {"w": (sem_name, v), "r": {}}

    def barrier(self):
        tops = dict(self.cnt)
        for e in self.queue:
            for se, sv in tops.items():
                if se != e and sv > 0 and self.mark[e].get(se, -1) < sv:
                    self.queue[e].append(("wait", se, sv))
                    self.mark[e][se] = sv


def build_program():
    nc = bass.Bass("TRN2", target_bir_lowering=False)

    u_full = nc.dram_tensor("u_full", [T, D], F32, kind="ExternalInput").ap()
    u_tok = nc.dram_tensor("u_tok", [TQ, D], F32, kind="ExternalInput").ap()
    wxdtT_d = nc.dram_tensor("wxdtT", [D, RK], F32, kind="ExternalInput").ap()
    bxdt_d = nc.dram_tensor("bxdt", [RK, 1], F32, kind="ExternalInput").ap()
    wdtT_d = nc.dram_tensor("wdtT_sl", [RK, DSL], F32, kind="ExternalInput").ap()
    bdt_d = nc.dram_tensor("bdt_sl", [DSL, 1], F32, kind="ExternalInput").ap()
    wbT_d = nc.dram_tensor("wbT", [D, NST], F32, kind="ExternalInput").ap()
    bb_d = nc.dram_tensor("bb", [NST, 1], F32, kind="ExternalInput").ap()
    wcT_d = nc.dram_tensor("wcT", [D, NST], F32, kind="ExternalInput").ap()
    bc_d = nc.dram_tensor("bc", [NST, 1], F32, kind="ExternalInput").ap()
    alog_d = nc.dram_tensor("alog_sl", [DSL, NST], F32, kind="ExternalInput").ap()
    dvec_d = nc.dram_tensor("dvec_sl", [DSL, 1], F32, kind="ExternalInput").ap()
    eye_d = nc.dram_tensor("eye", [128, 128], F32, kind="ExternalInput").ap()
    wgT_d = nc.dram_tensor("wgT_q", [D, 1024], F32, kind="ExternalInput").ap()
    wuT_d = nc.dram_tensor("wuT_q", [D, 1024], F32, kind="ExternalInput").ap()
    wdT_d = nc.dram_tensor("wdT_q", [1024, D], F32, kind="ExternalInput").ap()
    out_d = nc.dram_tensor("out", [TQ, D], F32, kind="ExternalOutput").ap()
    bc_B = nc.dram_tensor("bc_B", [NST, T], BF16).ap()
    bc_C = nc.dram_tensor("bc_C", [NST, T], BF16).ap()
    scal_dr = nc.dram_tensor("scal_dr", [1, 8], F32).ap()
    ag_in = [nc.dram_tensor(f"agi{c}", [Q, DSL], F32) for c in range(NCH)]
    ag_out = [nc.dram_tensor(f"ago{c}", [4, Q, DSL], F32) for c in range(NCH)]
    cc_w_in = nc.dram_tensor("ccwi", [3, 1024, 1024], BF16)
    cc_w_out = nc.dram_tensor("ccwo", [4, 3, 1024, 1024], BF16)
    cc_s_in = nc.dram_tensor("ccsi", [1, 4], F32)
    cc_s_out = nc.dram_tensor("ccso", [1, 4], F32)

    _st = ExitStack()
    with _st:
        eye_sb = _st.enter_context(nc.sbuf_tensor("eye_sb", [128, 128], F32))
        eyeb_sb = _st.enter_context(nc.sbuf_tensor("eyeb_sb", [128, 128], BF16))
        wxdtT_sb = _st.enter_context(nc.sbuf_tensor("wxdtT_sb", [128, 8 * RK], BF16))
        wbT_sb = _st.enter_context(nc.sbuf_tensor("wbT_sb", [128, 8 * NST], BF16))
        wcT_sb = _st.enter_context(nc.sbuf_tensor("wcT_sb", [128, 8 * NST], BF16))
        wdtT_sb = _st.enter_context(nc.sbuf_tensor("wdtT_sb", [RK, DSL], BF16))
        bxdt_sb = _st.enter_context(nc.sbuf_tensor("bxdt_sb", [RK, 1], F32))
        bdt_sb = _st.enter_context(nc.sbuf_tensor("bdt_sb", [128, 2], F32))
        dcol_sb = _st.enter_context(nc.sbuf_tensor("dcol_sb", [128, 2], F32))
        bb_sb = _st.enter_context(nc.sbuf_tensor("bb_sb", [NST, 1], F32))
        bc_sb = _st.enter_context(nc.sbuf_tensor("bc_sb", [NST, 1], F32))
        acol_sb = _st.enter_context(nc.sbuf_tensor("acol_sb", [128, 2 * NST], F32))
        hstate = _st.enter_context(nc.sbuf_tensor("hstate", [128, 2 * NST], F32))
        wsum_sb = _st.enter_context(nc.sbuf_tensor("wsum_sb", [128, 4], F32))
        wsc_sb = _st.enter_context(nc.sbuf_tensor("wsc_sb", [1, 8], F32))
        wscb_sb = _st.enter_context(nc.sbuf_tensor("wscb_sb", [128, 8], F32))
        u_ch = _st.enter_context(nc.sbuf_tensor("u_ch", [128, 4 * D], F32))
        ygq_sb = _st.enter_context(nc.sbuf_tensor("ygq_sb", [128, 2 * D], F32))
        ufq_sb = _st.enter_context(nc.sbuf_tensor("ufq_sb", [128, 2 * D], F32))
        fscr_sb = _st.enter_context(nc.sbuf_tensor("fscr_sb", [128, D], F32))
        st_ch = _st.enter_context(nc.sbuf_tensor("st_ch", [128, 12], F32))
        xnT_sb = _st.enter_context(nc.sbuf_tensor("xnT_sb", [128, 8 * Q], BF16))
        xnsl_sb = _st.enter_context(nc.sbuf_tensor("xnsl_sb", [128, 4 * Q], BF16))
        dt_sb = _st.enter_context(nc.sbuf_tensor("dt_sb", [128, 4 * Q], BF16))
        xdt_sb = _st.enter_context(nc.sbuf_tensor("xdt_sb", [RK, Q], BF16))
        bcq_sb = _st.enter_context(nc.sbuf_tensor("bcq_sb", [NST, Q], BF16))
        c_sb = _st.enter_context(nc.sbuf_tensor("c_sb", [128, 4 * Q], BF16))
        bbc_sb = _st.enter_context(nc.sbuf_tensor("bbc_sb", [128, 4 * Q], BF16))
        cbc_sb = _st.enter_context(nc.sbuf_tensor("cbc_sb", [128, 4 * Q], BF16))
        abar_sb = _st.enter_context(nc.sbuf_tensor("abar_sb", [128, 4 * Q], F32))
        in1_sb = _st.enter_context(nc.sbuf_tensor("in1_sb", [128, 4 * Q], F32))
        h_sb = _st.enter_context(nc.sbuf_tensor("h_sb", [128, 4 * Q], F32))
        hc_sb = _st.enter_context(nc.sbuf_tensor("hc_sb", [128, 2 * Q], F32))
        ych_sb = _st.enter_context(nc.sbuf_tensor("ych_sb", [128, 4 * Q], F32))
        ytok_sb = _st.enter_context(nc.sbuf_tensor("ytok_sb", [128, 2 * DSL], F32))
        x2_sb = _st.enter_context(nc.sbuf_tensor("x2_sb", [128, NTT * D], F32))
        fst_sb = _st.enter_context(nc.sbuf_tensor("fst_sb", [128, 16], F32))
        xq_sb = _st.enter_context(nc.sbuf_tensor("xq_sb", [128, NTT * D], BF16))
        xqT_sb = _st.enter_context(nc.sbuf_tensor("xqT_sb", [128, 8 * FBL], BF16))
        wA_sb = _st.enter_context(nc.sbuf_tensor("wA_sb", [128, 2 * 8 * 512], BF16))
        wB_sb = _st.enter_context(nc.sbuf_tensor("wB_sb", [128, 2 * 8 * 512], BF16))
        gate_sb = _st.enter_context(nc.sbuf_tensor("gate_sb", [128, 512], F32))
        arena_a = _st.enter_context(nc.sbuf_tensor("arena_a", [128, 8192], F32))
        arena_b = _st.enter_context(nc.sbuf_tensor("arena_b", [128, 4096], F32))
        g_sb = arena_a.ap()
        wq_ld = arena_a.ap()
        gq_sb = arena_a.ap().bitcast(BF16)[:, 0:NTT * INNER]
        gqT_sb = arena_b.ap().bitcast(BF16)[:, 0:NTT * INNER]
        wq_r = arena_a.ap()
        wq_b = wA_sb.ap()[:, 0:4096]
        yg_sb = ygq_sb.ap()
        uf_sb = ufq_sb.ap()
        sq_ch = xnT_sb.ap().bitcast(F32)[:, 0:D]
        edt_sb = xnT_sb.ap().bitcast(F32)[:, 0:Q]
        xnb_sb = ytok_sb.ap().bitcast(BF16)

        fsq_sb = fscr_sb.ap()
        fdu_sb = fscr_sb.ap()
        wrow_sb = u_ch.ap()[0:1, 0:128]
        ps0 = _st.enter_context(nc.psum_tensor("ps0", [128, 512], F32))
        ps1 = _st.enter_context(nc.psum_tensor("ps1", [128, 512], F32))
        ps2 = _st.enter_context(nc.psum_tensor("ps2", [128, 512], F32))
        ps3 = _st.enter_context(nc.psum_tensor("ps3", [128, 512], F32))
        ps4 = _st.enter_context(nc.psum_tensor("ps4", [128, 512], F32))
        ps5 = _st.enter_context(nc.psum_tensor("ps5", [128, 512], F32))
        ps6 = _st.enter_context(nc.psum_tensor("ps6", [128, 512], F32))
        ps7 = _st.enter_context(nc.psum_tensor("ps7", [128, 512], F32))
        sem_v = _st.enter_context(nc.semaphore("sem_v"))
        sem_s = _st.enter_context(nc.semaphore("sem_s"))
        sem_t = _st.enter_context(nc.semaphore("sem_t"))
        sem_g = _st.enter_context(nc.semaphore("sem_g"))
        sem_q = _st.enter_context(nc.semaphore("sem_q"))
        block = _st.enter_context(nc.Block())
        sems = {"v": sem_v, "s": sem_s, "t": sem_t, "g": sem_g, "q": sem_q}
        CHANS = ["qm", "qwl", "qwr", "qss", "qsc", "qsw", "qu0", "qu1",
                 "qsB", "qsC", "qb0", "qb1", "qb2", "qb3", "qc0", "qc1", "qc2", "qc3",
                 "qsy0", "qsy1", "qsy2", "qsy3", "qsy4", "qsy5", "qsy6", "qsy7", "qsy8", "qsy9", "qsy10", "qsy11", "qsy12", "qsy13", "qsy14", "qsy15", "qdbg", "qd", "qu2", "qu3", "qtr",
                 "qy0", "qy1", "qf0", "qf1", "qw0", "qw1", "qw20", "qw21", "qso"]
        for cn in CHANS:
            sems[cn] = _st.enter_context(nc.semaphore("sem_" + cn))
        L = Ledger(sems, ["v", "s", "t", "g", "q"])
        E = L.emit
        PID = [None]

        E("q", lambda q: q.dma_start(out=eye_sb[:, :], in_=eye_d[:, :]),
          writes=["eye"], dma=True, chan="qm")
        E("v", lambda v: v.tensor_copy(eyeb_sb[:, :], eye_sb[:, :]),
          reads=["eye"], writes=["eyeb"])
        for nm, dst, src, nk, w in [
            ("wxdtT", wxdtT_sb, wxdtT_d, 8, RK),
            ("wbT", wbT_sb, wbT_d, 8, NST),
            ("wcT", wcT_sb, wcT_d, 8, NST),
        ]:
            E("q", lambda q, src=src, nk=nk, w=w: q.dma_start(
                out=wq_ld[:, 0:nk * w].rearrange("p (k r) -> p k r", k=nk),
                in_=src.rearrange("(k p) r -> p k r", p=128)),
              writes=["wq_ld"], dma=True, chan="qwl")
            E("v", lambda v, dst=dst, nk=nk, w=w: v.tensor_copy(
                dst[:, :], wq_ld[:, 0:nk * w]),
              reads=["wq_ld"], writes=[nm])
        E("q", lambda q: q.dma_start(out=wq_ld[0:RK, 0:DSL], in_=wdtT_d[:, :]),
          writes=["wq_ld"], dma=True, chan="qwl")
        E("v", lambda v: v.tensor_copy(wdtT_sb[:, :], wq_ld[0:RK, 0:DSL]),
          reads=["wq_ld"], writes=["wdtT"])
        E("q", lambda q: q.dma_start(out=bxdt_sb[:, :], in_=bxdt_d[:, :]),
          writes=["bxdt"], dma=True, chan="qm")
        E("q", lambda q: q.dma_start(
            out=bdt_sb[:, :].rearrange("p (d c) -> p d c", d=2),
            in_=bdt_d.rearrange("(d p) c -> p d c", p=128),
            allow_slow_non_contiguous=True),
          writes=["bdt"], dma=True, chan="qm")
        E("q", lambda q: q.dma_start(
            out=dcol_sb[:, :].rearrange("p (d c) -> p d c", d=2),
            in_=dvec_d.rearrange("(d p) c -> p d c", p=128),
            allow_slow_non_contiguous=True),
          writes=["dcol"], dma=True, chan="qm")
        E("q", lambda q: q.dma_start(out=bb_sb[:, :], in_=bb_d[:, :]),
          writes=["bbias"], dma=True, chan="qm")
        E("q", lambda q: q.dma_start(out=bc_sb[:, :], in_=bc_d[:, :]),
          writes=["cbias"], dma=True, chan="qm")
        E("q", lambda q: q.dma_start(
            out=acol_sb[:, :].rearrange("p (d n) -> p d n", d=2),
            in_=alog_d.rearrange("(d p) n -> p d n", p=128)),
          writes=["acol"], dma=True, chan="qm")
        E("s", lambda s: s.activation(acol_sb[:, :], acol_sb[:, :], AF.Exp),
          reads=["acol"], writes=["acol"])
        E("v", lambda v: v.tensor_scalar_mul(acol_sb[:, :], acol_sb[:, :], -1.0),
          reads=["acol"], writes=["acol"])
        E("v", lambda v: v.memset(hstate[:, :], 0.0),
          writes=[f"hst{i}" for i in range(2 * NST)])

        wsrc = [wgT_d, wuT_d, wdT_d]
        for m in range(3):
            E("q", lambda q, m=m: q.dma_start(
                out=wq_ld[:, :].rearrange("p (k c) -> p k c", k=8),
                in_=wsrc[m].rearrange("(k p) c -> p k c", p=128)),
              writes=["wq_ld"], dma=True, chan="qwl")
            E("s", lambda s, m=m: s.activation(
                wq_r[:, :], wq_ld[:, :], AF.Abs,
                accum_out=wsum_sb[:, m:m + 1]),
              reads=["wq_ld"], writes=["wq_r", f"wsum{m}"])
        for m in range(3):
            E("q", lambda q, m=m: q.dma_start(
                out=wrow_sb[:, :], in_=wsum_sb[:, m:m + 1],
                allow_slow_non_contiguous=True),
              reads=[f"wsum{m}"], writes=["wrow"], dma=True, chan="qwr")
            E("v", lambda v, m=m: v.reduce_sum(
                wsc_sb[0:1, m:m + 1], wrow_sb[:, :], axis=mybir.AxisListType.X),
              reads=["wrow"], writes=["wsc"])
        E("q", lambda q: q.dma_start(out=cc_s_in[0:1, :], in_=wsc_sb[0:1, 0:4]),
          reads=["wsc"], writes=["ccsi"], dma=True, chan="qss")
        E("g", lambda g: g.collective_compute(
            "AllReduce", ALU.add, replica_groups=GRPS,
            ins=[cc_s_in.ap().opt()], outs=[cc_s_out.ap().opt()]),
          reads=["ccsi"], writes=["ccso"])
        E("q", lambda q: q.dma_start(out=wsc_sb[0:1, 0:4], in_=cc_s_out[0:1, :]),
          reads=["ccso"], writes=["wsc"], dma=True, chan="qss")
        E("v", lambda v: v.tensor_scalar(
            wsc_sb[0:1, 4:8], wsc_sb[0:1, 0:4], 1.0 / NW, 1e-5,
            op0=ALU.mult, op1=ALU.max),
          reads=["wsc"], writes=["wsc"])
        E("v", lambda v: v.reciprocal(wsc_sb[0:1, 0:4], wsc_sb[0:1, 4:8]),
          reads=["wsc"], writes=["wsc"])
        E("q", lambda q: q.dma_start(out=scal_dr[0:1, :], in_=wsc_sb[0:1, :]),
          reads=["wsc"], writes=["scal"], dma=True, chan="qsc")
        E("q", lambda q: q.dma_start(out=wscb_sb[:, :],
                                     in_=scal_dr.broadcast_to([128, 8])),
          reads=["scal"], writes=["wscb"], dma=True, chan="qsc")
        for m in range(3):
            E("q", lambda q, m=m: q.dma_start(
                out=wq_ld[:, :].rearrange("p (k c) -> p k c", k=8),
                in_=wsrc[m].rearrange("(k p) c -> p k c", p=128)),
              writes=["wq_ld"], dma=True, chan="qwl")
            E("v", lambda v, m=m: v.tensor_scalar(
                wq_ld[:, :], wq_ld[:, :], wscb_sb[:, m:m + 1], MAGIC,
                op0=ALU.mult, op1=ALU.add),
              reads=["wq_ld", "wscb"], writes=["wq_ld"])
            E("v", lambda v: v.tensor_scalar(
                wq_ld[:, :], wq_ld[:, :], MAGIC, None, op0=ALU.subtract),
              reads=["wq_ld"], writes=["wq_ld"])
            for hf in range(2):
                E("v", lambda v, hf=hf: v.tensor_scalar(
                    wq_b[:, 0:4096], wq_ld[:, hf * 4096:(hf + 1) * 4096],
                    1.0, -1.0, op0=ALU.min, op1=ALU.max),
                  reads=["wq_ld"], writes=["wq_b"])
                E("q", lambda q, m=m, hf=hf: q.dma_start(
                    out=cc_w_in[m, :, :].rearrange("(k p) c -> p k c", p=128)
                    [:, hf * 4:(hf + 1) * 4, :],
                    in_=wq_b[:, 0:4096].rearrange("p (k c) -> p k c", k=4)),
                  reads=["wq_b"], writes=["ccwi"], dma=True, chan="qsw")
        E("g", lambda g: g.collective_compute(
            "AllGather", ALU.bypass, replica_groups=GRPS,
            ins=[cc_w_in.ap().opt()], outs=[cc_w_out.ap().opt()]),
          reads=["ccwi"], writes=["ccwo"])

        psT = [ps3, ps4]

        def ln2_apply(src_key, src_tile, st, dst_key, dst_tile, scr):
            E("v", lambda v: v.reduce_sum(
                st[:, 0:1], src_tile, axis=mybir.AxisListType.X),
              reads=[src_key], writes=["fm"])
            E("s", lambda s: s.activation(
                scr[:, :], src_tile, AF.Square, accum_out=st[:, 2:3]),
              reads=[src_key], writes=["fdu", "fs2"])
            E("v", lambda v: v.tensor_scalar(
                st[:, 0:1], st[:, 0:1], 1.0 / D, None, op0=ALU.mult),
              reads=["fm"], writes=["fm"])
            E("v", lambda v: v.tensor_scalar(
                st[:, 2:3], st[:, 2:3], 1.0 / D, EPS, op0=ALU.mult, op1=ALU.add),
              reads=["fs2"], writes=["fs2"])
            E("v", lambda v: v.tensor_tensor(
                st[:, 3:4], st[:, 0:1], st[:, 0:1], op=ALU.mult),
              reads=["fm"], writes=["fm2"])
            E("v", lambda v: v.tensor_tensor(
                st[:, 1:2], st[:, 2:3], st[:, 3:4], op=ALU.subtract),
              reads=["fs2", "fm2"], writes=["fr"])
            E("s", lambda s: s.activation(st[:, 1:2], st[:, 1:2], AF.Ln),
              reads=["fr"], writes=["fr"])
            E("s", lambda s: s.activation(st[:, 1:2], st[:, 1:2], AF.Exp,
                                          scale=-0.5),
              reads=["fr"], writes=["fr"])
            E("v", lambda v: v.tensor_scalar(
                dst_tile, src_tile, st[:, 0:1], st[:, 1:2],
                op0=ALU.subtract, op1=ALU.mult),
              reads=[src_key, "fm", "fr"], writes=[dst_key])

        psG = [ps1, ps5]
        psU = [ps2, ps6]
        psD_ = [ps7, ps5]

        def ffn_block_emit(bl):
                fb0 = bl * FBL
                for tt in range(NTT):
                    r0 = fb0 + tt * 128
                    ych_ = 2 * bl + tt
                    E("q", lambda q, tt=tt, ych_=ych_: q.dma_start(
                        out=yg_sb[:, tt * D:(tt + 1) * D]
                        .rearrange("p (j c) -> p j c", j=4),
                        in_=ag_out[ych_].ap()[:, bass.ts(PID[0], 128), :]
                        .transpose([1, 0, 2])),
                      reads=[f"ag{ych_}"], writes=[f"yg{tt}"], dma=True,
                      chan=f"qy{tt}")
                    yield 1
                    E("q", lambda q, r0=r0, tt=tt: q.dma_start(
                        out=uf_sb[:, tt * D:(tt + 1) * D],
                        in_=u_tok[r0:r0 + 128, :]),
                      writes=[f"uf{tt}"], dma=True, chan=f"qf{tt}")
                    yield 1
                    uft = uf_sb[:, tt * D:(tt + 1) * D]
                    E("v", lambda v, tt=tt, uft=uft: v.tensor_tensor(
                        x2_sb[:, tt * D:(tt + 1) * D], uft,
                        yg_sb[:, tt * D:(tt + 1) * D], op=ALU.add),
                      reads=[f"uf{tt}", f"yg{tt}"], writes=[f"x2{tt}"])
                    yield 1
                    ln2_apply(f"x2{tt}", x2_sb[:, tt * D:(tt + 1) * D], fst_sb,
                              "fsq", fsq_sb[:, :], fdu_sb)
                    E("v", lambda v: v.reduce_max(
                        fst_sb[:, 4:5], fsq_sb[:, :], axis=mybir.AxisListType.X,
                        apply_absolute_value=True),
                      reads=["fsq"], writes=["fmax"])
                    yield 1
                    E("v", lambda v: v.tensor_scalar_max(
                        fst_sb[:, 4:5], fst_sb[:, 4:5], 1e-5),
                      reads=["fmax"], writes=["fmax"])
                    yield 1
                    E("v", lambda v: v.reciprocal(fst_sb[:, 5:6], fst_sb[:, 4:5]),
                      reads=["fmax"], writes=["fsx"])
                    yield 1
                    E("v", lambda v: v.tensor_scalar(
                        fst_sb[:, 5:6], fst_sb[:, 5:6], 127.0, None, op0=ALU.mult),
                      reads=["fsx"], writes=["fsx"])
                    yield 1
                    E("v", lambda v, tt=tt: v.tensor_scalar(
                        fst_sb[:, 6 + tt:7 + tt], fst_sb[:, 4:5], 1.0 / 127.0, None,
                        op0=ALU.mult),
                      reads=["fmax"], writes=[f"fisx{tt}"])
                    yield 1
                    E("v", lambda v: v.tensor_scalar(
                        fdu_sb[:, :], fsq_sb[:, :], fst_sb[:, 5:6], MAGIC,
                        op0=ALU.mult, op1=ALU.add),
                      reads=["fsq", "fsx"], writes=["fdu"])
                    yield 1
                    E("v", lambda v, tt=tt: v.tensor_scalar(
                        xq_sb[:, tt * D:(tt + 1) * D], fdu_sb[:, :], MAGIC, None,
                        op0=ALU.subtract),
                      reads=["fdu"], writes=[f"xq{tt}"])
                    yield 1
                    for k in range(8):
                        pst = psT[k % 2]
                        E("t", lambda t, k=k, tt=tt, pst=pst: t.transpose(
                            pst.bitcast(BF16)[:, 0:128],
                            xq_sb[:, tt * D + k * 128: tt * D + (k + 1) * 128],
                            eyeb_sb[:, :]),
                          reads=[f"xq{tt}", "eyeb"], writes=[f"psT{k % 2}"])
                        yield 1
                        E("s", lambda s, k=k, tt=tt, pst=pst: s.activation(
                            xqT_sb[:, k * FBL + tt * 128: k * FBL + (tt + 1) * 128],
                            pst.bitcast(BF16)[:, 0:128], AF.Copy),
                          reads=[f"psT{k % 2}"], writes=[f"xqT{k}"])
                        yield 1
                    E("v", lambda v, tt=tt: v.tensor_tensor(
                        fst_sb[:, 8 + tt:9 + tt], fst_sb[:, 6 + tt:7 + tt],
                        wscb_sb[:, 4:5], op=ALU.mult),
                      reads=[f"fisx{tt}", "wscb"], writes=[f"sgv{tt}"])
                    yield 1
                    E("v", lambda v, tt=tt: v.tensor_tensor(
                        fst_sb[:, 10 + tt:11 + tt], fst_sb[:, 6 + tt:7 + tt],
                        wscb_sb[:, 5:6], op=ALU.mult),
                      reads=[f"fisx{tt}", "wscb"], writes=[f"suv{tt}"])
                    yield 1
                for ic in range(8):
                    sl_ = ic % 2
                    W0 = sl_ * 8 * 512
                    E("q", lambda q, ic=ic, W0=W0: q.dma_start(
                        out=wA_sb[:, W0:W0 + 8 * 512].rearrange("p (k c) -> p k c", k=8),
                        in_=cc_w_out[ic // 2, 0, :, (ic % 2) * 512:(ic % 2 + 1) * 512]
                        .rearrange("(k p) c -> p k c", p=128)),
                      reads=["ccwo"], writes=[f"wA{sl_}"], dma=True, chan=f"qw{sl_}")
                    yield 1
                    E("q", lambda q, ic=ic, W0=W0: q.dma_start(
                        out=wB_sb[:, W0:W0 + 8 * 512].rearrange("p (k c) -> p k c", k=8),
                        in_=cc_w_out[ic // 2, 1, :, (ic % 2) * 512:(ic % 2 + 1) * 512]
                        .rearrange("(k p) c -> p k c", p=128)),
                      reads=["ccwo"], writes=[f"wB{sl_}"], dma=True, chan=f"qw2{sl_}")
                    yield 1
                    for tt in range(NTT):
                        for k in range(8):
                            E("t", lambda t, k=k, tt=tt, W0=W0: t.matmul(
                                psG[tt][:, :],
                                xqT_sb[:, k * FBL + tt * 128:
                                       k * FBL + (tt + 1) * 128],
                                wA_sb[:, W0 + k * 512:W0 + (k + 1) * 512],
                                start=(k == 0), stop=(k == 7)),
                              reads=[f"xqT{k}", f"wA{sl_}"], writes=["ps1" if tt == 0 else "ps5"])
                            yield 1
                        for k in range(8):
                            E("t", lambda t, k=k, tt=tt, W0=W0: t.matmul(
                                psU[tt][:, :],
                                xqT_sb[:, k * FBL + tt * 128:
                                       k * FBL + (tt + 1) * 128],
                                wB_sb[:, W0 + k * 512:W0 + (k + 1) * 512],
                                start=(k == 0), stop=(k == 7)),
                              reads=[f"xqT{k}", f"wB{sl_}"], writes=["ps2" if tt == 0 else "ps6"])
                            yield 1
                        E("s", lambda s, tt=tt: s.activation(
                            gate_sb[:, :], psG[tt][:, :], AF.Silu,
                            scale=fst_sb[:, 8 + tt:9 + tt]),
                          reads=[("ps1" if tt == 0 else "ps5"), f"sgv{tt}"], writes=["gate"])
                        yield 1
                        E("v", lambda v, tt=tt, ic=ic: v.scalar_tensor_tensor(
                            g_sb[:, tt * INNER + ic * 512:
                                 tt * INNER + (ic + 1) * 512],
                            psU[tt][:, :], fst_sb[:, 10 + tt:11 + tt],
                            gate_sb[:, :], op0=ALU.mult, op1=ALU.mult),
                          reads=[("ps2" if tt == 0 else "ps6"), f"suv{tt}", "gate"],
                          writes=[f"g{tt}_{ic}"])
                        yield 1
                for tt in range(NTT):
                    E("v", lambda v, tt=tt: v.reduce_max(
                        fst_sb[:, 12 + tt:13 + tt],
                        g_sb[:, tt * INNER:(tt + 1) * INNER],
                        axis=mybir.AxisListType.X, apply_absolute_value=True),
                      reads=[f"g{tt}_{ic}" for ic in range(8)], writes=[f"gmax{tt}"])
                    yield 1
                    E("v", lambda v, tt=tt: v.tensor_scalar_max(
                        fst_sb[:, 12 + tt:13 + tt], fst_sb[:, 12 + tt:13 + tt],
                        1e-5),
                      reads=[f"gmax{tt}"], writes=[f"gmax{tt}"])
                    yield 1
                    E("v", lambda v, tt=tt: v.reciprocal(
                        fst_sb[:, 14 + tt:15 + tt], fst_sb[:, 12 + tt:13 + tt]),
                      reads=[f"gmax{tt}"], writes=[f"gsx{tt}"])
                    yield 1
                    E("v", lambda v, tt=tt: v.tensor_scalar(
                        fst_sb[:, 14 + tt:15 + tt], fst_sb[:, 14 + tt:15 + tt],
                        127.0, None, op0=ALU.mult),
                      reads=[f"gsx{tt}"], writes=[f"gsx{tt}"])
                    yield 1
                    E("v", lambda v, tt=tt: v.tensor_scalar(
                        fst_sb[:, 12 + tt:13 + tt], fst_sb[:, 12 + tt:13 + tt],
                        1.0 / 127.0, None, op0=ALU.mult),
                      reads=[f"gmax{tt}"], writes=[f"gisx{tt}"])
                    yield 1
                    E("v", lambda v, tt=tt: v.tensor_tensor(
                        fst_sb[:, 12 + tt:13 + tt], fst_sb[:, 12 + tt:13 + tt],
                        wscb_sb[:, 6:7], op=ALU.mult),
                      reads=[f"gisx{tt}", "wscb"], writes=[f"sdv{tt}"])
                    yield 1
                    for hf in range(4):
                        E("v", lambda v, tt=tt, hf=hf: v.tensor_scalar(
                            fscr_sb[:, :],
                            g_sb[:, tt * INNER + hf * 1024:
                                 tt * INNER + (hf + 1) * 1024],
                            fst_sb[:, 14 + tt:15 + tt], MAGIC,
                            op0=ALU.mult, op1=ALU.add),
                          reads=[f"g{tt}_{ic}" for ic in range(8)] + [f"gsx{tt}"],
                          writes=["fdu"])
                        yield 1
                        E("v", lambda v, tt=tt, hf=hf: v.tensor_scalar(
                            gq_sb[:, tt * INNER + hf * 1024:
                                  tt * INNER + (hf + 1) * 1024],
                            fscr_sb[:, :], MAGIC, None, op0=ALU.subtract),
                          reads=["fdu"],
                      writes=[f"gq{tt}", f"gqg{tt}{hf % 2}"])
                        yield 1
                    for k in range(32):
                        pst = psT[k % 2]
                        E("t", lambda t, k=k, tt=tt, pst=pst: t.transpose(
                            pst.bitcast(BF16)[:, 0:128],
                            gq_sb[:, tt * INNER + k * 128:
                                  tt * INNER + (k + 1) * 128],
                            eyeb_sb[:, :]),
                          reads=[f"gq{tt}", f"g{tt}_0", "eyeb"],
                          writes=[f"psT{k % 2}"])
                        yield 1
                        E("s", lambda s, k=k, tt=tt, pst=pst: s.activation(
                            gqT_sb[:, k * FBL + tt * 128: k * FBL + (tt + 1) * 128],
                            pst.bitcast(BF16)[:, 0:128], AF.Copy),
                          reads=[f"psT{k % 2}"], writes=[f"gqT{k}_{tt}"])
                        yield 1
                yield "gq"
                for dh in range(2):
                    for kg in range(4):
                        sl_ = kg % 2
                        W0 = sl_ * 8 * 512
                        E("q", lambda q, dh=dh, kg=kg, W0=W0: q.dma_start(
                            out=wA_sb[:, W0:W0 + 8 * 512].rearrange("p (k c) -> p k c", k=8),
                            in_=cc_w_out[kg, 2, :, dh * 512:(dh + 1) * 512]
                            .rearrange("(k p) c -> p k c", p=128)),
                          reads=["ccwo"], writes=[f"wA{sl_}"], dma=True, chan=f"qw{sl_}")
                        yield 1
                        for tt in range(NTT):
                            for k8 in range(8):
                                k = kg * 8 + k8
                                E("t", lambda t, k=k, k8=k8, tt=tt, dh=dh, W0=W0:
                                  t.matmul(
                                      psD_[tt][:, :],
                                      gqT_sb[:, k * FBL + tt * 128:
                                             k * FBL + (tt + 1) * 128],
                                      wA_sb[:, W0 + k8 * 512:W0 + (k8 + 1) * 512],
                                      start=(k == 0), stop=(k == 31)),
                                  reads=[f"gqT{k}_{tt}", f"wA{sl_}"],
                                  writes=["ps7" if tt == 0 else "ps5"])
                                yield 1
                    for tt in range(NTT):
                        E("v", lambda v, tt=tt, dh=dh: v.scalar_tensor_tensor(
                            x2_sb[:, tt * D + dh * 512: tt * D + (dh + 1) * 512],
                            psD_[tt][:, :], fst_sb[:, 12 + tt:13 + tt],
                            x2_sb[:, tt * D + dh * 512: tt * D + (dh + 1) * 512],
                            op0=ALU.mult, op1=ALU.add),
                          reads=[("ps7" if tt == 0 else "ps5"), f"sdv{tt}",
                                 f"x2{tt}"],
                          writes=[f"x2{tt}"])
                        yield 1
                        E("q", lambda q, tt=tt, dh=dh, fb0=fb0: q.dma_start(
                            out=out_d[fb0 + tt * 128: fb0 + (tt + 1) * 128,
                                      dh * 512:(dh + 1) * 512],
                            in_=x2_sb[:, tt * D + dh * 512:
                                      tt * D + (dh + 1) * 512]),
                          reads=[f"x2{tt}"], writes=["outdr"], dma=True,
                          chan="qso")
                        yield 1



                yield "end"

        _gens = [ffn_block_emit(bl) for bl in range(NFB)]
        _gpos = [0]

        def drive_ffn(ch, k_yield=40, drain=False):
            while _gpos[0] < NFB:
                bl = _gpos[0]
                if not drain and ch < 2 * bl + 1:
                    return
                g = _gens[bl]
                steps = 0
                for _ in g:
                    steps += 1
                    if not drain and steps >= k_yield:
                        return
                _gpos[0] += 1

        def pre_emit(ch):
            cp = ch % 2
            tok0 = ch * Q
            for tt in range(4):
                ut = u_ch[:, tt * D:(tt + 1) * D]
                E("q", lambda q, tok0=tok0, tt=tt, ut=ut: q.dma_start(
                    out=ut, in_=u_full[tok0 + tt * 128: tok0 + (tt + 1) * 128, :]),
                  writes=[f"u{tt}", f"xn{tt}"]
                  + (["wrow"] if ch == 0 and tt == 0 else []),
                  dma=True, chan=f"qu{tt}")
                yield 1
                E("v", lambda v, tt=tt, ut=ut: v.reduce_sum(
                    st_ch[:, tt:tt + 1], ut, axis=mybir.AxisListType.X),
                  reads=[f"u{tt}"], writes=["stm"])
                yield 1
                E("s", lambda s, tt=tt, ut=ut: s.activation(
                    sq_ch[:, :], ut, AF.Square, accum_out=st_ch[:, 4 + tt:5 + tt]),
                  reads=[f"u{tt}"],
                  writes=["stsq", "sts2", "xnT0", "xnT1", "xnT2", "xnT3"])
                yield 1
            E("v", lambda v: v.tensor_scalar(
                st_ch[:, 0:4], st_ch[:, 0:4], 1.0 / D, None, op0=ALU.mult),
              reads=["stm"], writes=["stm"])
            E("v", lambda v: v.tensor_scalar(
                st_ch[:, 4:8], st_ch[:, 4:8], 1.0 / D, EPS,
                op0=ALU.mult, op1=ALU.add),
              reads=["sts2"], writes=["sts2"])
            E("v", lambda v: v.tensor_tensor(
                st_ch[:, 8:12], st_ch[:, 0:4], st_ch[:, 0:4], op=ALU.mult),
              reads=["stm"], writes=["stm2"])
            E("v", lambda v: v.tensor_tensor(
                st_ch[:, 4:8], st_ch[:, 4:8], st_ch[:, 8:12], op=ALU.subtract),
              reads=["sts2", "stm2"], writes=["str"])
            yield 1
            E("s", lambda s: s.activation(st_ch[:, 4:8], st_ch[:, 4:8], AF.Ln),
              reads=["str"], writes=["str"])
            E("s", lambda s: s.activation(st_ch[:, 4:8], st_ch[:, 4:8], AF.Exp,
                                          scale=-0.5),
              reads=["str"], writes=["str"])
            yield 1
            for tt in range(4):
                ut = u_ch[:, tt * D:(tt + 1) * D]
                E("v", lambda v, tt=tt, ut=ut: v.tensor_scalar(
                    ut, ut, st_ch[:, tt:tt + 1], st_ch[:, 4 + tt:5 + tt],
                    op0=ALU.subtract, op1=ALU.mult),
                  reads=[f"u{tt}", "stm", "str"], writes=[f"u{tt}", f"xn{tt}"])
                yield 1
                for k in range(8):
                    pst = psT[k % 2]
                    E("t", lambda t, ut=ut, k=k, pst=pst: t.transpose(
                        pst[:, 0:128], ut[:, k * 128:(k + 1) * 128],
                        eye_sb[:, :]),
                      reads=[f"xn{tt}", "eye"], writes=[f"psT{k % 2}"])
                    E("s", lambda s, k=k, tt=tt, pst=pst: s.activation(
                        xnT_sb[:, k * Q + tt * 128: k * Q + (tt + 1) * 128],
                        pst[:, 0:128], AF.Copy),
                      reads=[f"psT{k % 2}"], writes=[f"xnT{k}", "xnTall"])
                    yield 1
                    if k < 2:
                        E("s", lambda s, k=k, tt=tt, cp=cp, pst=pst:
                          s.activation(
                              xnsl_sb[:, cp * 2 * Q + k * Q + tt * 128:
                                      cp * 2 * Q + k * Q + (tt + 1) * 128],
                              pst[:, 0:128], AF.Copy),
                          reads=[f"psT{k % 2}"], writes=[f"xnsl{cp}{k}"])
                        yield 1
            for k in range(8):
                E("t", lambda t, k=k: t.matmul(
                    ps0[0:RK, 0:Q], wxdtT_sb[:, k * RK:(k + 1) * RK],
                    xnT_sb[:, k * Q:(k + 1) * Q],
                    start=(k == 0), stop=(k == 7)),
                  reads=[f"xnT{k}", "xnTall", "wxdtT"], writes=["ps0"])
                yield 1
            E("s", lambda s: s.activation(
                xdt_sb[:, :], ps0[0:RK, 0:Q], AF.Identity, bias=bxdt_sb[:, :]),
              reads=["ps0", "bxdt"], writes=["xdt"])
            yield 1
            for k in range(8):
                E("t", lambda t, k=k: t.matmul(
                    ps0[0:NST, 0:Q], wbT_sb[:, k * NST:(k + 1) * NST],
                    xnT_sb[:, k * Q:(k + 1) * Q],
                    start=(k == 0), stop=(k == 7)),
                  reads=[f"xnT{k}", "xnTall", "wbT"], writes=["ps0"])
                yield 1
            E("s", lambda s: s.activation(
                bcq_sb[:, :], ps0[0:NST, 0:Q], AF.Identity, bias=bb_sb[:, :]),
              reads=["ps0", "bbias"], writes=["bcq"])
            E("q", lambda q, tok0=tok0: q.dma_start(
                out=bc_B[:, tok0:tok0 + Q], in_=bcq_sb[:, :]),
              reads=["bcq"], writes=["bcBdr"], dma=True, chan="qsB")
            yield 1
            for k in range(8):
                E("t", lambda t, k=k: t.matmul(
                    ps0[0:NST, 0:Q], wcT_sb[:, k * NST:(k + 1) * NST],
                    xnT_sb[:, k * Q:(k + 1) * Q],
                    start=(k == 0), stop=(k == 7)),
                  reads=[f"xnT{k}", "xnTall", "wcT"], writes=["ps0"])
                yield 1
            E("s", lambda s: s.activation(
                bcq_sb[:, :], ps0[0:NST, 0:Q], AF.Identity, bias=bc_sb[:, :]),
              reads=["ps0", "cbias"], writes=["ccq", "bcq"])
            E("q", lambda q, tok0=tok0: q.dma_start(
                out=bc_C[:, tok0:tok0 + Q], in_=bcq_sb[:, :]),
              reads=["ccq"], writes=["bcCdr"], dma=True, chan="qsC")
            yield 1
            for dti in range(2):
                psd = ps0
                E("t", lambda t, dti=dti, psd=psd: t.matmul(
                    psd[:, 0:Q], wdtT_sb[:, dti * 128:(dti + 1) * 128],
                    xdt_sb[:, :], start=True, stop=True),
                  reads=["xdt", "wdtT"], writes=["ps0"])
                yield 1
                E("s", lambda s, dti=dti, psd=psd: s.activation(
                    edt_sb[:, :], psd[:, 0:Q], AF.Exp,
                    bias=bdt_sb[:, dti:dti + 1]),
                  reads=["ps0", "bdt"], writes=["stsq", "xnT0", "xnT1"])
                yield 1
                E("s", lambda s, dti=dti, cp=cp: s.activation(
                    dt_sb[:, cp * 2 * Q + dti * Q:
                          cp * 2 * Q + (dti + 1) * Q], edt_sb[:, :], AF.Ln,
                    bias=1.0),
                  reads=["stsq"], writes=[f"dt{cp}{dti}"])
                yield 1
                E("g", lambda g, dti=dti, cp=cp: g.tensor_tensor(
                    c_sb[:, cp * 2 * Q + dti * Q:cp * 2 * Q + (dti + 1) * Q],
                    dt_sb[:, cp * 2 * Q + dti * Q:cp * 2 * Q + (dti + 1) * Q],
                    xnsl_sb[:, cp * 2 * Q + dti * Q:
                            cp * 2 * Q + (dti + 1) * Q], op=ALU.mult),
                  reads=[f"dt{cp}{dti}", f"xnsl{cp}{dti}"],
                  writes=[f"c{cp}{dti}"])
                yield 1

        PG = {"g": iter(())}

        def drive_pre(k):
            for _ in range(k):
                if next(PG["g"], None) is None:
                    return

        PG["g"] = pre_emit(0)
        drive_pre(10 ** 9)

        for ch in range(NCH):
            tok0 = ch * Q
            cp = ch % 2
            PG["g"] = pre_emit(ch + 1) if ch + 1 < NCH else iter(())
            for n in range(NST + 1):
                if n < NST:
                    bsl = n % 4
                    E("q", lambda q, n=n, tok0=tok0, bsl=bsl: q.dma_start(
                        out=bbc_sb[:, bsl * Q:(bsl + 1) * Q],
                        in_=bc_B[n:n + 1, tok0:tok0 + Q].broadcast_to([128, Q])),
                      reads=["bcBdr"], writes=[f"bbc{bsl}"], dma=True,
                      chan=f"qb{bsl}")
                    E("q", lambda q, n=n, tok0=tok0, bsl=bsl: q.dma_start(
                        out=cbc_sb[:, bsl * Q:(bsl + 1) * Q],
                        in_=bc_C[n:n + 1, tok0:tok0 + Q].broadcast_to([128, Q])),
                      reads=["bcCdr"], writes=[f"cbc{bsl}"], dma=True,
                      chan=f"qc{bsl}")
                    for dti in range(2):
                        asl = (n % 2) * 2 + dti
                        E("s", lambda s, n=n, dti=dti, asl=asl, cp=cp:
                          s.activation(
                              abar_sb[:, asl * Q:(asl + 1) * Q],
                              dt_sb[:, cp * 2 * Q + dti * Q:
                                    cp * 2 * Q + (dti + 1) * Q], AF.Exp,
                              scale=acol_sb[:, dti * NST + n:
                                            dti * NST + n + 1]),
                          reads=[f"dt{cp}{dti}", "acol"], writes=[f"abar{asl}"])
                    for dti in range(2):
                        asl = (n % 2) * 2 + dti
                        E("g", lambda g, dti=dti, asl=asl, bsl=bsl, cp=cp:
                          g.tensor_tensor(
                              in1_sb[:, asl * Q:(asl + 1) * Q],
                              c_sb[:, cp * 2 * Q + dti * Q:
                                   cp * 2 * Q + (dti + 1) * Q],
                              bbc_sb[:, bsl * Q:(bsl + 1) * Q], op=ALU.mult),
                          reads=[f"c{cp}{dti}", f"bbc{bsl}"],
                          writes=[f"in1{asl}"])
                drive_pre(6)
                if n == 0:
                    continue
                m = n - 1
                msl0 = (m % 2) * 2
                for dti in range(2):
                    hsl = msl0 + dti
                    col = dti * NST + m
                    E("v", lambda v, hsl=hsl, col=col:
                      v.tensor_tensor_scan(
                          h_sb[:, hsl * Q:(hsl + 1) * Q],
                          abar_sb[:, hsl * Q:(hsl + 1) * Q],
                          in1_sb[:, hsl * Q:(hsl + 1) * Q],
                          hstate[:, col:col + 1], op0=ALU.mult, op1=ALU.add),
                      reads=[f"abar{hsl}", f"in1{hsl}", f"hst{col}"],
                      writes=[f"h{hsl}"])
                for dti in range(2):
                    hsl = msl0 + dti
                    col = dti * NST + m
                    E("v", lambda v, hsl=hsl, col=col: v.tensor_copy(
                        hstate[:, col:col + 1],
                        h_sb[:, hsl * Q + Q - 1: hsl * Q + Q]),
                      reads=[f"h{hsl}"], writes=[f"hst{col}"])
                ei = m % 2
                heng = "v" if ei == 0 else "g"
                bslm = m % 4
                for dti in range(2):
                    hsl = msl0 + dti
                    ysl = ei * 2 + dti
                    if m < 2:
                        E(heng, lambda e, hsl=hsl, bslm=bslm, ysl=ysl:
                          e.tensor_tensor(
                              ych_sb[:, ysl * Q:(ysl + 1) * Q],
                              h_sb[:, hsl * Q:(hsl + 1) * Q],
                              cbc_sb[:, bslm * Q:(bslm + 1) * Q], op=ALU.mult),
                          reads=[f"h{hsl}", f"cbc{bslm}"], writes=[f"ych{ysl}"])
                    else:
                        E(heng, lambda e, hsl=hsl, bslm=bslm, hcs=ei:
                          e.tensor_tensor(
                              hc_sb[:, hcs * Q:(hcs + 1) * Q],
                              h_sb[:, hsl * Q:(hsl + 1) * Q],
                              cbc_sb[:, bslm * Q:(bslm + 1) * Q], op=ALU.mult),
                          reads=[f"h{hsl}", f"cbc{bslm}"], writes=[f"hc{ei}"])
                        E(heng, lambda e, ysl=ysl, hsl=ei: e.tensor_tensor(
                            ych_sb[:, ysl * Q:(ysl + 1) * Q],
                            ych_sb[:, ysl * Q:(ysl + 1) * Q],
                            hc_sb[:, hsl * Q:(hsl + 1) * Q],
                            op=ALU.add),
                          reads=[f"hc{ei}", f"ych{ysl}"], writes=[f"ych{ysl}"])
            drive_pre(10 ** 9)
            for dti in range(2):
                E("v", lambda v, dti=dti: v.tensor_tensor(
                    ych_sb[:, dti * Q:(dti + 1) * Q],
                    ych_sb[:, dti * Q:(dti + 1) * Q],
                    ych_sb[:, (2 + dti) * Q:(3 + dti) * Q], op=ALU.add),
                  reads=[f"ych{dti}", f"ych{2 + dti}"], writes=[f"ych{dti}"])
                E("v", lambda v, dti=dti, cp=cp: v.scalar_tensor_tensor(
                    ych_sb[:, dti * Q:(dti + 1) * Q],
                    xnsl_sb[:, cp * 2 * Q + dti * Q:
                            cp * 2 * Q + (dti + 1) * Q],
                    dcol_sb[:, dti:dti + 1],
                    ych_sb[:, dti * Q:(dti + 1) * Q],
                    op0=ALU.mult, op1=ALU.add),
                  reads=[f"xnsl{cp}{dti}", "dcol", f"ych{dti}"],
                  writes=[f"ych{dti}"])
            for tt in range(4):
                for dti in range(2):
                    pst = psT[dti]
                    E("t", lambda t, dti=dti, tt=tt, pst=pst: t.transpose(
                        pst[:, 0:128],
                        ych_sb[:, dti * Q + tt * 128: dti * Q + (tt + 1) * 128],
                        eye_sb[:, :]),
                      reads=[f"ych{dti}", "eye"], writes=[f"psT{dti}"])
                    E("s", lambda s, dti=dti, tt=tt, pst=pst: s.activation(
                        ytok_sb[:, (tt % 2) * DSL + dti * 128:
                                (tt % 2) * DSL + (dti + 1) * 128],
                        pst[:, 0:128], AF.Copy),
                      reads=[f"psT{dti}"], writes=[f"ytok{tt % 2}", "xnb"])
                E("q", lambda q, tt=tt, ch=ch: q.dma_start(
                    out=ag_in[ch][tt * 128:(tt + 1) * 128, :],
                    in_=ytok_sb[:, (tt % 2) * DSL:(tt % 2 + 1) * DSL]),
                  reads=[f"ytok{tt % 2}"], writes=[f"ccy{ch}"], dma=True,
                  chan=f"qsy{ch}")
            E("g", lambda g, ch=ch: g.collective_compute(
                "AllGather", ALU.bypass, replica_groups=GRPS,
                ins=[ag_in[ch].ap().opt()],
                outs=[ag_out[ch].ap().opt()]),
              reads=[f"ccy{ch}"], writes=[f"ag{ch}"])
            drive_ffn(ch)

        drive_ffn(NCH, drain=True)

        L.barrier()

        def play(engine, name):
            if name == "q":
                pid = engine.partition_id()
                PID[0] = pid % 4
            for item in L.queue[name]:
                if item[0] == "wait":
                    engine.wait_ge(sems[item[1]], item[2])
                else:
                    _, fn, step, sem_name = item
                    fn(engine).then_inc(sems[sem_name], step)

        @block.vector
        def _(v):
            play(v, "v")

        @block.scalar
        def _(s):
            play(s, "s")

        @block.tensor
        def _(t):
            play(t, "t")

        @block.gpsimd
        def _(g):
            play(g, "g")

        @block.sync
        def _(q):
            play(q, "q")

    return nc


_CACHE = {}


def kernel(L0_attn_u, L0_attn_A_log, L0_attn_W_xdt, L0_attn_b_xdt,
           L0_attn_W_dt, L0_attn_b_dt, L0_attn_W_B, L0_attn_b_B,
           L0_attn_W_C, L0_attn_b_C, L0_attn_D, L0_ffn_w_gate,
           L0_ffn_w_up, L0_ffn_w_down):
    u = np.ascontiguousarray(np.asarray(L0_attn_u, dtype=np.float32))
    eye = np.eye(128, dtype=np.float32)
    in_maps = []
    for c in range(8):
        b, q = c // 4, c % 4
        perm = np.r_[np.arange(q * DSL, (q + 1) * DSL),
                     np.arange(0, q * DSL), np.arange((q + 1) * DSL, D)]
        im = {
            "u_full": np.ascontiguousarray(u[b][:, perm]),
            "u_tok": np.ascontiguousarray(
                u[b].reshape(NCH, 4, 128, D)[:, q].reshape(TQ, D)),
            "wxdtT": np.ascontiguousarray(
                np.asarray(L0_attn_W_xdt, np.float32).T[perm, :]),
            "bxdt": np.asarray(L0_attn_b_xdt, np.float32).reshape(RK, 1),
            "wdtT_sl": np.ascontiguousarray(
                np.asarray(L0_attn_W_dt, np.float32)[q * DSL:(q + 1) * DSL, :].T),
            "bdt_sl": np.ascontiguousarray(
                np.asarray(L0_attn_b_dt, np.float32)[q * DSL:(q + 1) * DSL]
                .reshape(DSL, 1)),
            "wbT": np.ascontiguousarray(
                np.asarray(L0_attn_W_B, np.float32).T[perm, :]),
            "bb": np.asarray(L0_attn_b_B, np.float32).reshape(NST, 1),
            "wcT": np.ascontiguousarray(
                np.asarray(L0_attn_W_C, np.float32).T[perm, :]),
            "bc": np.asarray(L0_attn_b_C, np.float32).reshape(NST, 1),
            "alog_sl": np.ascontiguousarray(
                np.asarray(L0_attn_A_log, np.float32)[q * DSL:(q + 1) * DSL, :]),
            "dvec_sl": np.ascontiguousarray(
                np.asarray(L0_attn_D, np.float32)[q * DSL:(q + 1) * DSL]
                .reshape(DSL, 1)),
            "eye": eye,
            "wgT_q": np.ascontiguousarray(
                np.asarray(L0_ffn_w_gate, np.float32).T[:, q * 1024:(q + 1) * 1024]),
            "wuT_q": np.ascontiguousarray(
                np.asarray(L0_ffn_w_up, np.float32).T[:, q * 1024:(q + 1) * 1024]),
            "wdT_q": np.ascontiguousarray(
                np.asarray(L0_ffn_w_down, np.float32).T[q * 1024:(q + 1) * 1024, :]),
        }
        in_maps.append(im)
    if "nc" not in _CACHE:
        _CACHE["nc"] = build_program()
    nc = _CACHE["nc"]
    res = run_bass_kernel_spmd(nc, in_maps, core_ids=list(range(8)))
    _CACHE["res"] = res
    out = np.empty((B, T, D), dtype=np.float32)
    ov = out.reshape(B, NCH, 4, 128, D)
    for c in range(8):
        b, q = c // 4, c % 4
        ov[b, :, q] = res.results[c]["out"].reshape(NCH, 128, D)
    return out


# version 12
# speedup vs baseline: 1.4593x; 1.0296x over previous
import numpy as np
from contextlib import ExitStack
import concourse.bass as bass
import concourse.mybir as mybir
from concourse.bass_utils import run_bass_kernel_spmd

F32 = mybir.dt.float32
BF16 = mybir.dt.bfloat16
AF = mybir.ActivationFunctionType
ALU = mybir.AluOpType

B, T, D, NST, RK, INNER = 2, 8192, 1024, 16, 64, 4096
DSL = 256
Q = 512
NCH = T // Q
TQ = T // 4
FBL = 256
NFB = TQ // FBL
NTT = FBL // 128
MAGIC = float(np.float32(1.5 * 2 ** 23))
EPS = 1e-5
NW = float(INNER * D)
GRPS = [[0, 1, 2, 3], [4, 5, 6, 7]]


class Ledger:

    def __init__(self, sems, engines):
        self.sems = sems
        self.engines = engines
        self.cnt = {e: 0 for e in sems}
        self.mark = {e: {} for e in engines}
        self.buf = {}
        self.queue = {e: [] for e in engines}

    def emit(self, e, fn, reads=(), writes=(), dma=False, chan=None):
        sem_name = chan if (dma and chan) else e
        assert sem_name in self.sems, sem_name
        need = {}
        for k in reads:
            b = self.buf.get(k)
            if b and b["w"] is not None:
                se, sv = b["w"]
                need[se] = max(need.get(se, -1), sv)
        for k in writes:
            b = self.buf.get(k)
            if b:
                if b["w"] is not None:
                    se, sv = b["w"]
                    need[se] = max(need.get(se, -1), sv)
                for se, sv in b["r"].items():
                    need[se] = max(need.get(se, -1), sv)
        for se in sorted(need):
            sv = need[se]
            if se == e == "t":
                continue
            if se not in self.engines:
                sv = self.cnt.get(se, sv)
            if self.mark[e].get(se, -1) < sv:
                self.queue[e].append(("wait", se, sv))
                self.mark[e][se] = sv
        step = 16 if dma else 1
        self.cnt[sem_name] = self.cnt.get(sem_name, 0) + step
        v = self.cnt[sem_name]
        self.queue[e].append(("op", fn, step, sem_name))
        for k in reads:
            b = self.buf.setdefault(k, {"w": None, "r": {}})
            b["r"][sem_name] = max(b["r"].get(sem_name, -1), v)
        for k in writes:
            self.buf[k] = {"w": (sem_name, v), "r": {}}

    def barrier(self):
        tops = dict(self.cnt)
        for e in self.queue:
            for se, sv in tops.items():
                if se != e and sv > 0 and self.mark[e].get(se, -1) < sv:
                    self.queue[e].append(("wait", se, sv))
                    self.mark[e][se] = sv


def build_program():
    nc = bass.Bass("TRN2", target_bir_lowering=False)

    u_full = nc.dram_tensor("u_full", [T, D], F32, kind="ExternalInput").ap()
    u_tok = nc.dram_tensor("u_tok", [TQ, D], F32, kind="ExternalInput").ap()
    wxdtT_d = nc.dram_tensor("wxdtT", [D, RK], F32, kind="ExternalInput").ap()
    bxdt_d = nc.dram_tensor("bxdt", [RK, 1], F32, kind="ExternalInput").ap()
    wdtT_d = nc.dram_tensor("wdtT_sl", [RK, DSL], F32, kind="ExternalInput").ap()
    bdt_d = nc.dram_tensor("bdt_sl", [DSL, 1], F32, kind="ExternalInput").ap()
    wbT_d = nc.dram_tensor("wbT", [D, NST], F32, kind="ExternalInput").ap()
    bb_d = nc.dram_tensor("bb", [NST, 1], F32, kind="ExternalInput").ap()
    wcT_d = nc.dram_tensor("wcT", [D, NST], F32, kind="ExternalInput").ap()
    bc_d = nc.dram_tensor("bc", [NST, 1], F32, kind="ExternalInput").ap()
    alog_d = nc.dram_tensor("alog_sl", [DSL, NST], F32, kind="ExternalInput").ap()
    dvec_d = nc.dram_tensor("dvec_sl", [DSL, 1], F32, kind="ExternalInput").ap()
    eye_d = nc.dram_tensor("eye", [128, 128], F32, kind="ExternalInput").ap()
    wgT_d = nc.dram_tensor("wgT_q", [D, 1024], F32, kind="ExternalInput").ap()
    wuT_d = nc.dram_tensor("wuT_q", [D, 1024], F32, kind="ExternalInput").ap()
    wdT_d = nc.dram_tensor("wdT_q", [1024, D], F32, kind="ExternalInput").ap()
    out_d = nc.dram_tensor("out", [TQ, D], F32, kind="ExternalOutput").ap()
    bc_B = nc.dram_tensor("bc_B", [NST, T], BF16).ap()
    bc_C = nc.dram_tensor("bc_C", [NST, T], BF16).ap()
    scal_dr = nc.dram_tensor("scal_dr", [1, 8], F32).ap()
    ag_in = [nc.dram_tensor(f"agi{c}", [Q, DSL], F32) for c in range(NCH)]
    ag_out = [nc.dram_tensor(f"ago{c}", [4, Q, DSL], F32) for c in range(NCH)]
    cc_w_in = nc.dram_tensor("ccwi", [3, 1024, 1024], BF16)
    cc_w_out = nc.dram_tensor("ccwo", [4, 3, 1024, 1024], BF16)
    cc_s_in = nc.dram_tensor("ccsi", [1, 4], F32)
    cc_s_out = nc.dram_tensor("ccso", [1, 4], F32)

    _st = ExitStack()
    with _st:
        eye_sb = _st.enter_context(nc.sbuf_tensor("eye_sb", [128, 128], F32))
        eyeb_sb = _st.enter_context(nc.sbuf_tensor("eyeb_sb", [128, 128], BF16))
        wxdtT_sb = _st.enter_context(nc.sbuf_tensor("wxdtT_sb", [128, 8 * RK], BF16))
        wbT_sb = _st.enter_context(nc.sbuf_tensor("wbT_sb", [128, 8 * NST], BF16))
        wcT_sb = _st.enter_context(nc.sbuf_tensor("wcT_sb", [128, 8 * NST], BF16))
        wdtT_sb = _st.enter_context(nc.sbuf_tensor("wdtT_sb", [RK, DSL], BF16))
        bxdt_sb = _st.enter_context(nc.sbuf_tensor("bxdt_sb", [RK, 1], F32))
        bdt_sb = _st.enter_context(nc.sbuf_tensor("bdt_sb", [128, 2], F32))
        dcol_sb = _st.enter_context(nc.sbuf_tensor("dcol_sb", [128, 2], F32))
        bb_sb = _st.enter_context(nc.sbuf_tensor("bb_sb", [NST, 1], F32))
        bc_sb = _st.enter_context(nc.sbuf_tensor("bc_sb", [NST, 1], F32))
        acol_sb = _st.enter_context(nc.sbuf_tensor("acol_sb", [128, 2 * NST], F32))
        hstate = _st.enter_context(nc.sbuf_tensor("hstate", [128, 2 * NST], F32))
        wsum_sb = _st.enter_context(nc.sbuf_tensor("wsum_sb", [128, 4], F32))
        wsc_sb = _st.enter_context(nc.sbuf_tensor("wsc_sb", [1, 8], F32))
        wscb_sb = _st.enter_context(nc.sbuf_tensor("wscb_sb", [128, 8], F32))
        u_ch = _st.enter_context(nc.sbuf_tensor("u_ch", [128, 4 * D], F32))
        ygq_sb = _st.enter_context(nc.sbuf_tensor("ygq_sb", [128, 2 * D], F32))
        ufq_sb = _st.enter_context(nc.sbuf_tensor("ufq_sb", [128, 2 * D], F32))
        fscr_sb = _st.enter_context(nc.sbuf_tensor("fscr_sb", [128, D], F32))
        st_ch = _st.enter_context(nc.sbuf_tensor("st_ch", [128, 12], F32))
        xnT_sb = _st.enter_context(nc.sbuf_tensor("xnT_sb", [128, 8 * Q], BF16))
        xnsl_sb = _st.enter_context(nc.sbuf_tensor("xnsl_sb", [128, 4 * Q], BF16))
        dt_sb = _st.enter_context(nc.sbuf_tensor("dt_sb", [128, 4 * Q], BF16))
        xdt_sb = _st.enter_context(nc.sbuf_tensor("xdt_sb", [RK, Q], BF16))
        bcq_sb = _st.enter_context(nc.sbuf_tensor("bcq_sb", [NST, Q], BF16))
        c_sb = _st.enter_context(nc.sbuf_tensor("c_sb", [128, 4 * Q], BF16))
        bbc_sb = _st.enter_context(nc.sbuf_tensor("bbc_sb", [128, 4 * Q], BF16))
        cbc_sb = _st.enter_context(nc.sbuf_tensor("cbc_sb", [128, 4 * Q], BF16))
        abar_sb = _st.enter_context(nc.sbuf_tensor("abar_sb", [128, 4 * Q], F32))
        in1_sb = _st.enter_context(nc.sbuf_tensor("in1_sb", [128, 4 * Q], F32))
        h_sb = _st.enter_context(nc.sbuf_tensor("h_sb", [128, 4 * Q], F32))
        hc_sb = _st.enter_context(nc.sbuf_tensor("hc_sb", [128, 2 * Q], F32))
        ych_sb = _st.enter_context(nc.sbuf_tensor("ych_sb", [128, 4 * Q], F32))
        ytok_sb = _st.enter_context(nc.sbuf_tensor("ytok_sb", [128, 2 * DSL], F32))
        x2_sb = _st.enter_context(nc.sbuf_tensor("x2_sb", [128, NTT * D], F32))
        fst_sb = _st.enter_context(nc.sbuf_tensor("fst_sb", [128, 16], F32))
        xq_sb = _st.enter_context(nc.sbuf_tensor("xq_sb", [128, NTT * D], BF16))
        xqT_sb = _st.enter_context(nc.sbuf_tensor("xqT_sb", [128, 8 * FBL], BF16))
        wA_sb = _st.enter_context(nc.sbuf_tensor("wA_sb", [128, 2 * 8 * 512], BF16))
        wB_sb = _st.enter_context(nc.sbuf_tensor("wB_sb", [128, 2 * 8 * 512], BF16))
        gate_sb = _st.enter_context(nc.sbuf_tensor("gate_sb", [128, 512], F32))
        arena_a = _st.enter_context(nc.sbuf_tensor("arena_a", [128, 8192], F32))
        arena_b = _st.enter_context(nc.sbuf_tensor("arena_b", [128, 4096], F32))
        g_sb = arena_a.ap()
        wq_ld = arena_a.ap()
        gq_sb = arena_a.ap().bitcast(BF16)[:, 0:NTT * INNER]
        gqT_sb = arena_b.ap().bitcast(BF16)[:, 0:NTT * INNER]
        wq_r = arena_a.ap()
        wq_b = wA_sb.ap()[:, 0:4096]
        yg_sb = ygq_sb.ap()
        uf_sb = ufq_sb.ap()
        sq_ch = xnT_sb.ap().bitcast(F32)[:, 0:D]
        edt_sb = xnT_sb.ap().bitcast(F32)[:, 0:Q]
        xnb_sb = ytok_sb.ap().bitcast(BF16)

        fsq_sb = fscr_sb.ap()
        fdu_sb = fscr_sb.ap()
        wrow_sb = u_ch.ap()[0:1, 0:128]
        ps0 = _st.enter_context(nc.psum_tensor("ps0", [128, 512], F32))
        ps1 = _st.enter_context(nc.psum_tensor("ps1", [128, 512], F32))
        ps2 = _st.enter_context(nc.psum_tensor("ps2", [128, 512], F32))
        ps3 = _st.enter_context(nc.psum_tensor("ps3", [128, 512], F32))
        ps4 = _st.enter_context(nc.psum_tensor("ps4", [128, 512], F32))
        ps5 = _st.enter_context(nc.psum_tensor("ps5", [128, 512], F32))
        ps6 = _st.enter_context(nc.psum_tensor("ps6", [128, 512], F32))
        ps7 = _st.enter_context(nc.psum_tensor("ps7", [128, 512], F32))
        sem_v = _st.enter_context(nc.semaphore("sem_v"))
        sem_s = _st.enter_context(nc.semaphore("sem_s"))
        sem_t = _st.enter_context(nc.semaphore("sem_t"))
        sem_g = _st.enter_context(nc.semaphore("sem_g"))
        sem_q = _st.enter_context(nc.semaphore("sem_q"))
        block = _st.enter_context(nc.Block())
        sems = {"v": sem_v, "s": sem_s, "t": sem_t, "g": sem_g, "q": sem_q}
        CHANS = ["qm", "qwl", "qwr", "qss", "qsc", "qsw", "qu0", "qu1",
                 "qsB", "qsC", "qb0", "qb1", "qb2", "qb3", "qc0", "qc1", "qc2", "qc3",
                 "qsy0", "qsy1", "qsy2", "qsy3", "qsy4", "qsy5", "qsy6", "qsy7", "qsy8", "qsy9", "qsy10", "qsy11", "qsy12", "qsy13", "qsy14", "qsy15", "qdbg", "qd", "qu2", "qu3", "qtr",
                 "qy0", "qy1", "qf0", "qf1", "qw0", "qw1", "qw20", "qw21", "qso"]
        for cn in CHANS:
            sems[cn] = _st.enter_context(nc.semaphore("sem_" + cn))
        L = Ledger(sems, ["v", "s", "t", "g", "q"])
        E = L.emit
        PID = [None]

        E("q", lambda q: q.dma_start(out=eye_sb[:, :], in_=eye_d[:, :]),
          writes=["eye"], dma=True, chan="qm")
        E("v", lambda v: v.tensor_copy(eyeb_sb[:, :], eye_sb[:, :]),
          reads=["eye"], writes=["eyeb"])
        for nm, dst, src, nk, w in [
            ("wxdtT", wxdtT_sb, wxdtT_d, 8, RK),
            ("wbT", wbT_sb, wbT_d, 8, NST),
            ("wcT", wcT_sb, wcT_d, 8, NST),
        ]:
            E("q", lambda q, src=src, nk=nk, w=w: q.dma_start(
                out=wq_ld[:, 0:nk * w].rearrange("p (k r) -> p k r", k=nk),
                in_=src.rearrange("(k p) r -> p k r", p=128)),
              writes=["wq_ld"], dma=True, chan="qwl")
            E("v", lambda v, dst=dst, nk=nk, w=w: v.tensor_copy(
                dst[:, :], wq_ld[:, 0:nk * w]),
              reads=["wq_ld"], writes=[nm])
        E("q", lambda q: q.dma_start(out=wq_ld[0:RK, 0:DSL], in_=wdtT_d[:, :]),
          writes=["wq_ld"], dma=True, chan="qwl")
        E("v", lambda v: v.tensor_copy(wdtT_sb[:, :], wq_ld[0:RK, 0:DSL]),
          reads=["wq_ld"], writes=["wdtT"])
        E("q", lambda q: q.dma_start(out=bxdt_sb[:, :], in_=bxdt_d[:, :]),
          writes=["bxdt"], dma=True, chan="qm")
        E("q", lambda q: q.dma_start(
            out=bdt_sb[:, :].rearrange("p (d c) -> p d c", d=2),
            in_=bdt_d.rearrange("(d p) c -> p d c", p=128),
            allow_slow_non_contiguous=True),
          writes=["bdt"], dma=True, chan="qm")
        E("q", lambda q: q.dma_start(
            out=dcol_sb[:, :].rearrange("p (d c) -> p d c", d=2),
            in_=dvec_d.rearrange("(d p) c -> p d c", p=128),
            allow_slow_non_contiguous=True),
          writes=["dcol"], dma=True, chan="qm")
        E("q", lambda q: q.dma_start(out=bb_sb[:, :], in_=bb_d[:, :]),
          writes=["bbias"], dma=True, chan="qm")
        E("q", lambda q: q.dma_start(out=bc_sb[:, :], in_=bc_d[:, :]),
          writes=["cbias"], dma=True, chan="qm")
        E("q", lambda q: q.dma_start(
            out=acol_sb[:, :].rearrange("p (d n) -> p d n", d=2),
            in_=alog_d.rearrange("(d p) n -> p d n", p=128)),
          writes=["acol"], dma=True, chan="qm")
        E("s", lambda s: s.activation(acol_sb[:, :], acol_sb[:, :], AF.Exp),
          reads=["acol"], writes=["acol"])
        E("v", lambda v: v.tensor_scalar_mul(acol_sb[:, :], acol_sb[:, :], -1.0),
          reads=["acol"], writes=["acol"])
        E("v", lambda v: v.memset(hstate[:, :], 0.0),
          writes=[f"hst{i}" for i in range(2 * NST)])

        wsrc = [wgT_d, wuT_d, wdT_d]
        for m in range(3):
            E("q", lambda q, m=m: q.dma_start(
                out=wq_ld[:, :].rearrange("p (k c) -> p k c", k=8),
                in_=wsrc[m].rearrange("(k p) c -> p k c", p=128)),
              writes=["wq_ld"], dma=True, chan="qwl")
            E("s", lambda s, m=m: s.activation(
                wq_r[:, :], wq_ld[:, :], AF.Abs,
                accum_out=wsum_sb[:, m:m + 1]),
              reads=["wq_ld"], writes=["wq_r", f"wsum{m}"])
        for m in range(3):
            E("q", lambda q, m=m: q.dma_start(
                out=wrow_sb[:, :], in_=wsum_sb[:, m:m + 1],
                allow_slow_non_contiguous=True),
              reads=[f"wsum{m}"], writes=["wrow"], dma=True, chan="qwr")
            E("v", lambda v, m=m: v.reduce_sum(
                wsc_sb[0:1, m:m + 1], wrow_sb[:, :], axis=mybir.AxisListType.X),
              reads=["wrow"], writes=["wsc"])
        E("q", lambda q: q.dma_start(out=cc_s_in[0:1, :], in_=wsc_sb[0:1, 0:4]),
          reads=["wsc"], writes=["ccsi"], dma=True, chan="qss")
        E("g", lambda g: g.collective_compute(
            "AllReduce", ALU.add, replica_groups=GRPS,
            ins=[cc_s_in.ap().opt()], outs=[cc_s_out.ap().opt()]),
          reads=["ccsi"], writes=["ccso"])
        E("q", lambda q: q.dma_start(out=wsc_sb[0:1, 0:4], in_=cc_s_out[0:1, :]),
          reads=["ccso"], writes=["wsc"], dma=True, chan="qss")
        E("v", lambda v: v.tensor_scalar(
            wsc_sb[0:1, 4:8], wsc_sb[0:1, 0:4], 1.0 / NW, 1e-5,
            op0=ALU.mult, op1=ALU.max),
          reads=["wsc"], writes=["wsc"])
        E("v", lambda v: v.reciprocal(wsc_sb[0:1, 0:4], wsc_sb[0:1, 4:8]),
          reads=["wsc"], writes=["wsc"])
        E("q", lambda q: q.dma_start(out=scal_dr[0:1, :], in_=wsc_sb[0:1, :]),
          reads=["wsc"], writes=["scal"], dma=True, chan="qsc")
        E("q", lambda q: q.dma_start(out=wscb_sb[:, :],
                                     in_=scal_dr.broadcast_to([128, 8])),
          reads=["scal"], writes=["wscb"], dma=True, chan="qsc")
        for m in range(3):
            E("q", lambda q, m=m: q.dma_start(
                out=wq_ld[:, :].rearrange("p (k c) -> p k c", k=8),
                in_=wsrc[m].rearrange("(k p) c -> p k c", p=128)),
              writes=["wq_ld"], dma=True, chan="qwl")
            E("v", lambda v, m=m: v.tensor_scalar(
                wq_ld[:, :], wq_ld[:, :], wscb_sb[:, m:m + 1], MAGIC,
                op0=ALU.mult, op1=ALU.add),
              reads=["wq_ld", "wscb"], writes=["wq_ld"])
            E("v", lambda v: v.tensor_scalar(
                wq_ld[:, :], wq_ld[:, :], MAGIC, None, op0=ALU.subtract),
              reads=["wq_ld"], writes=["wq_ld"])
            for hf in range(2):
                E("v", lambda v, hf=hf: v.tensor_scalar(
                    wq_b[:, 0:4096], wq_ld[:, hf * 4096:(hf + 1) * 4096],
                    1.0, -1.0, op0=ALU.min, op1=ALU.max),
                  reads=["wq_ld"], writes=["wq_b"])
                E("q", lambda q, m=m, hf=hf: q.dma_start(
                    out=cc_w_in[m, :, :].rearrange("(k p) c -> p k c", p=128)
                    [:, hf * 4:(hf + 1) * 4, :],
                    in_=wq_b[:, 0:4096].rearrange("p (k c) -> p k c", k=4)),
                  reads=["wq_b"], writes=["ccwi"], dma=True, chan="qsw")
        E("g", lambda g: g.collective_compute(
            "AllGather", ALU.bypass, replica_groups=GRPS,
            ins=[cc_w_in.ap().opt()], outs=[cc_w_out.ap().opt()]),
          reads=["ccwi"], writes=["ccwo"])

        psT = [ps3, ps4]

        def ln2_apply(src_key, src_tile, st, dst_key, dst_tile, scr):
            E("v", lambda v: v.reduce_sum(
                st[:, 0:1], src_tile, axis=mybir.AxisListType.X),
              reads=[src_key], writes=["fm"])
            E("s", lambda s: s.activation(
                scr[:, :], src_tile, AF.Square, accum_out=st[:, 2:3]),
              reads=[src_key], writes=["fdu", "fs2"])
            E("v", lambda v: v.tensor_scalar(
                st[:, 0:1], st[:, 0:1], 1.0 / D, None, op0=ALU.mult),
              reads=["fm"], writes=["fm"])
            E("v", lambda v: v.tensor_scalar(
                st[:, 2:3], st[:, 2:3], 1.0 / D, EPS, op0=ALU.mult, op1=ALU.add),
              reads=["fs2"], writes=["fs2"])
            E("v", lambda v: v.tensor_tensor(
                st[:, 3:4], st[:, 0:1], st[:, 0:1], op=ALU.mult),
              reads=["fm"], writes=["fm2"])
            E("v", lambda v: v.tensor_tensor(
                st[:, 1:2], st[:, 2:3], st[:, 3:4], op=ALU.subtract),
              reads=["fs2", "fm2"], writes=["fr"])
            E("s", lambda s: s.activation(st[:, 1:2], st[:, 1:2], AF.Ln),
              reads=["fr"], writes=["fr"])
            E("s", lambda s: s.activation(st[:, 1:2], st[:, 1:2], AF.Exp,
                                          scale=-0.5),
              reads=["fr"], writes=["fr"])
            E("v", lambda v: v.tensor_scalar(
                dst_tile, src_tile, st[:, 0:1], st[:, 1:2],
                op0=ALU.subtract, op1=ALU.mult),
              reads=[src_key, "fm", "fr"], writes=[dst_key])

        psG = [ps1, ps5]
        psU = [ps2, ps6]
        psD_ = [ps7, ps5]

        def ffn_block_emit(bl):
                fb0 = bl * FBL
                for tt in range(NTT):
                    r0 = fb0 + tt * 128
                    ych_ = 2 * bl + tt
                    E("q", lambda q, tt=tt, ych_=ych_: q.dma_start(
                        out=yg_sb[:, tt * D:(tt + 1) * D]
                        .rearrange("p (j c) -> p j c", j=4),
                        in_=ag_out[ych_].ap()[:, bass.ts(PID[0], 128), :]
                        .transpose([1, 0, 2])),
                      reads=[f"ag{ych_}"], writes=[f"yg{tt}"], dma=True,
                      chan=f"qy{tt}")
                    yield 1
                    E("q", lambda q, r0=r0, tt=tt: q.dma_start(
                        out=uf_sb[:, tt * D:(tt + 1) * D],
                        in_=u_tok[r0:r0 + 128, :]),
                      writes=[f"uf{tt}"], dma=True, chan=f"qf{tt}")
                    yield 1
                    uft = uf_sb[:, tt * D:(tt + 1) * D]
                    E("v", lambda v, tt=tt, uft=uft: v.tensor_tensor(
                        x2_sb[:, tt * D:(tt + 1) * D], uft,
                        yg_sb[:, tt * D:(tt + 1) * D], op=ALU.add),
                      reads=[f"uf{tt}", f"yg{tt}"], writes=[f"x2{tt}"])
                    yield 1
                    ln2_apply(f"x2{tt}", x2_sb[:, tt * D:(tt + 1) * D], fst_sb,
                              "fsq", fsq_sb[:, :], fdu_sb)
                    E("v", lambda v: v.reduce_max(
                        fst_sb[:, 4:5], fsq_sb[:, :], axis=mybir.AxisListType.X,
                        apply_absolute_value=True),
                      reads=["fsq"], writes=["fmax"])
                    yield 1
                    E("v", lambda v: v.tensor_scalar_max(
                        fst_sb[:, 4:5], fst_sb[:, 4:5], 1e-5),
                      reads=["fmax"], writes=["fmax"])
                    yield 1
                    E("v", lambda v: v.reciprocal(fst_sb[:, 5:6], fst_sb[:, 4:5]),
                      reads=["fmax"], writes=["fsx"])
                    yield 1
                    E("v", lambda v: v.tensor_scalar(
                        fst_sb[:, 5:6], fst_sb[:, 5:6], 127.0, None, op0=ALU.mult),
                      reads=["fsx"], writes=["fsx"])
                    yield 1
                    E("v", lambda v, tt=tt: v.tensor_scalar(
                        fst_sb[:, 6 + tt:7 + tt], fst_sb[:, 4:5], 1.0 / 127.0, None,
                        op0=ALU.mult),
                      reads=["fmax"], writes=[f"fisx{tt}"])
                    yield 1
                    E("v", lambda v: v.tensor_scalar(
                        fdu_sb[:, :], fsq_sb[:, :], fst_sb[:, 5:6], MAGIC,
                        op0=ALU.mult, op1=ALU.add),
                      reads=["fsq", "fsx"], writes=["fdu"])
                    yield 1
                    E("v", lambda v, tt=tt: v.tensor_scalar(
                        xq_sb[:, tt * D:(tt + 1) * D], fdu_sb[:, :], MAGIC, None,
                        op0=ALU.subtract),
                      reads=["fdu"], writes=[f"xq{tt}"])
                    yield 1
                    for k in range(8):
                        pst = psT[k % 2]
                        E("t", lambda t, k=k, tt=tt, pst=pst: t.transpose(
                            pst.bitcast(BF16)[:, 0:128],
                            xq_sb[:, tt * D + k * 128: tt * D + (k + 1) * 128],
                            eyeb_sb[:, :]),
                          reads=[f"xq{tt}", "eyeb"], writes=[f"psT{k % 2}"])
                        yield 1
                        E("s", lambda s, k=k, tt=tt, pst=pst: s.activation(
                            xqT_sb[:, k * FBL + tt * 128: k * FBL + (tt + 1) * 128],
                            pst.bitcast(BF16)[:, 0:128], AF.Copy),
                          reads=[f"psT{k % 2}"], writes=[f"xqT{k}"])
                        yield 1
                    E("v", lambda v, tt=tt: v.tensor_tensor(
                        fst_sb[:, 8 + tt:9 + tt], fst_sb[:, 6 + tt:7 + tt],
                        wscb_sb[:, 4:5], op=ALU.mult),
                      reads=[f"fisx{tt}", "wscb"], writes=[f"sgv{tt}"])
                    yield 1
                    E("v", lambda v, tt=tt: v.tensor_tensor(
                        fst_sb[:, 10 + tt:11 + tt], fst_sb[:, 6 + tt:7 + tt],
                        wscb_sb[:, 5:6], op=ALU.mult),
                      reads=[f"fisx{tt}", "wscb"], writes=[f"suv{tt}"])
                    yield 1
                for ic in range(8):
                    sl_ = ic % 2
                    W0 = sl_ * 8 * 512
                    E("q", lambda q, ic=ic, W0=W0: q.dma_start(
                        out=wA_sb[:, W0:W0 + 8 * 512].rearrange("p (k c) -> p k c", k=8),
                        in_=cc_w_out[ic // 2, 0, :, (ic % 2) * 512:(ic % 2 + 1) * 512]
                        .rearrange("(k p) c -> p k c", p=128)),
                      reads=["ccwo"], writes=[f"wA{sl_}"], dma=True, chan=f"qw{sl_}")
                    yield 1
                    E("q", lambda q, ic=ic, W0=W0: q.dma_start(
                        out=wB_sb[:, W0:W0 + 8 * 512].rearrange("p (k c) -> p k c", k=8),
                        in_=cc_w_out[ic // 2, 1, :, (ic % 2) * 512:(ic % 2 + 1) * 512]
                        .rearrange("(k p) c -> p k c", p=128)),
                      reads=["ccwo"], writes=[f"wB{sl_}"], dma=True, chan=f"qw2{sl_}")
                    yield 1
                    for tt in range(NTT):
                        for k in range(8):
                            E("t", lambda t, k=k, tt=tt, W0=W0: t.matmul(
                                psG[tt][:, :],
                                xqT_sb[:, k * FBL + tt * 128:
                                       k * FBL + (tt + 1) * 128],
                                wA_sb[:, W0 + k * 512:W0 + (k + 1) * 512],
                                start=(k == 0), stop=(k == 7)),
                              reads=[f"xqT{k}", f"wA{sl_}"], writes=["ps1" if tt == 0 else "ps5"])
                            yield 1
                        for k in range(8):
                            E("t", lambda t, k=k, tt=tt, W0=W0: t.matmul(
                                psU[tt][:, :],
                                xqT_sb[:, k * FBL + tt * 128:
                                       k * FBL + (tt + 1) * 128],
                                wB_sb[:, W0 + k * 512:W0 + (k + 1) * 512],
                                start=(k == 0), stop=(k == 7)),
                              reads=[f"xqT{k}", f"wB{sl_}"], writes=["ps2" if tt == 0 else "ps6"])
                            yield 1
                        E("s", lambda s, tt=tt: s.activation(
                            gate_sb[:, :], psG[tt][:, :], AF.Silu,
                            scale=fst_sb[:, 8 + tt:9 + tt]),
                          reads=[("ps1" if tt == 0 else "ps5"), f"sgv{tt}"], writes=["gate"])
                        yield 1
                        E("v", lambda v, tt=tt, ic=ic: v.scalar_tensor_tensor(
                            g_sb[:, tt * INNER + ic * 512:
                                 tt * INNER + (ic + 1) * 512],
                            psU[tt][:, :], fst_sb[:, 10 + tt:11 + tt],
                            gate_sb[:, :], op0=ALU.mult, op1=ALU.mult),
                          reads=[("ps2" if tt == 0 else "ps6"), f"suv{tt}", "gate"],
                          writes=[f"g{tt}_{ic}"])
                        yield 1
                for tt in range(NTT):
                    E("v", lambda v, tt=tt: v.reduce_max(
                        fst_sb[:, 12 + tt:13 + tt],
                        g_sb[:, tt * INNER:(tt + 1) * INNER],
                        axis=mybir.AxisListType.X, apply_absolute_value=True),
                      reads=[f"g{tt}_{ic}" for ic in range(8)], writes=[f"gmax{tt}"])
                    yield 1
                    E("v", lambda v, tt=tt: v.tensor_scalar_max(
                        fst_sb[:, 12 + tt:13 + tt], fst_sb[:, 12 + tt:13 + tt],
                        1e-5),
                      reads=[f"gmax{tt}"], writes=[f"gmax{tt}"])
                    yield 1
                    E("v", lambda v, tt=tt: v.reciprocal(
                        fst_sb[:, 14 + tt:15 + tt], fst_sb[:, 12 + tt:13 + tt]),
                      reads=[f"gmax{tt}"], writes=[f"gsx{tt}"])
                    yield 1
                    E("v", lambda v, tt=tt: v.tensor_scalar(
                        fst_sb[:, 14 + tt:15 + tt], fst_sb[:, 14 + tt:15 + tt],
                        127.0, None, op0=ALU.mult),
                      reads=[f"gsx{tt}"], writes=[f"gsx{tt}"])
                    yield 1
                    E("v", lambda v, tt=tt: v.tensor_scalar(
                        fst_sb[:, 12 + tt:13 + tt], fst_sb[:, 12 + tt:13 + tt],
                        1.0 / 127.0, None, op0=ALU.mult),
                      reads=[f"gmax{tt}"], writes=[f"gisx{tt}"])
                    yield 1
                    E("v", lambda v, tt=tt: v.tensor_tensor(
                        fst_sb[:, 12 + tt:13 + tt], fst_sb[:, 12 + tt:13 + tt],
                        wscb_sb[:, 6:7], op=ALU.mult),
                      reads=[f"gisx{tt}", "wscb"], writes=[f"sdv{tt}"])
                    yield 1
                    for hf in range(4):
                        E("v", lambda v, tt=tt, hf=hf: v.tensor_scalar(
                            fscr_sb[:, :],
                            g_sb[:, tt * INNER + hf * 1024:
                                 tt * INNER + (hf + 1) * 1024],
                            fst_sb[:, 14 + tt:15 + tt], MAGIC,
                            op0=ALU.mult, op1=ALU.add),
                          reads=[f"g{tt}_{ic}" for ic in range(8)] + [f"gsx{tt}"],
                          writes=["fdu"])
                        yield 1
                        E("v", lambda v, tt=tt, hf=hf: v.tensor_scalar(
                            gq_sb[:, tt * INNER + hf * 1024:
                                  tt * INNER + (hf + 1) * 1024],
                            fscr_sb[:, :], MAGIC, None, op0=ALU.subtract),
                          reads=["fdu"],
                      writes=[f"gq{tt}", f"gqg{tt}{hf % 2}"])
                        yield 1
                    for k in range(32):
                        pst = psT[k % 2]
                        E("t", lambda t, k=k, tt=tt, pst=pst: t.transpose(
                            pst.bitcast(BF16)[:, 0:128],
                            gq_sb[:, tt * INNER + k * 128:
                                  tt * INNER + (k + 1) * 128],
                            eyeb_sb[:, :]),
                          reads=[f"gq{tt}", f"g{tt}_0", "eyeb"],
                          writes=[f"psT{k % 2}"])
                        yield 1
                        E("s", lambda s, k=k, tt=tt, pst=pst: s.activation(
                            gqT_sb[:, k * FBL + tt * 128: k * FBL + (tt + 1) * 128],
                            pst.bitcast(BF16)[:, 0:128], AF.Copy),
                          reads=[f"psT{k % 2}"], writes=[f"gqT{k}_{tt}"])
                        yield 1
                yield "gq"
                for dh in range(2):
                    for kg in range(4):
                        sl_ = kg % 2
                        W0 = sl_ * 8 * 512
                        E("q", lambda q, dh=dh, kg=kg, W0=W0: q.dma_start(
                            out=wA_sb[:, W0:W0 + 8 * 512].rearrange("p (k c) -> p k c", k=8),
                            in_=cc_w_out[kg, 2, :, dh * 512:(dh + 1) * 512]
                            .rearrange("(k p) c -> p k c", p=128)),
                          reads=["ccwo"], writes=[f"wA{sl_}"], dma=True, chan=f"qw{sl_}")
                        yield 1
                        for tt in range(NTT):
                            for k8 in range(8):
                                k = kg * 8 + k8
                                E("t", lambda t, k=k, k8=k8, tt=tt, dh=dh, W0=W0:
                                  t.matmul(
                                      psD_[tt][:, :],
                                      gqT_sb[:, k * FBL + tt * 128:
                                             k * FBL + (tt + 1) * 128],
                                      wA_sb[:, W0 + k8 * 512:W0 + (k8 + 1) * 512],
                                      start=(k == 0), stop=(k == 31)),
                                  reads=[f"gqT{k}_{tt}", f"wA{sl_}"],
                                  writes=["ps7" if tt == 0 else "ps5"])
                                yield 1
                    for tt in range(NTT):
                        E("v", lambda v, tt=tt, dh=dh: v.scalar_tensor_tensor(
                            x2_sb[:, tt * D + dh * 512: tt * D + (dh + 1) * 512],
                            psD_[tt][:, :], fst_sb[:, 12 + tt:13 + tt],
                            x2_sb[:, tt * D + dh * 512: tt * D + (dh + 1) * 512],
                            op0=ALU.mult, op1=ALU.add),
                          reads=[("ps7" if tt == 0 else "ps5"), f"sdv{tt}",
                                 f"x2{tt}"],
                          writes=[f"x2{tt}"])
                        yield 1
                        E("q", lambda q, tt=tt, dh=dh, fb0=fb0: q.dma_start(
                            out=out_d[fb0 + tt * 128: fb0 + (tt + 1) * 128,
                                      dh * 512:(dh + 1) * 512],
                            in_=x2_sb[:, tt * D + dh * 512:
                                      tt * D + (dh + 1) * 512]),
                          reads=[f"x2{tt}"], writes=["outdr"], dma=True,
                          chan="qso")
                        yield 1



                yield "end"

        _gens = [ffn_block_emit(bl) for bl in range(NFB)]
        _gpos = [0]

        def drive_ffn(ch, k_yield=120, drain=False):
            while _gpos[0] < NFB:
                bl = _gpos[0]
                if not drain and ch < 2 * bl + 1:
                    return
                g = _gens[bl]
                steps = 0
                for _ in g:
                    steps += 1
                    if not drain and steps >= k_yield:
                        return
                _gpos[0] += 1

        def pre_emit(ch):
            cp = ch % 2
            tok0 = ch * Q
            for tt in range(4):
                ut = u_ch[:, tt * D:(tt + 1) * D]
                E("q", lambda q, tok0=tok0, tt=tt, ut=ut: q.dma_start(
                    out=ut, in_=u_full[tok0 + tt * 128: tok0 + (tt + 1) * 128, :]),
                  writes=[f"u{tt}", f"xn{tt}"]
                  + (["wrow"] if ch == 0 and tt == 0 else []),
                  dma=True, chan=f"qu{tt}")
                yield 1
                E("v", lambda v, tt=tt, ut=ut: v.reduce_sum(
                    st_ch[:, tt:tt + 1], ut, axis=mybir.AxisListType.X),
                  reads=[f"u{tt}"], writes=["stm"])
                yield 1
                E("s", lambda s, tt=tt, ut=ut: s.activation(
                    sq_ch[:, :], ut, AF.Square, accum_out=st_ch[:, 4 + tt:5 + tt]),
                  reads=[f"u{tt}"],
                  writes=["stsq", "sts2", "xnT0", "xnT1", "xnT2", "xnT3"])
                yield 1
            E("v", lambda v: v.tensor_scalar(
                st_ch[:, 0:4], st_ch[:, 0:4], 1.0 / D, None, op0=ALU.mult),
              reads=["stm"], writes=["stm"])
            E("v", lambda v: v.tensor_scalar(
                st_ch[:, 4:8], st_ch[:, 4:8], 1.0 / D, EPS,
                op0=ALU.mult, op1=ALU.add),
              reads=["sts2"], writes=["sts2"])
            E("v", lambda v: v.tensor_tensor(
                st_ch[:, 8:12], st_ch[:, 0:4], st_ch[:, 0:4], op=ALU.mult),
              reads=["stm"], writes=["stm2"])
            E("v", lambda v: v.tensor_tensor(
                st_ch[:, 4:8], st_ch[:, 4:8], st_ch[:, 8:12], op=ALU.subtract),
              reads=["sts2", "stm2"], writes=["str"])
            yield 1
            E("s", lambda s: s.activation(st_ch[:, 4:8], st_ch[:, 4:8], AF.Ln),
              reads=["str"], writes=["str"])
            E("s", lambda s: s.activation(st_ch[:, 4:8], st_ch[:, 4:8], AF.Exp,
                                          scale=-0.5),
              reads=["str"], writes=["str"])
            yield 1
            for tt in range(4):
                ut = u_ch[:, tt * D:(tt + 1) * D]
                E("v", lambda v, tt=tt, ut=ut: v.tensor_scalar(
                    ut, ut, st_ch[:, tt:tt + 1], st_ch[:, 4 + tt:5 + tt],
                    op0=ALU.subtract, op1=ALU.mult),
                  reads=[f"u{tt}", "stm", "str"], writes=[f"u{tt}", f"xn{tt}"])
                yield 1
                for k in range(8):
                    pst = psT[k % 2]
                    E("t", lambda t, ut=ut, k=k, pst=pst: t.transpose(
                        pst[:, 0:128], ut[:, k * 128:(k + 1) * 128],
                        eye_sb[:, :]),
                      reads=[f"xn{tt}", "eye"], writes=[f"psT{k % 2}"])
                    E("s", lambda s, k=k, tt=tt, pst=pst: s.activation(
                        xnT_sb[:, k * Q + tt * 128: k * Q + (tt + 1) * 128],
                        pst[:, 0:128], AF.Copy),
                      reads=[f"psT{k % 2}"], writes=[f"xnT{k}", "xnTall"])
                    yield 1
                    if k < 2:
                        E("s", lambda s, k=k, tt=tt, cp=cp, pst=pst:
                          s.activation(
                              xnsl_sb[:, cp * 2 * Q + k * Q + tt * 128:
                                      cp * 2 * Q + k * Q + (tt + 1) * 128],
                              pst[:, 0:128], AF.Copy),
                          reads=[f"psT{k % 2}"], writes=[f"xnsl{cp}{k}"])
                        yield 1
            for k in range(8):
                E("t", lambda t, k=k: t.matmul(
                    ps0[0:RK, 0:Q], wxdtT_sb[:, k * RK:(k + 1) * RK],
                    xnT_sb[:, k * Q:(k + 1) * Q],
                    start=(k == 0), stop=(k == 7)),
                  reads=[f"xnT{k}", "xnTall", "wxdtT"], writes=["ps0"])
                yield 1
            E("s", lambda s: s.activation(
                xdt_sb[:, :], ps0[0:RK, 0:Q], AF.Identity, bias=bxdt_sb[:, :]),
              reads=["ps0", "bxdt"], writes=["xdt"])
            yield 1
            for k in range(8):
                E("t", lambda t, k=k: t.matmul(
                    ps0[0:NST, 0:Q], wbT_sb[:, k * NST:(k + 1) * NST],
                    xnT_sb[:, k * Q:(k + 1) * Q],
                    start=(k == 0), stop=(k == 7)),
                  reads=[f"xnT{k}", "xnTall", "wbT"], writes=["ps0"])
                yield 1
            E("s", lambda s: s.activation(
                bcq_sb[:, :], ps0[0:NST, 0:Q], AF.Identity, bias=bb_sb[:, :]),
              reads=["ps0", "bbias"], writes=["bcq"])
            E("q", lambda q, tok0=tok0: q.dma_start(
                out=bc_B[:, tok0:tok0 + Q], in_=bcq_sb[:, :]),
              reads=["bcq"], writes=["bcBdr"], dma=True, chan="qsB")
            yield 1
            for k in range(8):
                E("t", lambda t, k=k: t.matmul(
                    ps0[0:NST, 0:Q], wcT_sb[:, k * NST:(k + 1) * NST],
                    xnT_sb[:, k * Q:(k + 1) * Q],
                    start=(k == 0), stop=(k == 7)),
                  reads=[f"xnT{k}", "xnTall", "wcT"], writes=["ps0"])
                yield 1
            E("s", lambda s: s.activation(
                bcq_sb[:, :], ps0[0:NST, 0:Q], AF.Identity, bias=bc_sb[:, :]),
              reads=["ps0", "cbias"], writes=["ccq", "bcq"])
            E("q", lambda q, tok0=tok0: q.dma_start(
                out=bc_C[:, tok0:tok0 + Q], in_=bcq_sb[:, :]),
              reads=["ccq"], writes=["bcCdr"], dma=True, chan="qsC")
            yield 1
            for dti in range(2):
                psd = ps0
                E("t", lambda t, dti=dti, psd=psd: t.matmul(
                    psd[:, 0:Q], wdtT_sb[:, dti * 128:(dti + 1) * 128],
                    xdt_sb[:, :], start=True, stop=True),
                  reads=["xdt", "wdtT"], writes=["ps0"])
                yield 1
                E("s", lambda s, dti=dti, psd=psd: s.activation(
                    edt_sb[:, :], psd[:, 0:Q], AF.Exp,
                    bias=bdt_sb[:, dti:dti + 1]),
                  reads=["ps0", "bdt"], writes=["stsq", "xnT0", "xnT1"])
                yield 1
                E("s", lambda s, dti=dti, cp=cp: s.activation(
                    dt_sb[:, cp * 2 * Q + dti * Q:
                          cp * 2 * Q + (dti + 1) * Q], edt_sb[:, :], AF.Ln,
                    bias=1.0),
                  reads=["stsq"], writes=[f"dt{cp}{dti}"])
                yield 1
                E("g", lambda g, dti=dti, cp=cp: g.tensor_tensor(
                    c_sb[:, cp * 2 * Q + dti * Q:cp * 2 * Q + (dti + 1) * Q],
                    dt_sb[:, cp * 2 * Q + dti * Q:cp * 2 * Q + (dti + 1) * Q],
                    xnsl_sb[:, cp * 2 * Q + dti * Q:
                            cp * 2 * Q + (dti + 1) * Q], op=ALU.mult),
                  reads=[f"dt{cp}{dti}", f"xnsl{cp}{dti}"],
                  writes=[f"c{cp}{dti}"])
                yield 1

        PG = {"g": iter(())}

        def drive_pre(k):
            for _ in range(k):
                if next(PG["g"], None) is None:
                    return

        PG["g"] = pre_emit(0)
        drive_pre(10 ** 9)

        for ch in range(NCH):
            tok0 = ch * Q
            cp = ch % 2
            PG["g"] = pre_emit(ch + 1) if ch + 1 < NCH else iter(())
            for n in range(NST + 1):
                if n < NST:
                    bsl = n % 4
                    E("q", lambda q, n=n, tok0=tok0, bsl=bsl: q.dma_start(
                        out=bbc_sb[:, bsl * Q:(bsl + 1) * Q],
                        in_=bc_B[n:n + 1, tok0:tok0 + Q].broadcast_to([128, Q])),
                      reads=["bcBdr"], writes=[f"bbc{bsl}"], dma=True,
                      chan=f"qb{bsl}")
                    E("q", lambda q, n=n, tok0=tok0, bsl=bsl: q.dma_start(
                        out=cbc_sb[:, bsl * Q:(bsl + 1) * Q],
                        in_=bc_C[n:n + 1, tok0:tok0 + Q].broadcast_to([128, Q])),
                      reads=["bcCdr"], writes=[f"cbc{bsl}"], dma=True,
                      chan=f"qc{bsl}")
                    for dti in range(2):
                        asl = (n % 2) * 2 + dti
                        E("s", lambda s, n=n, dti=dti, asl=asl, cp=cp:
                          s.activation(
                              abar_sb[:, asl * Q:(asl + 1) * Q],
                              dt_sb[:, cp * 2 * Q + dti * Q:
                                    cp * 2 * Q + (dti + 1) * Q], AF.Exp,
                              scale=acol_sb[:, dti * NST + n:
                                            dti * NST + n + 1]),
                          reads=[f"dt{cp}{dti}", "acol"], writes=[f"abar{asl}"])
                    for dti in range(2):
                        asl = (n % 2) * 2 + dti
                        E("g", lambda g, dti=dti, asl=asl, bsl=bsl, cp=cp:
                          g.tensor_tensor(
                              in1_sb[:, asl * Q:(asl + 1) * Q],
                              c_sb[:, cp * 2 * Q + dti * Q:
                                   cp * 2 * Q + (dti + 1) * Q],
                              bbc_sb[:, bsl * Q:(bsl + 1) * Q], op=ALU.mult),
                          reads=[f"c{cp}{dti}", f"bbc{bsl}"],
                          writes=[f"in1{asl}"])
                drive_pre(6)
                if n == 0:
                    continue
                m = n - 1
                msl0 = (m % 2) * 2
                for dti in range(2):
                    hsl = msl0 + dti
                    col = dti * NST + m
                    E("v", lambda v, hsl=hsl, col=col:
                      v.tensor_tensor_scan(
                          h_sb[:, hsl * Q:(hsl + 1) * Q],
                          abar_sb[:, hsl * Q:(hsl + 1) * Q],
                          in1_sb[:, hsl * Q:(hsl + 1) * Q],
                          hstate[:, col:col + 1], op0=ALU.mult, op1=ALU.add),
                      reads=[f"abar{hsl}", f"in1{hsl}", f"hst{col}"],
                      writes=[f"h{hsl}"])
                for dti in range(2):
                    hsl = msl0 + dti
                    col = dti * NST + m
                    E("v", lambda v, hsl=hsl, col=col: v.tensor_copy(
                        hstate[:, col:col + 1],
                        h_sb[:, hsl * Q + Q - 1: hsl * Q + Q]),
                      reads=[f"h{hsl}"], writes=[f"hst{col}"])
                ei = m % 2
                heng = "v" if ei == 0 else "g"
                bslm = m % 4
                for dti in range(2):
                    hsl = msl0 + dti
                    ysl = ei * 2 + dti
                    if m < 2:
                        E(heng, lambda e, hsl=hsl, bslm=bslm, ysl=ysl:
                          e.tensor_tensor(
                              ych_sb[:, ysl * Q:(ysl + 1) * Q],
                              h_sb[:, hsl * Q:(hsl + 1) * Q],
                              cbc_sb[:, bslm * Q:(bslm + 1) * Q], op=ALU.mult),
                          reads=[f"h{hsl}", f"cbc{bslm}"], writes=[f"ych{ysl}"])
                    else:
                        E(heng, lambda e, hsl=hsl, bslm=bslm, hcs=ei:
                          e.tensor_tensor(
                              hc_sb[:, hcs * Q:(hcs + 1) * Q],
                              h_sb[:, hsl * Q:(hsl + 1) * Q],
                              cbc_sb[:, bslm * Q:(bslm + 1) * Q], op=ALU.mult),
                          reads=[f"h{hsl}", f"cbc{bslm}"], writes=[f"hc{ei}"])
                        E(heng, lambda e, ysl=ysl, hsl=ei: e.tensor_tensor(
                            ych_sb[:, ysl * Q:(ysl + 1) * Q],
                            ych_sb[:, ysl * Q:(ysl + 1) * Q],
                            hc_sb[:, hsl * Q:(hsl + 1) * Q],
                            op=ALU.add),
                          reads=[f"hc{ei}", f"ych{ysl}"], writes=[f"ych{ysl}"])
            drive_pre(10 ** 9)
            for dti in range(2):
                E("v", lambda v, dti=dti: v.tensor_tensor(
                    ych_sb[:, dti * Q:(dti + 1) * Q],
                    ych_sb[:, dti * Q:(dti + 1) * Q],
                    ych_sb[:, (2 + dti) * Q:(3 + dti) * Q], op=ALU.add),
                  reads=[f"ych{dti}", f"ych{2 + dti}"], writes=[f"ych{dti}"])
                E("v", lambda v, dti=dti, cp=cp: v.scalar_tensor_tensor(
                    ych_sb[:, dti * Q:(dti + 1) * Q],
                    xnsl_sb[:, cp * 2 * Q + dti * Q:
                            cp * 2 * Q + (dti + 1) * Q],
                    dcol_sb[:, dti:dti + 1],
                    ych_sb[:, dti * Q:(dti + 1) * Q],
                    op0=ALU.mult, op1=ALU.add),
                  reads=[f"xnsl{cp}{dti}", "dcol", f"ych{dti}"],
                  writes=[f"ych{dti}"])
            for tt in range(4):
                for dti in range(2):
                    pst = psT[dti]
                    E("t", lambda t, dti=dti, tt=tt, pst=pst: t.transpose(
                        pst[:, 0:128],
                        ych_sb[:, dti * Q + tt * 128: dti * Q + (tt + 1) * 128],
                        eye_sb[:, :]),
                      reads=[f"ych{dti}", "eye"], writes=[f"psT{dti}"])
                    E("s", lambda s, dti=dti, tt=tt, pst=pst: s.activation(
                        ytok_sb[:, (tt % 2) * DSL + dti * 128:
                                (tt % 2) * DSL + (dti + 1) * 128],
                        pst[:, 0:128], AF.Copy),
                      reads=[f"psT{dti}"], writes=[f"ytok{tt % 2}", "xnb"])
                E("q", lambda q, tt=tt, ch=ch: q.dma_start(
                    out=ag_in[ch][tt * 128:(tt + 1) * 128, :],
                    in_=ytok_sb[:, (tt % 2) * DSL:(tt % 2 + 1) * DSL]),
                  reads=[f"ytok{tt % 2}"], writes=[f"ccy{ch}"], dma=True,
                  chan=f"qsy{ch}")
            E("g", lambda g, ch=ch: g.collective_compute(
                "AllGather", ALU.bypass, replica_groups=GRPS,
                ins=[ag_in[ch].ap().opt()],
                outs=[ag_out[ch].ap().opt()]),
              reads=[f"ccy{ch}"], writes=[f"ag{ch}"])
            drive_ffn(ch)

        drive_ffn(NCH, drain=True)

        L.barrier()

        def play(engine, name):
            if name == "q":
                pid = engine.partition_id()
                PID[0] = pid % 4
            for item in L.queue[name]:
                if item[0] == "wait":
                    engine.wait_ge(sems[item[1]], item[2])
                else:
                    _, fn, step, sem_name = item
                    fn(engine).then_inc(sems[sem_name], step)

        @block.vector
        def _(v):
            play(v, "v")

        @block.scalar
        def _(s):
            play(s, "s")

        @block.tensor
        def _(t):
            play(t, "t")

        @block.gpsimd
        def _(g):
            play(g, "g")

        @block.sync
        def _(q):
            play(q, "q")

    return nc


_CACHE = {}


def kernel(L0_attn_u, L0_attn_A_log, L0_attn_W_xdt, L0_attn_b_xdt,
           L0_attn_W_dt, L0_attn_b_dt, L0_attn_W_B, L0_attn_b_B,
           L0_attn_W_C, L0_attn_b_C, L0_attn_D, L0_ffn_w_gate,
           L0_ffn_w_up, L0_ffn_w_down):
    u = np.ascontiguousarray(np.asarray(L0_attn_u, dtype=np.float32))
    eye = np.eye(128, dtype=np.float32)
    in_maps = []
    for c in range(8):
        b, q = c // 4, c % 4
        perm = np.r_[np.arange(q * DSL, (q + 1) * DSL),
                     np.arange(0, q * DSL), np.arange((q + 1) * DSL, D)]
        im = {
            "u_full": np.ascontiguousarray(u[b][:, perm]),
            "u_tok": np.ascontiguousarray(
                u[b].reshape(NCH, 4, 128, D)[:, q].reshape(TQ, D)),
            "wxdtT": np.ascontiguousarray(
                np.asarray(L0_attn_W_xdt, np.float32).T[perm, :]),
            "bxdt": np.asarray(L0_attn_b_xdt, np.float32).reshape(RK, 1),
            "wdtT_sl": np.ascontiguousarray(
                np.asarray(L0_attn_W_dt, np.float32)[q * DSL:(q + 1) * DSL, :].T),
            "bdt_sl": np.ascontiguousarray(
                np.asarray(L0_attn_b_dt, np.float32)[q * DSL:(q + 1) * DSL]
                .reshape(DSL, 1)),
            "wbT": np.ascontiguousarray(
                np.asarray(L0_attn_W_B, np.float32).T[perm, :]),
            "bb": np.asarray(L0_attn_b_B, np.float32).reshape(NST, 1),
            "wcT": np.ascontiguousarray(
                np.asarray(L0_attn_W_C, np.float32).T[perm, :]),
            "bc": np.asarray(L0_attn_b_C, np.float32).reshape(NST, 1),
            "alog_sl": np.ascontiguousarray(
                np.asarray(L0_attn_A_log, np.float32)[q * DSL:(q + 1) * DSL, :]),
            "dvec_sl": np.ascontiguousarray(
                np.asarray(L0_attn_D, np.float32)[q * DSL:(q + 1) * DSL]
                .reshape(DSL, 1)),
            "eye": eye,
            "wgT_q": np.ascontiguousarray(
                np.asarray(L0_ffn_w_gate, np.float32).T[:, q * 1024:(q + 1) * 1024]),
            "wuT_q": np.ascontiguousarray(
                np.asarray(L0_ffn_w_up, np.float32).T[:, q * 1024:(q + 1) * 1024]),
            "wdT_q": np.ascontiguousarray(
                np.asarray(L0_ffn_w_down, np.float32).T[q * 1024:(q + 1) * 1024, :]),
        }
        in_maps.append(im)
    if "nc" not in _CACHE:
        _CACHE["nc"] = build_program()
    nc = _CACHE["nc"]
    res = run_bass_kernel_spmd(nc, in_maps, core_ids=list(range(8)))
    _CACHE["res"] = res
    out = np.empty((B, T, D), dtype=np.float32)
    ov = out.reshape(B, NCH, 4, 128, D)
    for c in range(8):
        b, q = c // 4, c % 4
        ov[b, :, q] = res.results[c]["out"].reshape(NCH, 128, D)
    return out
